# version 03
# baseline (speedup 1.0000x reference)
import numpy as np
import ml_dtypes
from contextlib import ExitStack

import concourse.bass as bass
import concourse.mybir as mybir
from concourse.bass_utils import run_bass_kernel_spmd

F32 = mybir.dt.float32
BF16 = mybir.dt.bfloat16
AF = mybir.ActivationFunctionType
ALU = mybir.AluOpType
AX = mybir.AxisListType
BT = ml_dtypes.bfloat16

T = 2048
D = 1024
NCH = T // 128
DC = D // 128


ENGINES = ("tensor", "vector", "scalar", "sync", "gpsimd")


class Prog:

    def __init__(self, nc, ctx):
        self.nc = nc
        self.ctx = ctx
        self.q = {e: [] for e in ENGINES}
        self.cnt = {}
        self.s = {}
        self.s_small = {}
        for e in ENGINES:
            h = ctx.enter_context(nc.semaphore("s_" + e))
            self.cnt[h] = 0
            self.s[e] = h
        for e in ("sync", "gpsimd"):
            h = ctx.enter_context(nc.semaphore("sm_" + e))
            self.cnt[h] = 0
            self.s_small[e] = h
        self.s_barrier = {}
        for e in ENGINES:
            h = ctx.enter_context(nc.semaphore("sb_" + e))
            self.cnt[h] = 0
            self.s_barrier[e] = h

    def emit(self, eng, fn, inc=False):
        incs = []
        if inc:
            incs = [(self.s[eng], 1)]
            self.cnt[self.s[eng]] += 1
        self.q[eng].append((fn, incs))
        return self.tok(eng)

    def tok(self, eng):
        s = self.s[eng]
        return (s, self.cnt[s])

    def wait(self, eng, tok):
        if tok is None:
            return
        if isinstance(tok, list):
            for t in tok:
                self.wait(eng, t)
            return
        s, v = tok
        if v <= 0:
            return
        self.q[eng].append((lambda E, s=s, v=v: E.wait_ge(s, v), []))

    def vdrain(self):
        self.q["vector"].append((lambda E: E.drain(), []))

    def dma(self, queue, out_ap, in_ap):
        sb = out_ap if str(out_ap.space).lower().find("sb") >= 0 else in_ap
        try:
            parts = sb.shape[0]
            perpart = sb.free_nbytes()
        except Exception:
            parts, perpart = 128, 4096
        small = parts < 128 or perpart < 512
        s = self.s_small[queue] if small else self.s[queue]
        self.cnt[s] += 16
        self.q[queue].append(
            (lambda E, o=out_ap, i=in_ap: E.dma_start(o, i), [(s, 16)]))
        tok = (s, self.cnt[s])
        if small:
            self.wait(queue, tok)
        return tok

    def finalize(self):
        block = self.ctx.enter_context(self.nc.Block())
        for ename in ENGINES:
            items = self.q[ename]

            def body(E, items=items):
                for fn, incs in items:
                    inst = fn(E)
                    for s, k in incs:
                        inst.then_inc(s, k)

            getattr(block, ename)(body)


class Banks:

    def __init__(self, p, pb):
        self.p = p
        self.pb = pb
        self.free_tok = [None] * len(pb)

    def acquire(self, i):
        self.p.wait("tensor", self.free_tok[i])
        return self.pb[i]

    def release(self, i, tok):
        self.free_tok[i] = tok



def emit_load_x(p, X, d_x):
    return [p.dma("sync", X[:, c * D:(c + 1) * D], d_x[c * 128:(c + 1) * 128, :])
            for c in range(NCH)]


def emit_store_x(p, d_out, X, wait_toks):
    for tk in wait_toks:
        p.wait("sync", tk)
    last = None
    for c in range(NCH):
        last = p.dma("sync", d_out[c * 128:(c + 1) * 128, :], X[:, c * D:(c + 1) * D])
    p.wait("sync", last)


def emit_ln(p, X, YT, bufs, x_toks, eps=1e-5, y_tm=None, chunks=None, yt_T=T):
    if chunks is None:
        chunks = list(range(NCH))
    ybf, stat, pbt, identb = bufs["ybf"], bufs["stat"], bufs["pbt"], bufs["identb"]
    evac_tok = bufs.setdefault("_ln_evac", [None, None])
    ybf_war = bufs.setdefault("_ln_ybf_war", [None, None])
    act_tok = bufs.setdefault("_ln_act", [None, None])
    chunk_done = []
    for li, c in enumerate(chunks):
        xc = X[:, c * D:(c + 1) * D]
        sl = stat[:, (li % 2) * 16:(li % 2) * 16 + 16]
        if x_toks is not None:
            p.wait("vector", x_toks[c])
        p.wait("vector", act_tok[li % 2])
        p.emit("vector", lambda E, a=sl[:, 0:6], b=xc[:, 0:512]: E.bn_stats(a, b))
        p.emit("vector", lambda E, a=sl[:, 6:12], b=xc[:, 512:1024]: E.bn_stats(a, b))
        p.vdrain()
        tdve = p.emit("vector", lambda E, a=sl[:, 12:14], b=sl[:, 0:12]:
                      E.bn_aggr(a, b), inc=True)
        p.wait("scalar", tdve)
        p.emit("scalar", lambda E, o=sl[:, 14:15], i=sl[:, 13:14], ee=bufs["epsb"]:
               E.activation(o, i, AF.Ln, bias=ee))
        tact = p.emit("scalar", lambda E, o=sl[:, 15:16], i=sl[:, 14:15]:
                      E.activation(o, i, AF.Exp, scale=-0.5), inc=True)
        act_tok[li % 2] = tact
        yb = ybf[li % 2]
        p.wait("vector", tact)
        p.wait("vector", ybf_war[li % 2])
        p.vdrain()
        if y_tm is not None:
            p.emit("vector", lambda E, o=y_tm[:, li * D:(li + 1) * D], i=xc,
                   m=sl[:, 12:13], r=sl[:, 15:16]:
                   E.tensor_scalar(o, i, m, r, ALU.subtract, ALU.mult))
            p.vdrain()
            tapply = p.emit("vector", lambda E, o=yb[:], i=y_tm[:, li * D:(li + 1) * D]:
                            E.tensor_copy(o, i), inc=True)
        else:
            tapply = p.emit("vector", lambda E, o=yb[:], i=xc,
                            m=sl[:, 12:13], r=sl[:, 15:16]:
                            E.tensor_scalar(o, i, m, r, ALU.subtract, ALU.mult), inc=True)
        p.wait("tensor", tapply)
        emit_transposes(p, bufs, yb, YT, li, yt_T)
        ybf_war[li % 2] = p.tok("tensor")
        chunk_done.append(p.tok("vector"))
    return chunk_done


def emit_transposes(p, bufs, yb, YT, li, yt_T):
    pbt, identb = bufs["pbt"], bufs["identb"]
    evac_tok = bufs.setdefault("_ln_evac", [None, None])
    for dc in range(DC):
        slot = dc % 2
        p.wait("tensor", evac_tok[slot])
        tpe = p.emit("tensor", lambda E, o=pbt[slot][:, 0:128],
                     i=yb[:, dc * 128:dc * 128 + 128], idn=identb[:]:
                     E.transpose(o, i, idn), inc=True)
        p.wait("vector", tpe)
        evac_tok[slot] = p.emit(
            "vector", lambda E, o=YT[:, dc * yt_T + li * 128: dc * yt_T + li * 128 + 128],
            i=pbt[slot][:, 0:128]: E.tensor_copy(o, i), inc=True)


def emit_cast_transpose(p, bufs, src_f32, YT, li, yt_T):
    ybf = bufs["ybf"]
    ybf_war = bufs.setdefault("_ln_ybf_war", [None, None])
    yb = ybf[li % 2]
    p.wait("vector", ybf_war[li % 2])
    p.vdrain()
    tcast = p.emit("vector", lambda E, o=yb[:], i=src_f32: E.tensor_copy(o, i), inc=True)
    p.wait("tensor", tcast)
    emit_transposes(p, bufs, yb, YT, li, yt_T)
    ybf_war[li % 2] = p.tok("tensor")



class Arena:

    def __init__(self, t):
        self.t = t
        self.n = t.shape[1]
        self.off = 0

    def take(self, cols):
        v = self.t[:, self.off:self.off + cols]
        self.off += cols
        assert self.off <= self.n, f"arena overflow {self.off} > {self.n}"
        return v

    def reset(self):
        self.off = 0


ABF_COLS = 58700
AF32_COLS = 5830


def make_common(nc, ctx, p):
    e = ctx.enter_context
    C = {}
    C["X"] = e(nc.sbuf_tensor("X", [128, NCH * D], F32))
    C["IDB"] = e(nc.sbuf_tensor("IDB", [128, 128], BF16))
    C["IDF"] = e(nc.sbuf_tensor("IDF", [128, 128], F32))
    C["ybf"] = [e(nc.sbuf_tensor("ybf0", [128, D], BF16)),
                e(nc.sbuf_tensor("ybf1", [128, D], BF16))]
    C["stat"] = e(nc.sbuf_tensor("stat", [128, 32], F32))
    C["epsb"] = e(nc.sbuf_tensor("epsb", [128, 1], F32))
    C["eps6"] = e(nc.sbuf_tensor("eps6", [128, 1], F32))
    C["abf"] = Arena(e(nc.sbuf_tensor("ABF", [128, ABF_COLS], BF16)))
    C["af32"] = Arena(e(nc.sbuf_tensor("AF32", [128, AF32_COLS], F32)))
    C["pb"] = [e(nc.psum_tensor(f"pb{i}", [128, 512], F32)) for i in range(6)]
    C["pbt"] = [e(nc.psum_tensor("pbt0", [128, 256], BF16)),
                e(nc.psum_tensor("pbt1", [128, 256], BF16))]
    C["banks"] = Banks(p, C["pb"])
    d_idb = nc.dram_tensor("identb", [128, 128], BF16, kind="ExternalInput").ap()
    d_idf = nc.dram_tensor("identf", [128, 128], F32, kind="ExternalInput").ap()
    p.dma("sync", C["IDB"][:], d_idb)
    p.dma("sync", C["IDF"][:], d_idf)
    ct = p.tok("sync")
    for en in ("vector", "tensor", "scalar"):
        p.wait(en, ct)
    p.emit("vector", lambda E, o=C["epsb"][:]: E.memset(o, 1e-5))
    p.emit("vector", lambda E, o=C["eps6"][:]: E.memset(o, 1e-6))
    p.vdrain()
    C["bufs"] = dict(ybf=C["ybf"], stat=C["stat"], pbt=C["pbt"], identb=C["IDB"],
                     epsb=C["epsb"][:])
    return C


BARRIER_WAIT_ON = list(ENGINES)


def emit_barrier(p, C):
    p.vdrain()
    toks = {}
    for en in ENGINES:
        sb = p.s_barrier[en]
        p.cnt[sb] += 1
        if en == "tensor":
            bank0 = C["banks"].acquire(0)
            p.q[en].append((lambda E, o=bank0[0:128, 0:128], l=C["IDB"][:],
                            r=C["IDB"][:]: E.matmul(o, l, r, start=True, stop=True),
                            [(sb, 1)]))
        else:
            p.q[en].append((lambda E: E.nop(nofuse=True), [(sb, 1)]))
        toks[en] = (sb, p.cnt[sb])
    for en in ENGINES:
        for other in ENGINES:
            if other != en and other in BARRIER_WAIT_ON:
                p.wait(en, toks[other])
    bt = p.tok("tensor")
    for i in range(len(C["banks"].free_tok)):
        C["banks"].free_tok[i] = toks["vector"]
    C["bufs"]["_ln_evac"] = [toks["vector"], toks["vector"]]
    C["bufs"]["_ln_ybf_war"] = [toks["tensor"], toks["tensor"]]
    C["bufs"]["_ln_act"] = [toks["scalar"], toks["scalar"]]


def prep_common():
    return {"identb": np.eye(128, dtype=BT), "identf": np.eye(128, dtype=np.float32)}


def mm_acc(p, banks, bi, ncols, parts, out_base=0, npart=128):
    bank = banks.acquire(bi)
    n = len(parts)
    tmm = None
    for i, (lhsT, rhs) in enumerate(parts):
        tmm = p.emit("tensor", lambda E, o=bank[out_base:out_base + npart, 0:ncols],
                     l=lhsT, r=rhs, st=(i == 0), sp=(i == n - 1):
                     E.matmul(o, l, r, start=st, stop=sp), inc=(i == n - 1))
    return bank, tmm


def emit_top_gate(p, gs, gm, gout, bank_ap, nexp, kth, drain=True):
    p.emit("vector", lambda E, o=gs[:, 0:8], i=bank_ap: E.max(o, i))
    p.vdrain()
    p.emit("vector", lambda E, o=gs[:, 8:9], i=gs[:, 0:1]:
           E.tensor_scalar(o, i, -1.0, None, ALU.mult))
    tmask = p.emit("vector", lambda E, o=gm[:, 0:nexp], i=bank_ap, mk=gs[:, kth:kth + 1]:
                   E.tensor_scalar(o, i, mk, None, ALU.is_ge), inc=True)
    p.vdrain()
    p.wait("scalar", tmask)
    texp = p.emit("scalar", lambda E, o=gout, i=bank_ap, b=gs[:, 8:9]:
                  E.activation(o, i, AF.Exp, bias=b), inc=True)
    p.wait("vector", texp)
    p.emit("vector", lambda E, o=gout, i=gout, m=gm[:, 0:nexp], s=gs[:, 9:10]:
           E.scalar_tensor_tensor(o, i, 1.0, m, ALU.mult, ALU.mult, accum_out=s))
    p.vdrain()
    p.emit("vector", lambda E, o=gs[:, 10:11], i=gs[:, 9:10]: E.reciprocal(o, i))
    p.vdrain()
    tgt = p.emit("vector", lambda E, o=gout, i=gout, r=gs[:, 10:11]:
                 E.tensor_scalar(o, i, r, None, ALU.mult), inc=True)
    if drain:
        p.vdrain()
    return tgt, texp



Q = 512
QC = Q // 128
NQ = T // Q


def emit_l0ffn(nc, p, C):
    d_dwk = nc.dram_tensor("l0f_dwk", [128, 24], F32, kind="ExternalInput").ap()
    d_dwb = nc.dram_tensor("l0f_dwb", [128, 8], F32, kind="ExternalInput").ap()
    d_pwt = nc.dram_tensor("l0f_pwt", [8, 128, 1024], BF16, kind="ExternalInput").ap()
    d_pwb = nc.dram_tensor("l0f_pwb", [1, 1024], BF16, kind="ExternalInput").ap()
    d_rwt = nc.dram_tensor("l0f_rwt", [8, 128, 8], BF16, kind="ExternalInput").ap()
    d_rb = nc.dram_tensor("l0f_rb", [1, 8], BF16, kind="ExternalInput").ap()
    d_ewt = nc.dram_tensor("l0f_ewt", [8, 8, 128, 1024], BF16, kind="ExternalInput").ap()
    d_eb = nc.dram_tensor("l0f_eb", [1, 8 * 1024], BF16, kind="ExternalInput").ap()
    d_owt = nc.dram_tensor("l0f_owt", [8, 128, 1024], BF16, kind="ExternalInput").ap()
    d_outb = nc.dram_tensor("l0f_outb", [1, 1024], BF16, kind="ExternalInput").ap()
    d_nwb = nc.dram_tensor("l0f_nwb", [128, 1024], F32, kind="ExternalInput").ap()
    d_fdt = nc.dram_tensor("l0f_fdt", [8, 128, 256], BF16, kind="ExternalInput").ap()
    d_fdb = nc.dram_tensor("l0f_fdb", [128, 2], F32, kind="ExternalInput").ap()
    d_fut = nc.dram_tensor("l0f_fut", [2, 128, 1024], BF16, kind="ExternalInput").ap()
    d_fub = nc.dram_tensor("l0f_fub", [1, 1024], BF16, kind="ExternalInput").ap()
    d_ones = nc.dram_tensor("onesb", [1, 128], BF16, kind="ExternalInput").ap()

    X, IDB, ybf, stat = C["X"], C["IDB"], C["ybf"], C["stat"]
    epsb, eps6, pb, pbt, banks = C["epsb"], C["eps6"], C["pb"], C["pbt"], C["banks"]
    abf, af32 = C["abf"], C["af32"]
    X1 = af32.take(QC * D)
    YTq = abf.take(DC * Q)
    CTq = abf.take(DC * Q)
    XTq = YTq
    CMB = abf.take(QC * D)
    PWT = abf.take(DC * 1024)
    OWT = abf.take(DC * 1024)
    FDT = abf.take(DC * 256)
    FUT = abf.take(2 * 1024)
    EWT = [abf.take(DC * 1024), abf.take(DC * 1024)]
    RWT = abf.take(DC * 8)
    NWB = af32.take(1024)
    DWK = af32.take(24)
    DWB = af32.take(8)
    FDB = af32.take(2)
    ROWS = abf.take(3 * 1024 + 8 + 128)[0:1, :]
    EBR = [abf.take(1024)[0:1, :], abf.take(1024)[0:1, :]]
    RW = af32.take(NCH * 8)
    gsc = af32.take(16)
    gm = af32.take(16)
    tmb = [abf.take(512), abf.take(512)]
    yet = [abf.take(512), abf.take(512)]
    TCIN = abf.take(Q + 2)
    DISC = abf.take(1024)
    tail = abf.take(DC * 2)
    cacc = af32.take(Q)
    tmpf = [cacc]

    R_PWB, R_OUTB, R_FUB, R_RB, R_ONES = 0, 1024, 2048, 3072, 3080
    p.dma("sync", DWK[:], d_dwk)
    p.dma("sync", DWB[:], d_dwb)
    p.dma("sync", FDB[:], d_fdb)
    p.dma("sync", NWB[:], d_nwb)
    p.dma("sync", ROWS[:, R_PWB:R_PWB + 1024], d_pwb)
    p.dma("sync", ROWS[:, R_OUTB:R_OUTB + 1024], d_outb)
    p.dma("sync", ROWS[:, R_FUB:R_FUB + 1024], d_fub)
    p.dma("sync", ROWS[:, R_RB:R_RB + 8], d_rb)
    p.dma("sync", ROWS[:, R_ONES:R_ONES + 128], d_ones)
    for dc in range(DC):
        p.dma("sync", RWT[:, dc * 8:(dc + 1) * 8], d_rwt[dc])
    p.dma("sync", PWT[:].rearrange("p (s f) -> p s f", s=8), d_pwt.transpose([1, 0, 2]))
    p.dma("sync", OWT[:].rearrange("p (s f) -> p s f", s=8), d_owt.transpose([1, 0, 2]))
    p.dma("sync", FDT[:].rearrange("p (s f) -> p s f", s=8), d_fdt.transpose([1, 0, 2]))
    p.dma("sync", FUT[:].rearrange("p (s f) -> p s f", s=2), d_fut.transpose([1, 0, 2]))
    consts_tok = p.tok("sync")
    x_toks = C.get("x_toks")

    def load_ewt(ei, buf):
        p.dma("gpsimd", EWT[buf][:].rearrange("p (s f) -> p s f", s=8),
              d_ewt[ei].transpose([1, 0, 2]))
        p.dma("gpsimd", EBR[buf][:], d_eb[:, ei * 1024:(ei + 1) * 1024])
        return p.tok("gpsimd")

    ONES = ROWS[:, R_ONES:R_ONES + 128]
    ewt_tok = [load_ewt(0, 0), None]

    p.wait("vector", consts_tok)
    p.wait("tensor", consts_tok)
    p.wait("scalar", consts_tok)
    p.emit("vector", lambda E, o=tail[:]: E.memset(o, 0.0))
    p.vdrain()
    bufs = C["bufs"]

    tmb_war = [None, None]
    yet_war = [None, None]
    ewt_pe_done = [None, None]
    for q in range(NQ):
        chunks = list(range(q * QC, (q + 1) * QC))
        ln_done = emit_ln(p, X, YTq, bufs, x_toks, eps=1e-5, y_tm=X1,
                          chunks=chunks, yt_T=Q)
        for dc in range(DC):
            w0 = DWK[:, dc * 3:dc * 3 + 1]
            w1 = DWK[:, dc * 3 + 1:dc * 3 + 2]
            w2 = DWK[:, dc * 3 + 2:dc * 3 + 3]
            bcol = DWB[:, dc:dc + 1]
            ytc = YTq[:, dc * Q:(dc + 1) * Q]
            p.wait("vector", ln_done[-1])
            p.vdrain()
            p.emit("vector", lambda E, o=TCIN[:, 0:2], i=tail[:, dc * 2:dc * 2 + 2]:
                   E.tensor_copy(o, i))
            p.emit("vector", lambda E, o=TCIN[:, 2:Q + 2], i=ytc: E.tensor_copy(o, i))
            p.emit("vector", lambda E, o=tail[:, dc * 2:dc * 2 + 2], i=ytc[:, Q - 2:Q]:
                   E.tensor_copy(o, i))
            p.vdrain()
            p.emit("vector", lambda E, o=cacc[:], i=TCIN[:, 2:Q + 2], a=w2, b=bcol:
                   E.tensor_scalar(o, i, a, b, ALU.mult, ALU.add))
            p.vdrain()
            p.emit("vector", lambda E, o=cacc[:], i=TCIN[:, 1:Q + 1], a=w1, x=cacc[:]:
                   E.scalar_tensor_tensor(o, i, a, x, ALU.mult, ALU.add))
            p.vdrain()
            p.emit("vector", lambda E, o=CTq[:, dc * Q:(dc + 1) * Q], i=TCIN[:, 0:Q],
                   a=w0, x=cacc[:]:
                   E.scalar_tensor_tensor(o, i, a, x, ALU.mult, ALU.add), inc=True)
        p.vdrain()
        conv_done = p.tok("vector")

        p.wait("tensor", conv_done)
        for tc in range(QC):
            gc = q * QC + tc
            for eh in range(2):
                parts = [(CTq[:, dc * Q + tc * 128: dc * Q + tc * 128 + 128],
                          PWT[:, dc * 1024 + eh * 512: dc * 1024 + (eh + 1) * 512])
                         for dc in range(DC)]
                parts.append((ONES, ROWS[:, R_PWB + eh * 512:R_PWB + (eh + 1) * 512]))
                bank, tmm = mm_acc(p, banks, (tc * 2 + eh) % 4, 512, parts)
                tb = tmb[eh]
                p.wait("scalar", tmm)
                p.wait("scalar", tmb_war[eh])
                tev = p.emit("scalar", lambda E, o=tb[:], i=bank[:]:
                             E.activation(o, i, AF.Silu), inc=True)
                banks.release((tc * 2 + eh) % 4, tev)
                x1s = X1[:, tc * D + eh * 512: tc * D + (eh + 1) * 512]
                p.wait("vector", tev)
                tadd = p.emit("vector", lambda E, o=x1s, i=x1s, b=tb[:]:
                              E.tensor_add(o, i, b), inc=True)
                tmb_war[eh] = tadd
            p.vdrain()
            emit_cast_transpose(p, bufs, X1[:, tc * D:(tc + 1) * D], XTq, tc, Q)
            parts = [(XTq[:, dc * Q + tc * 128: dc * Q + tc * 128 + 128],
                      RWT[:, dc * 8:(dc + 1) * 8]) for dc in range(DC)]
            parts.append((ONES, ROWS[:, R_RB:R_RB + 8]))
            xt_tok = p.tok("vector")
            p.wait("tensor", xt_tok)
            bank, tmm = mm_acc(p, banks, 4 + tc % 2, 8, parts)
            p.wait("vector", tmm)
            rwc = RW[:, gc * 8:(gc + 1) * 8]
            tgt, texp = emit_top_gate(p, gsc, gm, rwc, bank[:, 0:8], 8, 1)
            banks.release(4 + tc % 2, texp)

        gb = 0
        for ei in range(8):
            cur = ei % 2
            p.wait("tensor", ewt_tok[cur])
            for tc in range(QC):
                gc = q * QC + tc
                for oh in range(2):
                    parts = [(XTq[:, dc * Q + tc * 128: dc * Q + tc * 128 + 128],
                              EWT[cur][:, dc * 1024 + oh * 512: dc * 1024 + (oh + 1) * 512])
                             for dc in range(DC)]
                    parts.append((ONES, EBR[cur][:, oh * 512:(oh + 1) * 512]))
                    bank, tmm = mm_acc(p, banks, gb % 4, 512, parts)
                    ys = yet[gb % 2]
                    p.wait("scalar", tmm)
                    p.wait("scalar", yet_war[gb % 2])
                    tev = p.emit("scalar", lambda E, o=ys[:], i=bank[:]:
                                 E.activation(o, i, AF.Silu), inc=True)
                    banks.release(gb % 4, tev)
                    dst = CMB[:, tc * D + oh * 512: tc * D + (oh + 1) * 512]
                    rcol = RW[:, gc * 8 + ei: gc * 8 + ei + 1]
                    p.wait("vector", tev)
                    if ei == 0:
                        tacc = p.emit("vector", lambda E, o=dst, i=ys[:], r=rcol:
                                      E.tensor_scalar(o, i, r, None, ALU.mult), inc=True)
                    else:
                        tacc = p.emit("vector", lambda E, o=dst, i=ys[:], r=rcol:
                                      E.scalar_tensor_tensor(o, i, r, o, ALU.mult, ALU.add),
                                      inc=True)
                    yet_war[gb % 2] = tacc
                    gb += 1
            ewt_pe_done[cur] = p.tok("tensor")
            nxt = ei + 1 if ei < 7 else (0 if q < NQ - 1 else None)
            if nxt is not None:
                p.wait("gpsimd", ewt_pe_done[1 - cur])
                ewt_tok[1 - cur] = load_ewt(nxt, 1 - cur)
        p.vdrain()

        for tc in range(QC):
            emit_cast_transpose(p, bufs, CMB[:, tc * D:(tc + 1) * D], CTq, tc, Q)
        cmbt_tok = p.tok("vector")

        p.wait("tensor", cmbt_tok)
        for tc in range(QC):
            for oh in range(2):
                parts = [(CTq[:, dc * Q + tc * 128: dc * Q + tc * 128 + 128],
                          OWT[:, dc * 1024 + oh * 512: dc * 1024 + (oh + 1) * 512])
                         for dc in range(DC)]
                parts.append((ONES, ROWS[:, R_OUTB + oh * 512:R_OUTB + (oh + 1) * 512]))
                bank, tmm = mm_acc(p, banks, (tc * 2 + oh) % 4, 512, parts)
                dst = X1[:, tc * D + oh * 512: tc * D + (oh + 1) * 512]
                p.wait("vector", tmm)
                tev = p.emit("vector", lambda E, o=dst, i=bank[:]:
                             E.tensor_add(o, o, i), inc=True)
                banks.release((tc * 2 + oh) % 4, tev)

        for tc in range(QC):
            zc = X1[:, tc * D:(tc + 1) * D]
            sl = stat[:, (tc % 2) * 16:(tc % 2) * 16 + 16]
            tz = p.tok("vector")
            p.wait("scalar", tz)
            p.wait("scalar", bufs["_ln_act"][tc % 2])
            tsq = p.emit("scalar", lambda E, o=DISC[:], i=zc, a=sl[:, 0:1]:
                         E.activation(o, i, AF.Square, accum_out=a))
            p.emit("scalar", lambda E, o=sl[:, 1:2], i=sl[:, 0:1], ee=eps6[:]:
                   E.activation(o, i, AF.Ln, bias=ee, scale=1.0 / D))
            tact = p.emit("scalar", lambda E, o=sl[:, 2:3], i=sl[:, 1:2]:
                          E.activation(o, i, AF.Exp, scale=-0.5), inc=True)
            bufs["_ln_act"][tc % 2] = tact
            p.wait("vector", tact)
            p.vdrain()
            th = p.emit("vector", lambda E, o=zc, i=zc, r=sl[:, 2:3], n=NWB[:]:
                        E.scalar_tensor_tensor(o, i, r, n, ALU.mult, ALU.mult), inc=True)
            p.vdrain()
            emit_cast_transpose(p, bufs, zc, XTq, tc, Q)
        ht_tok = p.tok("vector")

        FTq = CTq
        p.wait("tensor", ht_tok)
        for hh in range(2):
            parts = [(FDT[:, dc * 256 + hh * 128: dc * 256 + (hh + 1) * 128],
                      XTq[:, dc * Q:(dc + 1) * Q]) for dc in range(DC)]
            bank, tmm = mm_acc(p, banks, 4 + hh, Q, parts)
            p.wait("scalar", tmm)
            tev = p.emit("scalar", lambda E, o=FTq[:, hh * Q:(hh + 1) * Q], i=bank[:],
                         b=FDB[:, hh:hh + 1]: E.activation(o, i, AF.Gelu, bias=b), inc=True)
            banks.release(4 + hh, tev)
        ff1_tok = p.tok("scalar")

        p.wait("tensor", ff1_tok)
        for tc in range(QC):
            gc = q * QC + tc
            for oh in range(2):
                parts = [(FTq[:, hh * Q + tc * 128: hh * Q + tc * 128 + 128],
                          FUT[:, hh * 1024 + oh * 512: hh * 1024 + (oh + 1) * 512])
                         for hh in range(2)]
                parts.append((ONES, ROWS[:, R_FUB + oh * 512:R_FUB + (oh + 1) * 512]))
                bank, tmm = mm_acc(p, banks, (tc * 2 + oh) % 4, 512, parts)
                tf = tmpf[0]
                zs = X1[:, tc * D + oh * 512: tc * D + (oh + 1) * 512]
                xs = X[:, gc * D + oh * 512: gc * D + (oh + 1) * 512]
                p.wait("vector", tmm)
                p.emit("vector", lambda E, o=tf[:], i=bank[:], z=zs:
                       E.scalar_tensor_tensor(o, i, 1.0, z, ALU.mult, ALU.add))
                p.vdrain()
                tev = p.emit("vector", lambda E, o=xs, i=tf[:]:
                             E.tensor_add(o, o, i), inc=True)
                banks.release((tc * 2 + oh) % 4, tev)



def prep_l0ffn(inputs):
    dwk = np.asarray(inputs["L0_ffn_dw_kernel"]).reshape(8, 128, 3).transpose(1, 0, 2)
    out = {
        "l0f_dwk": np.ascontiguousarray(dwk.reshape(128, 24), dtype=np.float32),
        "l0f_dwb": np.ascontiguousarray(
            np.asarray(inputs["L0_ffn_dw_bias"]).reshape(8, 128).T, dtype=np.float32),
        "l0f_pwt": np.ascontiguousarray(
            np.asarray(inputs["L0_ffn_pw_w"]).T.reshape(8, 128, 1024)).astype(BT),
        "l0f_pwb": np.asarray(inputs["L0_ffn_pw_b"]).reshape(1, 1024).astype(BT),
        "l0f_rwt": np.ascontiguousarray(
            np.asarray(inputs["L0_ffn_router_w"]).T.reshape(8, 128, 8)).astype(BT),
        "l0f_rb": np.asarray(inputs["L0_ffn_router_b"]).reshape(1, 8).astype(BT),
        "l0f_ewt": np.ascontiguousarray(
            np.asarray(inputs["L0_ffn_expert_w"]).transpose(0, 2, 1).reshape(8, 8, 128, 1024)).astype(BT),
        "l0f_eb": np.asarray(inputs["L0_ffn_expert_b"]).reshape(1, 8 * 1024).astype(BT),
        "l0f_owt": np.ascontiguousarray(
            np.asarray(inputs["L0_ffn_out_w"]).T.reshape(8, 128, 1024)).astype(BT),
        "l0f_outb": np.asarray(inputs["L0_ffn_out_b"]).reshape(1, 1024).astype(BT),
        "l0f_nwb": np.ascontiguousarray(
            np.broadcast_to(np.asarray(inputs["L0_ffn_norm_w"]), (128, 1024)),
            dtype=np.float32),
        "l0f_fdt": np.ascontiguousarray(
            np.asarray(inputs["L0_ffn_ff_down_w"]).T.reshape(8, 128, 256)).astype(BT),
        "l0f_fdb": np.ascontiguousarray(
            np.asarray(inputs["L0_ffn_ff_down_b"]).reshape(2, 128).T, dtype=np.float32),
        "l0f_fut": np.ascontiguousarray(
            np.asarray(inputs["L0_ffn_ff_up_w"]).T.reshape(2, 128, 1024)).astype(BT),
        "l0f_fub": np.asarray(inputs["L0_ffn_ff_up_b"]).reshape(1, 1024).astype(BT),
        "onesb": np.ones((1, 128), dtype=BT),
        "identb": np.eye(128, dtype=BT),
    }
    return out




TT0 = 256
ATW = 512


def emit_l0attn(nc, p, C):
    d_wq = nc.dram_tensor("l0a_wq", [32, 8, 128, 128], BF16, kind="ExternalInput").ap()
    d_wk = nc.dram_tensor("l0a_wk", [32, 8, 128, 128], BF16, kind="ExternalInput").ap()
    d_wv = nc.dram_tensor("l0a_wv", [32, 8, 128, 128], BF16, kind="ExternalInput").ap()
    d_wo = nc.dram_tensor("l0a_wo", [8, 128, 64, 128], BF16, kind="ExternalInput").ap()
    d_bo = nc.dram_tensor("l0a_bo", [128, 8], F32, kind="ExternalInput").ap()
    d_em = nc.dram_tensor("l0a_emat2", [128, 128], BF16, kind="ExternalInput").ap()

    X, IDB, ybf, stat = C["X"], C["IDB"], C["ybf"], C["stat"]
    epsb, pb, pbt, banks = C["epsb"], C["pb"], C["pbt"], C["banks"]
    abf, af32 = C["abf"], C["af32"]
    YTt = abf.take(DC * TT0)
    SQK = abf.take(64 * 128)
    VSL = abf.take(65 * 128)
    ATE = abf.take(256 * 64)
    WB = [abf.take(4 * 8 * 128), abf.take(4 * 8 * 128)]
    WOB = [abf.take(32 * 128), abf.take(32 * 128)]
    KSUM = af32.take(TT0)
    KSC = abf.take(TT0)
    QSC = abf.take(TT0)
    ATB = [abf.take(256), abf.take(256)]
    BOC = af32.take(8)
    EM2 = abf.take(128)
    rq = af32.take(16)

    p.dma("sync", EM2[:], d_em)
    p.dma("sync", BOC[:], d_bo)
    consts_tok = p.tok("sync")
    x_toks = C.get("x_toks")

    def load_w(dw, grp, buf):
        src = dw[grp * 4:(grp + 1) * 4].transpose([2, 0, 1, 3])
        p.dma("gpsimd", WB[buf][:].rearrange("p (a s f) -> p a s f", a=4, s=8), src)
        return p.tok("gpsimd")

    def load_wo(cc, jh, buf):
        p.dma("sync", WOB[buf][:].rearrange("p (j f) -> p j f", j=32),
              d_wo[cc][:, jh * 32:(jh + 1) * 32, :])
        return p.tok("sync")

    p.wait("vector", consts_tok)
    p.wait("tensor", consts_tok)
    p.wait("scalar", consts_tok)
    p.emit("vector", lambda E, o=VSL[:, 64 * 128:65 * 128]: E.memset(o, 1.0))
    p.vdrain()
    bufs = C["bufs"]

    ksc_war = [None]
    qsc_war = [None]
    wb_toks = [None, None]
    wb_pe_done = [None, None]
    wob_toks = [None, None]
    wob_pe_done = [None, None]
    gi = 0
    slab_pe_done = None
    ate_dve = None
    ksum_bank = 4

    for tile in range(T // TT0):
        r0 = tile * TT0
        chunks = [r0 // 128, r0 // 128 + 1]
        ln_done = emit_ln(p, X, YTt, bufs, x_toks, eps=1e-5, chunks=chunks, yt_T=TT0)
        p.wait("tensor", ln_done[-1])

        def qkv_path(dw, kind):
            nonlocal gi, slab_pe_done, ate_dve
            for grp in range(8):
                buf = gi % 2
                if gi == 0 or wb_toks[buf] is None:
                    p.wait("gpsimd", wb_pe_done[buf])
                    wb_toks[buf] = load_w(dw, grp, buf)
                p.wait("tensor", wb_toks[buf])
                nxt = None
                if grp < 7:
                    nxt = (dw, grp + 1)
                elif kind == 'k':
                    nxt = (d_wq, 0)
                elif kind == 'q':
                    nxt = (d_wv, 0)
                if nxt is not None:
                    p.wait("gpsimd", wb_pe_done[1 - buf])
                    wb_toks[1 - buf] = load_w(nxt[0], nxt[1], 1 - buf)
                for lc in range(4):
                    fc = grp * 4 + lc
                    bi = (fc % 2) + (4 if kind == 'k' and False else 0)
                    bi = fc % 2 if kind != 'k' else 2 + fc % 2
                    bank = banks.acquire(bi)
                    tmm = None
                    for dc in range(DC):
                        lhsT = WB[buf][:, (lc * 8 + dc) * 128:(lc * 8 + dc) * 128 + 128]
                        rhs = YTt[:, dc * TT0:(dc + 1) * TT0]
                        tmm = p.emit("tensor", lambda E, o=bank[:, 0:TT0], l=lhsT, r=rhs,
                                     st=(dc == 0), sp=(dc == DC - 1):
                                     E.matmul(o, l, r, start=st, stop=sp), inc=(dc == DC - 1))
                    if kind == 'k':
                        p.wait("scalar", tmm)
                        p.wait("scalar", ksc_war[0])
                        tev = p.emit("scalar", lambda E, o=KSC[:], i=bank[:, 0:TT0]:
                                     E.activation(o, i, AF.Relu), inc=True)
                        banks.release(bi, tev)
                        p.wait("tensor", tev)
                        kb = pb[ksum_bank]
                        p.emit("tensor", lambda E, o=kb[:, 0:TT0], l=EM2[:], r=KSC[:],
                               st=(fc == 0), sp=(fc == 31):
                               E.matmul(o, l, r, start=st, stop=sp), inc=True)
                        ksc_war[0] = p.tok("tensor")
                    elif kind == 'q':
                        p.wait("scalar", tmm)
                        p.wait("scalar", qsc_war[0])
                        tev = p.emit("scalar", lambda E, o=QSC[:], i=bank[:, 0:TT0]:
                                     E.activation(o, i, AF.Relu, scale=0.125), inc=True)
                        banks.release(bi, tev)
                        p.wait("vector", tev)
                        p.wait("vector", slab_pe_done)
                        for hh in range(2):
                            hd = 2 * fc + hh
                            for th in range(2):
                                tq = p.emit("vector", lambda E,
                                            o=SQK[th * 64:th * 64 + 64, hd * 128:(hd + 1) * 128],
                                            a=QSC[hh * 64:hh * 64 + 64, th * 128:(th + 1) * 128],
                                            b=KSUM[hh * 64:hh * 64 + 64, th * 128:(th + 1) * 128]:
                                            E.tensor_tensor(o, a, b, ALU.mult),
                                            inc=(hh == 1 and th == 1))
                        qsc_war[0] = p.tok("vector")
                    else:
                        p.wait("vector", tmm)
                        p.wait("vector", slab_pe_done)
                        for hh in range(2):
                            hd = 2 * fc + hh
                            for th in range(2):
                                tv = p.emit("vector", lambda E,
                                            o=VSL[th * 64:th * 64 + 64, hd * 128:(hd + 1) * 128],
                                            i=bank[hh * 64:hh * 64 + 64, th * 128:(th + 1) * 128]:
                                            E.tensor_copy(o, i), inc=(hh == 1 and th == 1))
                        banks.release(bi, p.tok("vector"))
                wb_pe_done[buf] = p.tok("tensor")
                gi += 1

        bank4 = banks.acquire(ksum_bank)
        wb_toks[gi % 2] = None
        qkv_path(d_wk, 'k')
        tks = p.tok("tensor")
        p.wait("vector", tks)
        tkev = p.emit("vector", lambda E, o=KSUM[:], i=pb[ksum_bank][:, 0:TT0]:
                      E.tensor_copy(o, i), inc=True)
        banks.release(ksum_bank, tkev)
        p.vdrain()
        qkv_path(d_wq, 'q')
        qkv_path(d_wv, 'v')
        slab_dve = p.tok("vector")
        p.vdrain()

        p.wait("tensor", slab_dve)
        sq_v = [SQK[h * 64:h * 64 + 64, :].rearrange("p (i t) -> p t i", t=128) for h in (0, 1)]
        vs_v = [VSL[h * 64:h * 64 + 64, :].rearrange("p (j t) -> p t j", t=128) for h in (0, 1)]
        npair = 128
        nbk = (npair + 6) // 7
        for bk in range(nbk):
            bi = bk % 4
            bank = banks.acquire(bi)
            pairs = range(bk * 7, min((bk + 1) * 7, npair))
            tmm = None
            for s, g in enumerate(pairs):
                for h in (0, 1):
                    tmm = p.emit("tensor", lambda E, o=bank[h * 64:h * 64 + 64, s * 65:s * 65 + 65],
                                 l=sq_v[h][:, g, :], r=vs_v[h][:, g, :],
                                 tp=(h * 64, h * 64):
                                 E.matmul(o, l, r, start=True, stop=True, tile_position=tp),
                                 inc=(h == 1 and s == len(list(pairs)) - 1))
            ns = len(list(pairs))
            p.wait("vector", tmm)
            qn_view = bank[:, 0:ns * 65].rearrange("p (s c) -> p s c", c=65)[:, :, 64]
            p.emit("vector", lambda E, o=rq[:, 0:ns], i=qn_view:
                   E.tensor_scalar(o, i, 1e-6, None, ALU.add))
            p.vdrain()
            p.emit("vector", lambda E, o=rq[:, 8:8 + ns], i=rq[:, 0:ns]: E.reciprocal(o, i))
            p.vdrain()
            tev = None
            for s, g in enumerate(pairs):
                dst = ATE[:, (tile % 2) * 8192 + g * 64: (tile % 2) * 8192 + g * 64 + 64]
                tev = p.emit("vector", lambda E, o=dst, i=bank[:, s * 65:s * 65 + 64],
                             r=rq[:, 8 + s:9 + s]:
                             E.tensor_scalar(o, i, r, None, ALU.mult),
                             inc=(s == len(list(pairs)) - 1))
            banks.release(bi, tev)
        slab_pe_done = p.tok("tensor")
        ate_dve = p.tok("vector")

        if tile % 2 == 1:
            p.wait("tensor", ate_dve)
            rr0 = (tile - 1) * TT0
            wi = 0
            for h in (0, 1):
                for cc in range(DC):
                    bi = 4 + h
                    bank = banks.acquire(bi)
                    tmm = None
                    for jh in (0, 1):
                        wbuf = wi % 2
                        if wob_toks[wbuf] is None:
                            p.wait("sync", wob_pe_done[wbuf])
                            wob_toks[wbuf] = load_wo(cc, jh, wbuf)
                        ncc, njh = (cc, 1) if jh == 0 else (cc + 1, 0)
                        if njh == 0 and ncc == DC:
                            ncc = 0
                        p.wait("sync", wob_pe_done[1 - wbuf])
                        wob_toks[1 - wbuf] = load_wo(ncc, njh, 1 - wbuf)
                        p.wait("tensor", wob_toks[wbuf])
                        for jj in range(32):
                            j = jh * 32 + jj
                            lhsT = WOB[wbuf][h * 64:h * 64 + 64, jj * 128:(jj + 1) * 128]
                            rhs = ATE[h * 64:h * 64 + 64, :].rearrange(
                                "p (g j) -> p j g", j=64)[:, j, :]
                            tmm = p.emit("tensor", lambda E, o=bank[:, 0:256], l=lhsT, r=rhs,
                                         st=(j == 0), sp=(j == 63):
                                         E.matmul(o, l, r, start=st, stop=sp), inc=(jj == 31))
                        wob_pe_done[wbuf] = p.tok("tensor")
                        wob_toks[wbuf] = None
                        wi += 1
                    p.wait("scalar", tmm)
                    tab = p.emit("scalar", lambda E, o=ATB[h][:], i=bank[:, 0:256],
                                 b=BOC[:, cc:cc + 1]:
                                 E.activation(o, i, AF.Identity, bias=b), inc=True)
                    banks.release(bi, tab)
                    p.wait("tensor", tab)
                    for gb in range(2):
                        gchunk = (rr0 + h * 128 + gb * 256) // 128
                        slot = gb
                        p.wait("tensor", bufs["_ln_evac"][slot])
                        tpe = p.emit("tensor", lambda E, o=pbt[slot][:, 0:128],
                                     i=ATB[h][:, gb * 128:(gb + 1) * 128], idn=IDB[:]:
                                     E.transpose(o, i, idn), inc=True)
                        p.wait("vector", tpe)
                        bufs["_ln_evac"][slot] = p.emit(
                            "vector", lambda E, o=X[:, gchunk * D + cc * 128: gchunk * D + cc * 128 + 128],
                            i=pbt[slot][:, 0:128]:
                            E.tensor_tensor(o, o, i, ALU.add), inc=True)



def prep_l0attn(inputs):
    wq = np.asarray(inputs["L0_attn_Wq"])
    wk = np.asarray(inputs["L0_attn_Wk"])
    wv = np.asarray(inputs["L0_attn_Wv"])
    wo = np.asarray(inputs["L0_attn_Wo"])
    bo = np.asarray(inputs["L0_attn_bo"])

    def chunk_qkv(w):
        return np.ascontiguousarray(
            w.reshape(8, 128, 32, 128).transpose(2, 0, 1, 3)).astype(BT)

    wop = wo.reshape(64, 64, 1024)
    wop = np.concatenate([wop, wop], axis=0)
    wopc = np.ascontiguousarray(
        wop.reshape(128, 64, 8, 128).transpose(2, 0, 1, 3)).astype(BT)
    em2 = np.fromfunction(lambda p_, m: (p_ % 64 == m % 64), (128, 128)).astype(BT)
    return {
        "l0a_wq": chunk_qkv(wq),
        "l0a_wk": chunk_qkv(wk),
        "l0a_wv": chunk_qkv(wv),
        "l0a_wo": wopc,
        "l0a_bo": np.ascontiguousarray(bo.reshape(8, 128).T, dtype=np.float32),
        "l0a_emat2": em2,
        "identb": np.eye(128, dtype=BT),
    }




NHP_L1A = 8
NHH_L1A = 2
STG_L1A = 10


def emit_l1attn(nc, p, C):
    d_wqkv = nc.dram_tensor("l1a_wqkv", [8, 3, 8, 128, 128], BF16, kind="ExternalInput").ap()
    d_wpj = nc.dram_tensor("l1a_wpj", [8, 128, 1024], BF16, kind="ExternalInput").ap()
    d_i7 = nc.dram_tensor("l1a_i7", [64, 3 * 64], F32, kind="ExternalInput").ap()

    X, IDB, IDF, ybf, stat = C["X"], C["IDB"], C["IDF"], C["ybf"], C["stat"]
    epsb, pb, pbt, banks = C["epsb"], C["pb"], C["pbt"], C["banks"]
    abf, af32 = C["abf"], C["af32"]
    YT = abf.take(DC * T)
    SVT = abf.take(DC * T)
    QKV = abf.take(3 * T)
    WQB = [abf.take(24 * 128)]
    WPJ = abf.take(DC * 1024)
    S3s = af32.take(T)[0:64, :]
    E3 = abf.take(T)[0:64, :]
    E3T = abf.take(16 * 64)
    VTM = abf.take(16 * 64)
    KN1T = abf.take(T)[0:64, :]
    WTs = abf.take(T)[0:64, :]
    KN1C = abf.take(64)
    SM = af32.take(3 * 64)
    QLT2 = abf.take(64)
    KLT2 = abf.take(64)
    _sm = af32.take(1056)
    K2 = _sm[0:64, 0:64]
    K2T = _sm[0:64, 64:128]
    Vcur = _sm[0:64, 128:192]
    VcurT = _sm[0:64, 192:256]
    KVs = _sm[0:64, 256:320]
    KVT = _sm[0:64, 320:384]
    T1 = _sm[0:64, 384:448]
    T2 = _sm[0:64, 448:512]
    T3 = _sm[0:64, 512:576]
    Vn = _sm[0:64, 576:640]
    I7 = _sm[0:64, 640:832]
    ONE64 = _sm[0:64, 832:833]
    ONER = _sm[0:1, 833:897]
    sden = _sm[0:64, 897:901]
    srow = _sm[0:1, 901:981]
    sm2 = _sm[:, 981:989]
    VnT = _sm[0:64, 989:1053]
    _zb = abf.take(128)
    Zb = _zb[0:64, 0:64]
    Bn = _zb[0:64, 64:128]

    p.dma("sync", I7[:], d_i7)
    p.dma("sync", WPJ[:].rearrange("p (s f) -> p s f", s=8), d_wpj.transpose([1, 0, 2]))
    consts_tok = p.tok("sync")
    x_toks = C.get("x_toks")

    def load_wq(hp):
        src = d_wqkv[hp].transpose([2, 0, 1, 3])
        p.dma("gpsimd", WQB[0][:].rearrange("p (a s f) -> p a s f", a=3, s=8), src)
        return p.tok("gpsimd")

    wq_tok = load_wq(0)

    p.wait("vector", consts_tok)
    p.wait("tensor", consts_tok)
    p.wait("scalar", consts_tok)
    p.emit("vector", lambda E, o=ONE64[:]: E.memset(o, 1.0))
    p.emit("vector", lambda E, o=ONER[:]: E.memset(o, 1.0))
    p.vdrain()
    bufs = C["bufs"]

    ln_done = emit_ln(p, X, YT, bufs, x_toks, eps=1e-5)
    p.wait("tensor", ln_done[-1])

    def mm64(bi, lhsT, rhs, n, extra=None, fp=False):
        bank = banks.acquire(bi)
        tmm = p.emit("tensor", lambda E, o=bank[0:lhsT.shape[1], 0:n], l=lhsT, r=rhs:
                     E.matmul(o, l, r, start=True, stop=True), inc=True)
        return bank, tmm

    qkv_dve = None
    for hp in range(NHP_L1A):
        buf = 0
        p.wait("tensor", wq_tok)
        for slot in range(3):
            for ts in range(4):
                bi = (slot * 4 + ts) % 4
                bank = banks.acquire(bi)
                tmm = None
                for dc in range(DC):
                    lhsT = WQB[buf][:, (slot * 8 + dc) * 128:(slot * 8 + dc) * 128 + 128]
                    rhs = YT[:, dc * T + ts * 512: dc * T + (ts + 1) * 512]
                    tmm = p.emit("tensor", lambda E, o=bank[:], l=lhsT, r=rhs,
                                 st=(dc == 0), sp=(dc == DC - 1):
                                 E.matmul(o, l, r, start=st, stop=sp), inc=(dc == DC - 1))
                dst = QKV[:, slot * T + ts * 512: slot * T + (ts + 1) * 512]
                if slot == 0:
                    p.wait("scalar", tmm)
                    p.wait("scalar", qkv_dve)
                    tev = p.emit("scalar", lambda E, o=dst, i=bank[:]:
                                 E.activation(o, i, AF.Identity, scale=0.125), inc=True)
                    eng = "scalar"
                else:
                    p.wait("vector", tmm)
                    p.wait("vector", qkv_dve)
                    tev = p.emit("vector", lambda E, o=dst, i=bank[:]:
                                 E.tensor_copy(o, i), inc=True)
                    eng = "vector"
                banks.release(bi, tev)
        qkv_done_v = p.tok("vector")
        qkv_done_s = p.tok("scalar")
        if hp < NHP_L1A - 1:
            p.wait("gpsimd", p.tok("tensor"))
            wq_tok = load_wq(hp + 1)

        for hh in range(NHH_L1A):
            h = 2 * hp + hh
            Qc = QKV[hh * 64:hh * 64 + 64, 0:T]
            Kc = QKV[hh * 64:hh * 64 + 64, T:2 * T]
            Vc = QKV[hh * 64:hh * 64 + 64, 2 * T:3 * T]
            QLT = QLT2[hh * 64:hh * 64 + 64, :]
            KLT = KLT2[hh * 64:hh * 64 + 64, :]
            p.wait("vector", qkv_done_v)
            p.wait("vector", qkv_done_s)
            p.wait("vector", p.tok("scalar"))
            p.emit("vector", lambda E, o=SM[0:64, 0:64], i=Qc.rearrange("p (m s) -> p m s", s=32):
                   E.tensor_reduce(o, i, AX.X, ALU.add))
            p.emit("vector", lambda E, o=SM[0:64, 64:128], i=Kc.rearrange("p (m s) -> p m s", s=32):
                   E.tensor_reduce(o, i, AX.X, ALU.add))
            p.vdrain()
            p.emit("vector", lambda E, o=QLT, i=SM[0:64, 0:64]:
                   E.tensor_scalar(o, i, 1.0 / 32, None, ALU.mult))
            tql = p.emit("vector", lambda E, o=KLT, i=SM[0:64, 64:128]:
                         E.tensor_scalar(o, i, 1.0 / 32, None, ALU.mult), inc=True)
            p.vdrain()
            if STG_L1A < 2:
                continue
            p.wait("tensor", tql)
            bank, tmm = mm64(4, QLT, KLT, 64)
            p.wait("vector", tmm)
            tng = p.emit("vector", lambda E, o=sden[:, 0:1], i=bank[0:64, 0:64]:
                         E.reduce_max(o, i, AX.X, negate=True), inc=True)
            p.vdrain()
            p.wait("scalar", tng)
            tex = p.emit("scalar", lambda E, o=K2[:], i=bank[0:64, 0:64], b=sden[:, 0:1],
                         a=sden[:, 1:2]:
                         E.activation(o, i, AF.Exp, bias=b, accum_out=a), inc=True)
            p.wait("vector", tex)
            banks.release(4, tex)
            p.emit("vector", lambda E, o=sden[:, 2:3], i=sden[:, 1:2]: E.reciprocal(o, i))
            p.vdrain()
            tk2 = p.emit("vector", lambda E, o=K2[:], i=K2[:], r=sden[:, 2:3]:
                         E.tensor_scalar(o, i, r, None, ALU.mult), inc=True)
            p.vdrain()
            p.wait("tensor", tk2)
            bank5 = banks.acquire(5)
            ttr = p.emit("tensor", lambda E, o=bank5[0:64, 0:64], i=K2[:], idn=IDF[0:64, 0:64]:
                         E.transpose(o, i, idn), inc=True)
            p.wait("vector", ttr)
            tk2t = p.emit("vector", lambda E, o=K2T[:], i=bank5[0:64, 0:64]:
                          E.tensor_copy(o, i), inc=True)
            banks.release(5, tk2t)
            if STG_L1A < 3:
                continue
            p.wait("tensor", tk2t)
            bank, tmm = mm64(4, K2[:], ONE64[:], 1)
            p.wait("vector", tmm)
            tcs = p.emit("vector", lambda E, o=sden[:, 3:4], i=bank[0:64, 0:1]:
                         E.tensor_copy(o, i), inc=True)
            banks.release(4, tcs)
            p.wait("tensor", tcs)
            bank5 = banks.acquire(5)
            ttr = p.emit("tensor", lambda E, o=bank5[0:1, 0:64], i=sden[:, 3:4],
                         idn=IDF[0:64, 0:64]: E.transpose(o, i, idn), inc=True)
            p.wait("vector", ttr)
            trow = p.emit("vector", lambda E, o=srow[:, 0:64], i=bank5[0:1, 0:64]:
                          E.tensor_copy(o, i), inc=True)
            banks.release(5, trow)
            p.vdrain()
            p.emit("vector", lambda E, o=srow[:, 64:65], i=srow[:, 0:64]:
                   E.reduce_max(o, i, AX.X))
            p.vdrain()
            trcp = p.emit("vector", lambda E, o=srow[:, 65:66], i=srow[:, 64:65]:
                          E.reciprocal(o, i), inc=True)
            p.wait("tensor", trcp)
            bank, tmm = mm64(4, ONER[:], srow[:, 65:66], 1)
            p.wait("vector", tmm)
            tden = p.emit("vector", lambda E, o=sden[:, 2:3], i=bank[0:64, 0:1]:
                          E.tensor_copy(o, i), inc=True)
            banks.release(4, tden)
            p.vdrain()
            p.emit("vector", lambda E, o=Vcur[:], i=K2T[:], r=sden[:, 2:3]:
                   E.tensor_scalar(o, i, r, None, ALU.mult))
            tv0 = p.emit("vector", lambda E, o=VcurT[:], i=K2[:], r=sden[:, 2:3]:
                         E.tensor_scalar(o, i, r, None, ALU.mult), inc=True)
            p.vdrain()
            if STG_L1A < 4:
                continue
            tprev = tv0
            for it in range(6):
                p.wait("tensor", tprev)
                bank, tmm = mm64(4, K2T[:], Vcur[:], 64)
                p.wait("vector", tmm)
                tkv = p.emit("vector", lambda E, o=KVs[:], i=bank[0:64, 0:64]:
                             E.tensor_copy(o, i), inc=True)
                banks.release(4, tkv)
                p.vdrain()
                t1t = p.emit("vector", lambda E, o=T1[:], i=KVs[:], s7=I7[:, 0:64]:
                             E.scalar_tensor_tensor(o, i, -1.0, s7, ALU.mult, ALU.add),
                             inc=True)
                p.wait("tensor", t1t)
                bank5 = banks.acquire(5)
                ttr = p.emit("tensor", lambda E, o=bank5[0:64, 0:64], i=KVs[:],
                             idn=IDF[0:64, 0:64]: E.transpose(o, i, idn), inc=True)
                p.wait("vector", ttr)
                tkvt = p.emit("vector", lambda E, o=KVT[:], i=bank5[0:64, 0:64]:
                              E.tensor_copy(o, i), inc=True)
                banks.release(5, tkvt)
                p.wait("tensor", tkvt)
                bank, tmm = mm64(4, KVT[:], T1[:], 64)
                p.wait("vector", tmm)
                tm2 = p.emit("vector", lambda E, o=T2[:], i=bank[0:64, 0:64],
                             s15=I7[:, 64:128]:
                             E.scalar_tensor_tensor(o, i, -1.0, s15, ALU.mult, ALU.add),
                             inc=True)
                banks.release(4, tm2)
                p.vdrain()
                p.wait("tensor", tm2)
                bank, tmm = mm64(4, KVT[:], T2[:], 64)
                p.wait("vector", tmm)
                tm3 = p.emit("vector", lambda E, o=T3[:], i=bank[0:64, 0:64],
                             s13=I7[:, 128:192]:
                             E.scalar_tensor_tensor(o, i, -1.0, s13, ALU.mult, ALU.add),
                             inc=True)
                banks.release(4, tm3)
                p.vdrain()
                p.wait("tensor", tm3)
                bank, tmm = mm64(4, VcurT[:], T3[:], 64)
                bank5, tmm2 = mm64(5, T3[:], VcurT[:], 64)
                p.wait("vector", tmm)
                p.wait("vector", tmm2)
                p.emit("vector", lambda E, o=Vn[:], i=bank[0:64, 0:64]:
                       E.tensor_scalar(o, i, 0.25, None, ALU.mult))
                tvn = p.emit("vector", lambda E, o=VnT[:], i=bank5[0:64, 0:64]:
                             E.tensor_scalar(o, i, 0.25, None, ALU.mult), inc=True)
                banks.release(4, tvn)
                banks.release(5, tvn)
                p.vdrain()
                p.emit("vector", lambda E, o=Vcur[:], i=Vn[:]: E.tensor_copy(o, i))
                tprev = p.emit("vector", lambda E, o=VcurT[:], i=VnT[:]:
                               E.tensor_copy(o, i), inc=True)
                p.vdrain()
            tzb = p.emit("vector", lambda E, o=Zb[:], i=Vcur[:]: E.tensor_copy(o, i),
                         inc=True)
            p.vdrain()

            if STG_L1A < 5:
                continue
            tex = None
            for tc in range(NCH):
                bi = tc % 2
                bank = banks.acquire(bi)
                lhsT = Qc[:, tc * 128:(tc + 1) * 128]
                tmm = p.emit("tensor", lambda E, o=bank[:, 0:64], l=lhsT, r=KLT:
                             E.matmul(o, l, r, start=True, stop=True), inc=True)
                p.wait("vector", tmm)
                p.wait("vector", tex)
                tng = p.emit("vector", lambda E, o=sm2[:, 0:1], i=bank[:, 0:64]:
                             E.reduce_max(o, i, AX.X, negate=True), inc=True)
                p.vdrain()
                p.wait("scalar", tng)
                tex = p.emit("scalar", lambda E, o=KN1C[:], i=bank[:, 0:64],
                             b=sm2[:, 0:1], a=sm2[:, 1:2]:
                             E.activation(o, i, AF.Exp, bias=b, accum_out=a), inc=True)
                banks.release(bi, tex)
                p.wait("vector", tex)
                p.emit("vector", lambda E, o=sm2[:, 2:3], i=sm2[:, 1:2]: E.reciprocal(o, i))
                p.vdrain()
                tkn = p.emit("vector", lambda E, o=KN1C[:], i=KN1C[:], r=sm2[:, 2:3]:
                             E.tensor_scalar(o, i, r, None, ALU.mult), inc=True)
                p.wait("tensor", tkn)
                slot = tc % 2
                p.wait("tensor", bufs["_ln_evac"][slot])
                tpe = p.emit("tensor", lambda E, o=pbt[slot][0:64, 0:128], i=KN1C[:],
                             idn=IDB[:]: E.transpose(o, i, idn), inc=True)
                p.wait("vector", tpe)
                bufs["_ln_evac"][slot] = p.emit(
                    "vector", lambda E, o=KN1T[:, tc * 128:(tc + 1) * 128],
                    i=pbt[slot][0:64, 0:128]: E.tensor_copy(o, i), inc=True)
            kn1_done = p.tok("vector")

            if STG_L1A < 6:
                continue
            for ts in range(4):
                bi = 2 + ts % 2
                bank = banks.acquire(bi)
                tmm = p.emit("tensor", lambda E, o=bank[0:64, 0:512], l=QLT,
                             r=Kc[:, ts * 512:(ts + 1) * 512]:
                             E.matmul(o, l, r, start=True, stop=True), inc=True)
                p.wait("vector", tmm)
                tcp = p.emit("vector", lambda E, o=S3s[:, ts * 512:(ts + 1) * 512],
                             i=bank[0:64, 0:512]: E.tensor_copy(o, i), inc=True)
                banks.release(bi, tcp)
            p.vdrain()
            p.wait("vector", p.tok("scalar"))
            tng = p.emit("vector", lambda E, o=sden[:, 0:1], i=S3s[:]:
                         E.reduce_max(o, i, AX.X, negate=True), inc=True)
            p.vdrain()
            p.wait("scalar", tng)
            tex = p.emit("scalar", lambda E, o=E3[:], i=S3s[:], b=sden[:, 0:1],
                         a=sden[:, 1:2]: E.activation(o, i, AF.Exp, bias=b,
                                                      accum_out=a), inc=True)
            p.wait("vector", tex)
            p.emit("vector", lambda E, o=sden[:, 2:3], i=sden[:, 1:2]: E.reciprocal(o, i))
            p.vdrain()
            if STG_L1A < 7:
                continue
            p.wait("tensor", tex)
            idv = IDB[0:64, 0:64] if hh == 0 else IDB[64:128, 64:128]
            for tc in range(NCH):
                slot = tc % 2
                p.wait("tensor", bufs["_ln_evac"][slot])
                tpe = p.emit("tensor", lambda E, o=pbt[slot][:, 0:64],
                             i=E3[:, tc * 128:(tc + 1) * 128], idn=IDB[0:64, 0:64]:
                             E.transpose(o, i, idn), inc=True)
                p.wait("vector", tpe)
                bufs["_ln_evac"][slot] = p.emit(
                    "vector", lambda E, o=E3T[:, tc * 64:(tc + 1) * 64],
                    i=pbt[slot][:, 0:64]: E.tensor_copy(o, i), inc=True)
                p.wait("tensor", bufs["_ln_evac"][slot])
                tpe = p.emit("tensor", lambda E, o=pbt[slot][:, 0:64],
                             i=Vc[:, tc * 128:(tc + 1) * 128], idn=idv:
                             E.transpose(o, i, idn), inc=True)
                p.wait("vector", tpe)
                bufs["_ln_evac"][slot] = p.emit(
                    "vector", lambda E, o=VTM[:, tc * 64:(tc + 1) * 64],
                    i=pbt[slot][:, 0:64]: E.tensor_copy(o, i), inc=True)
            tev3 = p.tok("vector")

            if STG_L1A < 8:
                continue
            p.wait("tensor", tev3)
            bank = banks.acquire(4)
            tmm = None
            for tc in range(NCH):
                tmm = p.emit("tensor", lambda E, o=bank[0:64, 0:64],
                             l=E3T[:, tc * 64:(tc + 1) * 64],
                             r=VTM[:, tc * 64:(tc + 1) * 64],
                             st=(tc == 0), sp=(tc == NCH - 1):
                             E.matmul(o, l, r, start=st, stop=sp), inc=(tc == NCH - 1))
            p.wait("vector", tmm)
            tb = p.emit("vector", lambda E, o=Bn[:], i=bank[0:64, 0:64], r=sden[:, 2:3]:
                        E.tensor_scalar(o, i, r, None, ALU.mult), inc=True)
            banks.release(4, tb)
            p.vdrain()

            if STG_L1A < 9:
                continue
            p.wait("tensor", tzb)
            p.wait("tensor", kn1_done)
            p.wait("tensor", tb)
            for ts in range(4):
                bi = 2 + ts % 2
                bank = banks.acquire(bi)
                tmm = p.emit("tensor", lambda E, o=bank[0:64, 0:512], l=Zb[:],
                             r=KN1T[:, ts * 512:(ts + 1) * 512]:
                             E.matmul(o, l, r, start=True, stop=True), inc=True)
                p.wait("vector", tmm)
                tcp = p.emit("vector", lambda E, o=WTs[:, ts * 512:(ts + 1) * 512],
                             i=bank[0:64, 0:512]: E.tensor_copy(o, i), inc=True)
                banks.release(bi, tcp)
            wt_done = p.tok("vector")
            if STG_L1A < 10:
                continue
            p.wait("tensor", wt_done)
            for ts in range(4):
                bi = 2 + ts % 2
                bank = banks.acquire(bi)
                tmm = p.emit("tensor", lambda E, o=bank[0:64, 0:512], l=Bn[:],
                             r=WTs[:, ts * 512:(ts + 1) * 512]:
                             E.matmul(o, l, r, start=True, stop=True), inc=True)
                dst = SVT[hh * 64:hh * 64 + 64, hp * T + ts * 512: hp * T + (ts + 1) * 512]
                p.wait("vector", tmm)
                tcp = p.emit("vector", lambda E, o=dst, i=bank[0:64, 0:512]:
                             E.tensor_copy(o, i), inc=True)
                banks.release(bi, tcp)
        qkv_dve = p.tok("vector")

    svt_done = p.tok("vector")
    p.wait("tensor", svt_done)
    for tc in range(NCH):
        for oh in range(2):
            bi = (tc * 2 + oh) % 4
            bank = banks.acquire(bi)
            tmm = None
            for dc in range(DC):
                lhsT = SVT[:, dc * T + tc * 128: dc * T + tc * 128 + 128]
                rhs = WPJ[:, dc * 1024 + oh * 512: dc * 1024 + (oh + 1) * 512]
                tmm = p.emit("tensor", lambda E, o=bank[:], l=lhsT, r=rhs,
                             st=(dc == 0), sp=(dc == DC - 1):
                             E.matmul(o, l, r, start=st, stop=sp), inc=(dc == DC - 1))
            xs = X[:, tc * D + oh * 512: tc * D + (oh + 1) * 512]
            p.wait("vector", tmm)
            tev = p.emit("vector", lambda E, o=xs, i=bank[:]:
                         E.tensor_add(o, o, i), inc=True)
            banks.release(bi, tev)



def prep_l1attn(inputs):
    wqkv = np.asarray(inputs["L1_attn_w_qkv"])
    wpj = np.asarray(inputs["L1_attn_w_proj"])
    wq = wqkv.reshape(8, 128, 3, 8, 128)
    wq = np.ascontiguousarray(wq.transpose(3, 2, 0, 1, 4)).astype(BT)
    i7 = np.concatenate([7 * np.eye(64), 15 * np.eye(64), 13 * np.eye(64)],
                        axis=1).astype(np.float32)
    return {
        "l1a_wqkv": wq,
        "l1a_wpj": np.ascontiguousarray(wpj.reshape(8, 128, 1024)).astype(BT),
        "l1a_i7": i7,
        "identb": np.eye(128, dtype=BT),
        "identf": np.eye(128, dtype=np.float32),
    }



NEXP_L1F = 16
STG_L1F = 3


def emit_l1ffn(nc, p, C):
    d_gate = nc.dram_tensor("l1f_gate", [D, 16], BF16, kind="ExternalInput").ap()
    d_w1 = nc.dram_tensor("l1f_w1", [16, 2, 4, 128, 2048], BF16, kind="ExternalInput").ap()
    d_w2 = nc.dram_tensor("l1f_w2", [16, 2, 8, 128, 1024], BF16, kind="ExternalInput").ap()
    d_b1 = nc.dram_tensor("l1f_b1c", [128, 256], F32, kind="ExternalInput").ap()
    d_b2 = nc.dram_tensor("l1f_b2", [16, 1024], F32, kind="ExternalInput").ap()

    X, IDB, IDF, ybf, stat = C["X"], C["IDB"], C["IDF"], C["ybf"], C["stat"]
    epsb, pb, pbt, banks = C["epsb"], C["pb"], C["pbt"], C["banks"]
    abf, af32 = C["abf"], C["af32"]
    YT = abf.take(DC * T)
    HID = abf.take(16 * 512)
    W1buf = [abf.take(4 * 2048), abf.take(4 * 2048)]
    W2buf = [abf.take(8 * 1024), abf.take(8 * 1024)]
    GATE = abf.take(DC * 16)
    B1C = af32.take(256)
    B2 = af32.take(1024)[0:16, :]
    GT = af32.take(NCH * 16)
    GTT = af32.take(2 * 128)[0:16, :]
    gsc = af32.take(NCH * 16)
    gm = af32.take(16)

    for ap, dr in ((B1C[:], d_b1), (B2[:], d_b2)):
        p.dma("sync", ap, dr)
    for dc in range(DC):
        p.dma("sync", GATE[:, dc * 16:(dc + 1) * 16], d_gate[dc * 128:(dc + 1) * 128, :])
    consts_tok = p.tok("sync")
    x_toks = C.get("x_toks")

    def load_w1(ei):
        for h in range(2):
            src = d_w1[ei, h].transpose([1, 0, 2])
            p.dma("gpsimd", W1buf[h][:].rearrange("p (s f) -> p s f", s=4), src)
        return p.tok("gpsimd")

    def load_w2(ei):
        for h in range(2):
            src = d_w2[ei, h].transpose([1, 0, 2])
            p.dma("gpsimd", W2buf[h][:].rearrange("p (s f) -> p s f", s=8), src)
        return p.tok("gpsimd")

    w1_tok = load_w1(0)
    w2_tok = load_w2(0)

    p.wait("vector", consts_tok)
    p.wait("tensor", consts_tok)
    p.wait("scalar", consts_tok)
    lnbufs = C["bufs"]
    ln_done = emit_ln(p, X, YT, lnbufs, x_toks, eps=1e-5)

    gtt_war = [None, None]
    for c in (range(NCH) if STG_L1F >= 2 else []):
        bi = c % 2
        bank = banks.acquire(bi)
        p.wait("tensor", ln_done[c])
        tpe = None
        for dc in range(DC):
            lhsT = YT[:, dc * T + c * 128: dc * T + c * 128 + 128]
            rhs = GATE[:, dc * 16:(dc + 1) * 16]
            tpe = p.emit("tensor", lambda E, o=bank[:, 0:16], l=lhsT, r=rhs,
                         st=(dc == 0), sp=(dc == DC - 1):
                         E.matmul(o, l, r, start=st, stop=sp),
                         inc=(dc == DC - 1))
        gs = gsc[:, c * 16:(c + 1) * 16]
        g8 = gs[:, 0:8]
        gtc = GT[:, c * 16:(c + 1) * 16]
        p.wait("vector", tpe)
        p.emit("vector", lambda E, o=g8, i=bank[:, 0:16]: E.max(o, i))
        p.vdrain()
        p.emit("vector", lambda E, o=gs[:, 8:9], i=gs[:, 0:1]:
               E.tensor_scalar(o, i, -1.0, None, ALU.mult))
        tmask = p.emit("vector", lambda E, o=gm[:, 0:16], i=bank[:, 0:16], m3=gs[:, 2:3]:
                       E.tensor_scalar(o, i, m3, None, ALU.is_ge), inc=True)
        p.vdrain()
        p.wait("scalar", tmask)
        texp = p.emit("scalar", lambda E, o=gtc, i=bank[:, 0:16], b=gs[:, 8:9]:
                      E.activation(o, i, AF.Exp, bias=b), inc=True)
        banks.release(bi, texp)
        p.wait("vector", texp)
        p.emit("vector", lambda E, o=gtc, i=gtc, m=gm[:, 0:16], s=gs[:, 9:10]:
               E.scalar_tensor_tensor(o, i, 1.0, m, ALU.mult, ALU.mult, accum_out=s))
        p.vdrain()
        p.emit("vector", lambda E, o=gs[:, 10:11], i=gs[:, 9:10]: E.reciprocal(o, i))
        p.vdrain()
        tgt = p.emit("vector", lambda E, o=gtc, i=gtc, r=gs[:, 10:11]:
                     E.tensor_scalar(o, i, r, None, ALU.mult), inc=True)
        if STG_L1F < 3:
            continue
        b2i = 2 + (c % 2)
        bank2 = banks.acquire(b2i)
        gslot = GTT[:, (c % 2) * 128:(c % 2) * 128 + 128]
        p.wait("tensor", tgt)
        ttr = p.emit("tensor", lambda E, o=bank2[0:16, 0:128], i=gtc, idn=IDF[:]:
                     E.transpose(o, i, idn), inc=True)
        p.wait("vector", ttr)
        p.wait("vector", gtt_war[c % 2])
        tcp = p.emit("vector", lambda E, o=gslot,
                     i=bank2[0:16, 0:128]: E.tensor_copy(o, i), inc=True)
        p.wait("tensor", tcp)
        tfree = None
        for oh in range(2):
            p.wait("tensor", tfree)
            tmm = p.emit("tensor", lambda E, o=bank2[:, 0:512], l=gslot,
                         r=B2[:, oh * 512:(oh + 1) * 512]:
                         E.matmul(o, l, r, start=True, stop=True), inc=True)
            p.wait("vector", tmm)
            tfree = p.emit("vector", lambda E, o=X[:, c * D + oh * 512: c * D + (oh + 1) * 512],
                           a=bank2[:, 0:512]: E.tensor_add(o, o, a), inc=True)
        banks.release(b2i, tfree)
        gtt_war[c % 2] = p.tok("tensor")
    p.vdrain()
    init_done_dve = p.tok("vector")

    hid_free_pe = None
    evac_done = {}
    for ei in range(NEXP_L1F):
        p.wait("tensor", w1_tok)
        mm2_last_pe = None
        for th in range(4):
            g = 0
            hid_evac_act = None
            hid_evac_dve = None
            for hc in range(16):
                bi = g % 4
                bank = banks.acquire(bi)
                tmm = None
                for dc in range(DC):
                    lhsT = W1buf[dc // 4][:, (dc % 4) * 2048 + hc * 128: (dc % 4) * 2048 + hc * 128 + 128]
                    rhs = YT[:, dc * T + th * 512: dc * T + (th + 1) * 512]
                    tmm = p.emit("tensor", lambda E, o=bank[:], l=lhsT, r=rhs,
                                 st=(dc == 0), sp=(dc == DC - 1):
                                 E.matmul(o, l, r, start=st, stop=sp),
                                 inc=(dc == DC - 1))
                dst = HID[:, hc * 512:(hc + 1) * 512]
                bcol = B1C[:, ei * 16 + hc: ei * 16 + hc + 1]
                eng = "scalar" if bi < 2 else "vector"
                p.wait(eng, tmm)
                p.wait(eng, hid_free_pe)
                if eng == "scalar":
                    tev = p.emit("scalar", lambda E, o=dst, i=bank[:], b=bcol:
                                 E.activation(o, i, AF.Relu, bias=b), inc=True)
                    hid_evac_act = tev
                else:
                    tev = p.emit("vector", lambda E, o=dst, i=bank[:], b=bcol:
                                 E.tensor_scalar(o, i, b, 0.0, ALU.add, ALU.max), inc=True)
                    hid_evac_dve = tev
                banks.release(bi, tev)
                g += 1
            if th == 3 and ei < NEXP_L1F - 1:
                mm1_pe_done = p.tok("tensor")
                p.wait("gpsimd", mm1_pe_done)
                w1_next = load_w1(ei + 1)
            if th == 0:
                p.wait("tensor", w2_tok)
            p.wait("tensor", hid_evac_act)
            p.wait("tensor", hid_evac_dve)
            g = 0
            for tc in range(4):
                gc = th * 4 + tc
                for oh in range(2):
                    bi = 4 + (g % 2)
                    bank = banks.acquire(bi)
                    tmm = None
                    for hc in range(16):
                        lhsT = HID[:, hc * 512 + tc * 128: hc * 512 + tc * 128 + 128]
                        rhs = W2buf[hc // 8][:, (hc % 8) * 1024 + oh * 512: (hc % 8) * 1024 + (oh + 1) * 512]
                        tmm = p.emit("tensor", lambda E, o=bank[:], l=lhsT, r=rhs,
                                     st=(hc == 0), sp=(hc == 15):
                                     E.matmul(o, l, r, start=st, stop=sp),
                                     inc=(hc == 15))
                    Xs = X[:, gc * D + oh * 512: gc * D + (oh + 1) * 512]
                    gcol = GT[:, gc * 16 + ei: gc * 16 + ei + 1]
                    p.wait("vector", tmm)
                    tev = p.emit("vector", lambda E, o=Xs, i=bank[:], s=gcol:
                                 E.scalar_tensor_tensor(o, i, s, o, ALU.mult, ALU.add),
                                 inc=True)
                    banks.release(bi, tev)
                    g += 1
            mm2_last_pe = p.tok("tensor")
            hid_free_pe = mm2_last_pe
        if ei < NEXP_L1F - 1:
            p.wait("gpsimd", mm2_last_pe)
            w2_tok = load_w2(ei + 1)
            w1_tok = w1_next




def prep_l1ffn(inputs):
    w1 = np.asarray(inputs["L1_ffn_W1"])
    w2 = np.asarray(inputs["L1_ffn_W2"])
    b1 = np.asarray(inputs["L1_ffn_b1"])
    b2 = np.asarray(inputs["L1_ffn_b2"])
    gate = np.asarray(inputs["L1_ffn_gate"])
    return {
        "l1f_gate": gate.astype(BT),
        "l1f_w1": np.ascontiguousarray(w1.reshape(16, 2, 4, 128, 2048)).astype(BT),
        "l1f_w2": np.ascontiguousarray(w2.reshape(16, 2, 8, 128, 1024)).astype(BT),
        "l1f_b1c": np.ascontiguousarray(
            b1.reshape(16, 16, 128).transpose(2, 0, 1).reshape(128, 256)).astype(np.float32),
        "l1f_b2": b2.astype(np.float32),
        "identb": np.eye(128, dtype=BT),
        "identf": np.eye(128, dtype=np.float32),
    }


def build_nc(build_fn):
    nc = bass.Bass("TRN2", target_bir_lowering=False)
    with ExitStack() as ctx:
        build_fn(nc, ctx)
    return nc


def run_launch(nc, prep, x_percore, n_cores=8):
    in_maps = []
    for c in range(n_cores):
        m = dict(prep)
        m["x_in"] = np.ascontiguousarray(x_percore[c], dtype=np.float32)
        in_maps.append(m)
    res = run_bass_kernel_spmd(nc, in_maps, core_ids=list(range(n_cores)))
    return np.stack([r["x_out"] for r in res.results])


def _standalone(emit_fn):
    def build(nc, ctx):
        p = Prog(nc, ctx)
        C = make_common(nc, ctx, p)
        d_x = nc.dram_tensor("x_in", [T, D], F32, kind="ExternalInput").ap()
        d_out = nc.dram_tensor("x_out", [T, D], F32, kind="ExternalOutput").ap()
        C["x_toks"] = emit_load_x(p, C["X"], d_x)
        emit_fn(nc, p, C)
        emit_store_x(p, d_out, C["X"], [p.tok("vector")])
        p.finalize()
    return build


build_l0attn = _standalone(emit_l0attn)
build_l0ffn = _standalone(emit_l0ffn)
build_l1attn = _standalone(emit_l1attn)
build_l1ffn = _standalone(emit_l1ffn)


FUSE_N = 4
FUSE_LIST = ["a0", "f0", "a1", "f1"]
PAD_NOPS = 0
FORCE_BARRIER = False


def build_fused(nc, ctx):
    p = Prog(nc, ctx)
    C = make_common(nc, ctx, p)
    d_x = nc.dram_tensor("x_in", [T, D], F32, kind="ExternalInput").ap()
    d_out = nc.dram_tensor("x_out", [T, D], F32, kind="ExternalOutput").ap()
    C["x_toks"] = emit_load_x(p, C["X"], d_x)
    _all = {"a0": emit_l0attn, "f0": emit_l0ffn, "a1": emit_l1attn, "f1": emit_l1ffn}
    fns = [_all[k] for k in FUSE_LIST[:FUSE_N]]
    for i, fn in enumerate(fns):
        C["abf"].reset()
        C["af32"].reset()
        fn(nc, p, C)
        if i < len(fns) - 1 or FORCE_BARRIER:
            emit_barrier(p, C)
            C["x_toks"] = None
    for _ in range(PAD_NOPS):
        p.emit("vector", lambda E: E.nop())
    emit_store_x(p, d_out, C["X"], [p.tok("vector")])
    p.finalize()


def prep_all(inputs):
    prep = prep_common()
    for pf in (prep_l0attn, prep_l0ffn, prep_l1attn, prep_l1ffn):
        prep.update(pf(inputs))
    return prep


_NC_CACHE = {}


def _get_nc(name, build_fn):
    if name not in _NC_CACHE:
        _NC_CACHE[name] = build_nc(build_fn)
    return _NC_CACHE[name]


def kernel(**inputs):
    x = np.ascontiguousarray(np.asarray(inputs["L0_attn_x"], dtype=np.float32))
    nc = _get_nc("fused", build_fused)
    prep = prep_all(inputs)
    return run_launch(nc, prep, x)


# version 04 (final)
# speedup vs baseline: 1.0060x; 1.0060x over previous
import numpy as np
import ml_dtypes
from contextlib import ExitStack

import concourse.bass as bass
import concourse.mybir as mybir
from concourse.bass_utils import run_bass_kernel_spmd

F32 = mybir.dt.float32
BF16 = mybir.dt.bfloat16
AF = mybir.ActivationFunctionType
ALU = mybir.AluOpType
AX = mybir.AxisListType
BT = ml_dtypes.bfloat16

T = 2048
D = 1024
NCH = T // 128
DC = D // 128


ENGINES = ("tensor", "vector", "scalar", "sync", "gpsimd")


class Prog:

    def __init__(self, nc, ctx):
        self.nc = nc
        self.ctx = ctx
        self.q = {e: [] for e in ENGINES}
        self.cnt = {}
        self.s = {}
        self.s_small = {}
        for e in ENGINES:
            h = ctx.enter_context(nc.semaphore("s_" + e))
            self.cnt[h] = 0
            self.s[e] = h
        for e in ("sync", "gpsimd"):
            h = ctx.enter_context(nc.semaphore("sm_" + e))
            self.cnt[h] = 0
            self.s_small[e] = h
        self.s_barrier = {}
        for e in ENGINES:
            h = ctx.enter_context(nc.semaphore("sb_" + e))
            self.cnt[h] = 0
            self.s_barrier[e] = h

    def emit(self, eng, fn, inc=False):
        incs = []
        if inc:
            incs = [(self.s[eng], 1)]
            self.cnt[self.s[eng]] += 1
        self.q[eng].append((fn, incs))
        return self.tok(eng)

    def tok(self, eng):
        s = self.s[eng]
        return (s, self.cnt[s])

    def wait(self, eng, tok):
        if tok is None:
            return
        if isinstance(tok, list):
            for t in tok:
                self.wait(eng, t)
            return
        s, v = tok
        if v <= 0:
            return
        self.q[eng].append((lambda E, s=s, v=v: E.wait_ge(s, v), []))

    def vdrain(self):
        self.q["vector"].append((lambda E: E.drain(), []))

    def dma(self, queue, out_ap, in_ap):
        sb = out_ap if str(out_ap.space).lower().find("sb") >= 0 else in_ap
        try:
            parts = sb.shape[0]
            perpart = sb.free_nbytes()
        except Exception:
            parts, perpart = 128, 4096
        small = parts < 128 or perpart < 512
        s = self.s_small[queue] if small else self.s[queue]
        self.cnt[s] += 16
        self.q[queue].append(
            (lambda E, o=out_ap, i=in_ap: E.dma_start(o, i), [(s, 16)]))
        tok = (s, self.cnt[s])
        if small:
            self.wait(queue, tok)
        return tok

    def finalize(self):
        block = self.ctx.enter_context(self.nc.Block())
        for ename in ENGINES:
            items = self.q[ename]

            def body(E, items=items):
                for fn, incs in items:
                    inst = fn(E)
                    for s, k in incs:
                        inst.then_inc(s, k)

            getattr(block, ename)(body)


class Banks:

    def __init__(self, p, pb):
        self.p = p
        self.pb = pb
        self.free_tok = [None] * len(pb)

    def acquire(self, i):
        self.p.wait("tensor", self.free_tok[i])
        return self.pb[i]

    def release(self, i, tok):
        self.free_tok[i] = tok



def emit_load_x(p, X, d_x):
    return [p.dma("sync", X[:, c * D:(c + 1) * D], d_x[c * 128:(c + 1) * 128, :])
            for c in range(NCH)]


def emit_store_x(p, d_out, X, wait_toks):
    for tk in wait_toks:
        p.wait("sync", tk)
    last = None
    for c in range(NCH):
        last = p.dma("sync", d_out[c * 128:(c + 1) * 128, :], X[:, c * D:(c + 1) * D])
    p.wait("sync", last)


def emit_ln(p, X, YT, bufs, x_toks, eps=1e-5, y_tm=None, chunks=None, yt_T=T):
    if chunks is None:
        chunks = list(range(NCH))
    ybf, stat, pbt, identb = bufs["ybf"], bufs["stat"], bufs["pbt"], bufs["identb"]
    evac_tok = bufs.setdefault("_ln_evac", [None, None])
    ybf_war = bufs.setdefault("_ln_ybf_war", [None, None])
    act_tok = bufs.setdefault("_ln_act", [None, None])
    chunk_done = []
    for li, c in enumerate(chunks):
        xc = X[:, c * D:(c + 1) * D]
        sl = stat[:, (li % 2) * 16:(li % 2) * 16 + 16]
        if x_toks is not None:
            p.wait("vector", x_toks[c])
        p.wait("vector", act_tok[li % 2])
        p.emit("vector", lambda E, a=sl[:, 0:6], b=xc[:, 0:512]: E.bn_stats(a, b))
        p.emit("vector", lambda E, a=sl[:, 6:12], b=xc[:, 512:1024]: E.bn_stats(a, b))
        p.vdrain()
        tdve = p.emit("vector", lambda E, a=sl[:, 12:14], b=sl[:, 0:12]:
                      E.bn_aggr(a, b), inc=True)
        p.wait("scalar", tdve)
        p.emit("scalar", lambda E, o=sl[:, 14:15], i=sl[:, 13:14], ee=bufs["epsb"]:
               E.activation(o, i, AF.Ln, bias=ee))
        tact = p.emit("scalar", lambda E, o=sl[:, 15:16], i=sl[:, 14:15]:
                      E.activation(o, i, AF.Exp, scale=-0.5), inc=True)
        act_tok[li % 2] = tact
        yb = ybf[li % 2]
        p.wait("vector", tact)
        p.wait("vector", ybf_war[li % 2])
        p.vdrain()
        if y_tm is not None:
            p.emit("vector", lambda E, o=y_tm[:, li * D:(li + 1) * D], i=xc,
                   m=sl[:, 12:13], r=sl[:, 15:16]:
                   E.tensor_scalar(o, i, m, r, ALU.subtract, ALU.mult))
            p.vdrain()
            tapply = p.emit("vector", lambda E, o=yb[:], i=y_tm[:, li * D:(li + 1) * D]:
                            E.tensor_copy(o, i), inc=True)
        else:
            tapply = p.emit("vector", lambda E, o=yb[:], i=xc,
                            m=sl[:, 12:13], r=sl[:, 15:16]:
                            E.tensor_scalar(o, i, m, r, ALU.subtract, ALU.mult), inc=True)
        p.wait("tensor", tapply)
        emit_transposes(p, bufs, yb, YT, li, yt_T)
        ybf_war[li % 2] = p.tok("tensor")
        chunk_done.append(p.tok("vector"))
    return chunk_done


def emit_transposes(p, bufs, yb, YT, li, yt_T):
    pbt, identb = bufs["pbt"], bufs["identb"]
    evac_tok = bufs.setdefault("_ln_evac", [None, None])
    for dc in range(DC):
        slot = dc % 2
        p.wait("tensor", evac_tok[slot])
        tpe = p.emit("tensor", lambda E, o=pbt[slot][:, 0:128],
                     i=yb[:, dc * 128:dc * 128 + 128], idn=identb[:]:
                     E.transpose(o, i, idn), inc=True)
        p.wait("vector", tpe)
        evac_tok[slot] = p.emit(
            "vector", lambda E, o=YT[:, dc * yt_T + li * 128: dc * yt_T + li * 128 + 128],
            i=pbt[slot][:, 0:128]: E.tensor_copy(o, i), inc=True)


def emit_cast_transpose(p, bufs, src_f32, YT, li, yt_T):
    ybf = bufs["ybf"]
    ybf_war = bufs.setdefault("_ln_ybf_war", [None, None])
    yb = ybf[li % 2]
    p.wait("vector", ybf_war[li % 2])
    p.vdrain()
    tcast = p.emit("vector", lambda E, o=yb[:], i=src_f32: E.tensor_copy(o, i), inc=True)
    p.wait("tensor", tcast)
    emit_transposes(p, bufs, yb, YT, li, yt_T)
    ybf_war[li % 2] = p.tok("tensor")



class Arena:

    def __init__(self, t):
        self.t = t
        self.n = t.shape[1]
        self.off = 0

    def take(self, cols):
        v = self.t[:, self.off:self.off + cols]
        self.off += cols
        assert self.off <= self.n, f"arena overflow {self.off} > {self.n}"
        return v

    def reset(self):
        self.off = 0


ABF_COLS = 58700
AF32_COLS = 5830


def make_common(nc, ctx, p):
    e = ctx.enter_context
    C = {}
    C["X"] = e(nc.sbuf_tensor("X", [128, NCH * D], F32))
    C["IDB"] = e(nc.sbuf_tensor("IDB", [128, 128], BF16))
    C["IDF"] = e(nc.sbuf_tensor("IDF", [128, 128], F32))
    C["ybf"] = [e(nc.sbuf_tensor("ybf0", [128, D], BF16)),
                e(nc.sbuf_tensor("ybf1", [128, D], BF16))]
    C["stat"] = e(nc.sbuf_tensor("stat", [128, 32], F32))
    C["epsb"] = e(nc.sbuf_tensor("epsb", [128, 1], F32))
    C["eps6"] = e(nc.sbuf_tensor("eps6", [128, 1], F32))
    C["abf"] = Arena(e(nc.sbuf_tensor("ABF", [128, ABF_COLS], BF16)))
    C["af32"] = Arena(e(nc.sbuf_tensor("AF32", [128, AF32_COLS], F32)))
    C["pb"] = [e(nc.psum_tensor(f"pb{i}", [128, 512], F32)) for i in range(6)]
    C["pbt"] = [e(nc.psum_tensor("pbt0", [128, 256], BF16)),
                e(nc.psum_tensor("pbt1", [128, 256], BF16))]
    C["banks"] = Banks(p, C["pb"])
    d_idb = nc.dram_tensor("identb", [128, 128], BF16, kind="ExternalInput").ap()
    d_idf = nc.dram_tensor("identf", [128, 128], F32, kind="ExternalInput").ap()
    p.dma("sync", C["IDB"][:], d_idb)
    p.dma("sync", C["IDF"][:], d_idf)
    ct = p.tok("sync")
    for en in ("vector", "tensor", "scalar"):
        p.wait(en, ct)
    p.emit("vector", lambda E, o=C["epsb"][:]: E.memset(o, 1e-5))
    p.emit("vector", lambda E, o=C["eps6"][:]: E.memset(o, 1e-6))
    p.vdrain()
    C["bufs"] = dict(ybf=C["ybf"], stat=C["stat"], pbt=C["pbt"], identb=C["IDB"],
                     epsb=C["epsb"][:])
    return C


BARRIER_WAIT_ON = list(ENGINES)


def emit_barrier(p, C):
    p.vdrain()
    toks = {}
    for en in ENGINES:
        sb = p.s_barrier[en]
        p.cnt[sb] += 1
        if en == "tensor":
            bank0 = C["banks"].acquire(0)
            p.q[en].append((lambda E, o=bank0[0:128, 0:128], l=C["IDB"][:],
                            r=C["IDB"][:]: E.matmul(o, l, r, start=True, stop=True),
                            [(sb, 1)]))
        else:
            p.q[en].append((lambda E: E.nop(nofuse=True), [(sb, 1)]))
        toks[en] = (sb, p.cnt[sb])
    for en in ENGINES:
        for other in ENGINES:
            if other != en and other in BARRIER_WAIT_ON:
                p.wait(en, toks[other])
    bt = p.tok("tensor")
    for i in range(len(C["banks"].free_tok)):
        C["banks"].free_tok[i] = toks["vector"]
    C["bufs"]["_ln_evac"] = [toks["vector"], toks["vector"]]
    C["bufs"]["_ln_ybf_war"] = [toks["tensor"], toks["tensor"]]
    C["bufs"]["_ln_act"] = [toks["scalar"], toks["scalar"]]


def prep_common():
    return {"identb": np.eye(128, dtype=BT), "identf": np.eye(128, dtype=np.float32)}


def mm_acc(p, banks, bi, ncols, parts, out_base=0, npart=128):
    bank = banks.acquire(bi)
    n = len(parts)
    tmm = None
    for i, (lhsT, rhs) in enumerate(parts):
        tmm = p.emit("tensor", lambda E, o=bank[out_base:out_base + npart, 0:ncols],
                     l=lhsT, r=rhs, st=(i == 0), sp=(i == n - 1):
                     E.matmul(o, l, r, start=st, stop=sp), inc=(i == n - 1))
    return bank, tmm


def emit_top_gate(p, gs, gm, gout, bank_ap, nexp, kth, drain=True):
    p.emit("vector", lambda E, o=gs[:, 0:8], i=bank_ap: E.max(o, i))
    p.vdrain()
    p.emit("vector", lambda E, o=gs[:, 8:9], i=gs[:, 0:1]:
           E.tensor_scalar(o, i, -1.0, None, ALU.mult))
    tmask = p.emit("vector", lambda E, o=gm[:, 0:nexp], i=bank_ap, mk=gs[:, kth:kth + 1]:
                   E.tensor_scalar(o, i, mk, None, ALU.is_ge), inc=True)
    p.vdrain()
    p.wait("scalar", tmask)
    texp = p.emit("scalar", lambda E, o=gout, i=bank_ap, b=gs[:, 8:9]:
                  E.activation(o, i, AF.Exp, bias=b), inc=True)
    p.wait("vector", texp)
    p.emit("vector", lambda E, o=gout, i=gout, m=gm[:, 0:nexp], s=gs[:, 9:10]:
           E.scalar_tensor_tensor(o, i, 1.0, m, ALU.mult, ALU.mult, accum_out=s))
    p.vdrain()
    p.emit("vector", lambda E, o=gs[:, 10:11], i=gs[:, 9:10]: E.reciprocal(o, i))
    p.vdrain()
    tgt = p.emit("vector", lambda E, o=gout, i=gout, r=gs[:, 10:11]:
                 E.tensor_scalar(o, i, r, None, ALU.mult), inc=True)
    if drain:
        p.vdrain()
    return tgt, texp



Q = 512
QC = Q // 128
NQ = T // Q


def emit_l0ffn(nc, p, C):
    d_dwk = nc.dram_tensor("l0f_dwk", [128, 24], F32, kind="ExternalInput").ap()
    d_dwb = nc.dram_tensor("l0f_dwb", [128, 8], F32, kind="ExternalInput").ap()
    d_pwt = nc.dram_tensor("l0f_pwt", [8, 128, 1024], BF16, kind="ExternalInput").ap()
    d_pwb = nc.dram_tensor("l0f_pwb", [1, 1024], BF16, kind="ExternalInput").ap()
    d_rwt = nc.dram_tensor("l0f_rwt", [8, 128, 8], BF16, kind="ExternalInput").ap()
    d_rb = nc.dram_tensor("l0f_rb", [1, 8], BF16, kind="ExternalInput").ap()
    d_ewt = nc.dram_tensor("l0f_ewt", [8, 8, 128, 1024], BF16, kind="ExternalInput").ap()
    d_eb = nc.dram_tensor("l0f_eb", [1, 8 * 1024], BF16, kind="ExternalInput").ap()
    d_owt = nc.dram_tensor("l0f_owt", [8, 128, 1024], BF16, kind="ExternalInput").ap()
    d_outb = nc.dram_tensor("l0f_outb", [1, 1024], BF16, kind="ExternalInput").ap()
    d_nwb = nc.dram_tensor("l0f_nwb", [128, 1024], F32, kind="ExternalInput").ap()
    d_fdt = nc.dram_tensor("l0f_fdt", [8, 128, 256], BF16, kind="ExternalInput").ap()
    d_fdb = nc.dram_tensor("l0f_fdb", [128, 2], F32, kind="ExternalInput").ap()
    d_fut = nc.dram_tensor("l0f_fut", [2, 128, 1024], BF16, kind="ExternalInput").ap()
    d_fub = nc.dram_tensor("l0f_fub", [1, 1024], BF16, kind="ExternalInput").ap()
    d_ones = nc.dram_tensor("onesb", [1, 128], BF16, kind="ExternalInput").ap()

    X, IDB, ybf, stat = C["X"], C["IDB"], C["ybf"], C["stat"]
    epsb, eps6, pb, pbt, banks = C["epsb"], C["eps6"], C["pb"], C["pbt"], C["banks"]
    abf, af32 = C["abf"], C["af32"]
    X1 = af32.take(QC * D)
    YTq = abf.take(DC * Q)
    CTq = abf.take(DC * Q)
    XTq = YTq
    CMB = abf.take(QC * D)
    PWT = abf.take(DC * 1024)
    OWT = abf.take(DC * 1024)
    FDT = abf.take(DC * 256)
    FUT = abf.take(2 * 1024)
    EWT = [abf.take(DC * 1024), abf.take(DC * 1024)]
    RWT = abf.take(DC * 8)
    NWB = af32.take(1024)
    DWK = af32.take(24)
    DWB = af32.take(8)
    FDB = af32.take(2)
    ROWS = abf.take(3 * 1024 + 8 + 128)[0:1, :]
    EBR = [abf.take(1024)[0:1, :], abf.take(1024)[0:1, :]]
    RW = af32.take(NCH * 8)
    gsc = af32.take(16)
    gm = af32.take(16)
    tmb = [abf.take(512), abf.take(512)]
    yet = [abf.take(512), abf.take(512)]
    TCIN = abf.take(Q + 2)
    DISC = abf.take(1024)
    tail = abf.take(DC * 2)
    cacc = af32.take(Q)
    tmpf = [cacc]

    R_PWB, R_OUTB, R_FUB, R_RB, R_ONES = 0, 1024, 2048, 3072, 3080
    p.dma("sync", DWK[:], d_dwk)
    p.dma("sync", DWB[:], d_dwb)
    p.dma("sync", FDB[:], d_fdb)
    p.dma("sync", NWB[:], d_nwb)
    p.dma("sync", ROWS[:, R_PWB:R_PWB + 1024], d_pwb)
    p.dma("sync", ROWS[:, R_OUTB:R_OUTB + 1024], d_outb)
    p.dma("sync", ROWS[:, R_FUB:R_FUB + 1024], d_fub)
    p.dma("sync", ROWS[:, R_RB:R_RB + 8], d_rb)
    p.dma("sync", ROWS[:, R_ONES:R_ONES + 128], d_ones)
    for dc in range(DC):
        p.dma("sync", RWT[:, dc * 8:(dc + 1) * 8], d_rwt[dc])
    p.dma("sync", PWT[:].rearrange("p (s f) -> p s f", s=8), d_pwt.transpose([1, 0, 2]))
    p.dma("sync", OWT[:].rearrange("p (s f) -> p s f", s=8), d_owt.transpose([1, 0, 2]))
    p.dma("sync", FDT[:].rearrange("p (s f) -> p s f", s=8), d_fdt.transpose([1, 0, 2]))
    p.dma("sync", FUT[:].rearrange("p (s f) -> p s f", s=2), d_fut.transpose([1, 0, 2]))
    consts_tok = p.tok("sync")
    x_toks = C.get("x_toks")

    def load_ewt(ei, buf):
        p.dma("gpsimd", EWT[buf][:].rearrange("p (s f) -> p s f", s=8),
              d_ewt[ei].transpose([1, 0, 2]))
        p.dma("gpsimd", EBR[buf][:], d_eb[:, ei * 1024:(ei + 1) * 1024])
        return p.tok("gpsimd")

    ONES = ROWS[:, R_ONES:R_ONES + 128]
    ewt_tok = [load_ewt(0, 0), None]

    p.wait("vector", consts_tok)
    p.wait("tensor", consts_tok)
    p.wait("scalar", consts_tok)
    p.emit("vector", lambda E, o=tail[:]: E.memset(o, 0.0))
    p.vdrain()
    bufs = C["bufs"]

    tmb_war = [None, None]
    yet_war = [None, None]
    ewt_pe_done = [None, None]
    for q in range(NQ):
        chunks = list(range(q * QC, (q + 1) * QC))
        ln_done = emit_ln(p, X, YTq, bufs, x_toks, eps=1e-5, y_tm=X1,
                          chunks=chunks, yt_T=Q)
        for dc in range(DC):
            w0 = DWK[:, dc * 3:dc * 3 + 1]
            w1 = DWK[:, dc * 3 + 1:dc * 3 + 2]
            w2 = DWK[:, dc * 3 + 2:dc * 3 + 3]
            bcol = DWB[:, dc:dc + 1]
            ytc = YTq[:, dc * Q:(dc + 1) * Q]
            p.wait("vector", ln_done[-1])
            p.vdrain()
            p.emit("vector", lambda E, o=TCIN[:, 0:2], i=tail[:, dc * 2:dc * 2 + 2]:
                   E.tensor_copy(o, i))
            p.emit("vector", lambda E, o=TCIN[:, 2:Q + 2], i=ytc: E.tensor_copy(o, i))
            p.emit("vector", lambda E, o=tail[:, dc * 2:dc * 2 + 2], i=ytc[:, Q - 2:Q]:
                   E.tensor_copy(o, i))
            p.vdrain()
            p.emit("vector", lambda E, o=cacc[:], i=TCIN[:, 2:Q + 2], a=w2, b=bcol:
                   E.tensor_scalar(o, i, a, b, ALU.mult, ALU.add))
            p.vdrain()
            p.emit("vector", lambda E, o=cacc[:], i=TCIN[:, 1:Q + 1], a=w1, x=cacc[:]:
                   E.scalar_tensor_tensor(o, i, a, x, ALU.mult, ALU.add))
            p.vdrain()
            p.emit("vector", lambda E, o=CTq[:, dc * Q:(dc + 1) * Q], i=TCIN[:, 0:Q],
                   a=w0, x=cacc[:]:
                   E.scalar_tensor_tensor(o, i, a, x, ALU.mult, ALU.add), inc=True)
        p.vdrain()
        conv_done = p.tok("vector")

        p.wait("tensor", conv_done)
        for tc in range(QC):
            gc = q * QC + tc
            for eh in range(2):
                parts = [(CTq[:, dc * Q + tc * 128: dc * Q + tc * 128 + 128],
                          PWT[:, dc * 1024 + eh * 512: dc * 1024 + (eh + 1) * 512])
                         for dc in range(DC)]
                parts.append((ONES, ROWS[:, R_PWB + eh * 512:R_PWB + (eh + 1) * 512]))
                bank, tmm = mm_acc(p, banks, (tc * 2 + eh) % 4, 512, parts)
                tb = tmb[eh]
                p.wait("scalar", tmm)
                p.wait("scalar", tmb_war[eh])
                tev = p.emit("scalar", lambda E, o=tb[:], i=bank[:]:
                             E.activation(o, i, AF.Silu), inc=True)
                banks.release((tc * 2 + eh) % 4, tev)
                x1s = X1[:, tc * D + eh * 512: tc * D + (eh + 1) * 512]
                p.wait("vector", tev)
                tadd = p.emit("vector", lambda E, o=x1s, i=x1s, b=tb[:]:
                              E.tensor_add(o, i, b), inc=True)
                tmb_war[eh] = tadd
            p.vdrain()
            emit_cast_transpose(p, bufs, X1[:, tc * D:(tc + 1) * D], XTq, tc, Q)
            parts = [(XTq[:, dc * Q + tc * 128: dc * Q + tc * 128 + 128],
                      RWT[:, dc * 8:(dc + 1) * 8]) for dc in range(DC)]
            parts.append((ONES, ROWS[:, R_RB:R_RB + 8]))
            xt_tok = p.tok("vector")
            p.wait("tensor", xt_tok)
            bank, tmm = mm_acc(p, banks, 4 + tc % 2, 8, parts)
            p.wait("vector", tmm)
            rwc = RW[:, gc * 8:(gc + 1) * 8]
            tgt, texp = emit_top_gate(p, gsc, gm, rwc, bank[:, 0:8], 8, 1)
            banks.release(4 + tc % 2, texp)

        gb = 0
        for ei in range(8):
            cur = ei % 2
            p.wait("tensor", ewt_tok[cur])
            for tc in range(QC):
                gc = q * QC + tc
                for oh in range(2):
                    parts = [(XTq[:, dc * Q + tc * 128: dc * Q + tc * 128 + 128],
                              EWT[cur][:, dc * 1024 + oh * 512: dc * 1024 + (oh + 1) * 512])
                             for dc in range(DC)]
                    parts.append((ONES, EBR[cur][:, oh * 512:(oh + 1) * 512]))
                    bank, tmm = mm_acc(p, banks, gb % 4, 512, parts)
                    ys = yet[gb % 2]
                    p.wait("scalar", tmm)
                    p.wait("scalar", yet_war[gb % 2])
                    tev = p.emit("scalar", lambda E, o=ys[:], i=bank[:]:
                                 E.activation(o, i, AF.Silu), inc=True)
                    banks.release(gb % 4, tev)
                    dst = CMB[:, tc * D + oh * 512: tc * D + (oh + 1) * 512]
                    rcol = RW[:, gc * 8 + ei: gc * 8 + ei + 1]
                    p.wait("vector", tev)
                    if ei == 0:
                        tacc = p.emit("vector", lambda E, o=dst, i=ys[:], r=rcol:
                                      E.tensor_scalar(o, i, r, None, ALU.mult), inc=True)
                    else:
                        tacc = p.emit("vector", lambda E, o=dst, i=ys[:], r=rcol:
                                      E.scalar_tensor_tensor(o, i, r, o, ALU.mult, ALU.add),
                                      inc=True)
                    yet_war[gb % 2] = tacc
                    gb += 1
            ewt_pe_done[cur] = p.tok("tensor")
            nxt = ei + 1 if ei < 7 else (0 if q < NQ - 1 else None)
            if nxt is not None:
                p.wait("gpsimd", ewt_pe_done[1 - cur])
                ewt_tok[1 - cur] = load_ewt(nxt, 1 - cur)
        p.vdrain()

        for tc in range(QC):
            emit_cast_transpose(p, bufs, CMB[:, tc * D:(tc + 1) * D], CTq, tc, Q)
        cmbt_tok = p.tok("vector")

        p.wait("tensor", cmbt_tok)
        for tc in range(QC):
            for oh in range(2):
                parts = [(CTq[:, dc * Q + tc * 128: dc * Q + tc * 128 + 128],
                          OWT[:, dc * 1024 + oh * 512: dc * 1024 + (oh + 1) * 512])
                         for dc in range(DC)]
                parts.append((ONES, ROWS[:, R_OUTB + oh * 512:R_OUTB + (oh + 1) * 512]))
                bank, tmm = mm_acc(p, banks, (tc * 2 + oh) % 4, 512, parts)
                dst = X1[:, tc * D + oh * 512: tc * D + (oh + 1) * 512]
                p.wait("vector", tmm)
                tev = p.emit("vector", lambda E, o=dst, i=bank[:]:
                             E.tensor_add(o, o, i), inc=True)
                banks.release((tc * 2 + oh) % 4, tev)

        for tc in range(QC):
            zc = X1[:, tc * D:(tc + 1) * D]
            sl = stat[:, (tc % 2) * 16:(tc % 2) * 16 + 16]
            tz = p.tok("vector")
            p.wait("scalar", tz)
            p.wait("scalar", bufs["_ln_act"][tc % 2])
            tsq = p.emit("scalar", lambda E, o=DISC[:], i=zc, a=sl[:, 0:1]:
                         E.activation(o, i, AF.Square, accum_out=a))
            p.emit("scalar", lambda E, o=sl[:, 1:2], i=sl[:, 0:1], ee=eps6[:]:
                   E.activation(o, i, AF.Ln, bias=ee, scale=1.0 / D))
            tact = p.emit("scalar", lambda E, o=sl[:, 2:3], i=sl[:, 1:2]:
                          E.activation(o, i, AF.Exp, scale=-0.5), inc=True)
            bufs["_ln_act"][tc % 2] = tact
            p.wait("vector", tact)
            p.vdrain()
            th = p.emit("vector", lambda E, o=zc, i=zc, r=sl[:, 2:3], n=NWB[:]:
                        E.scalar_tensor_tensor(o, i, r, n, ALU.mult, ALU.mult), inc=True)
            p.vdrain()
            emit_cast_transpose(p, bufs, zc, XTq, tc, Q)
        ht_tok = p.tok("vector")

        FTq = CTq
        p.wait("tensor", ht_tok)
        for hh in range(2):
            parts = [(FDT[:, dc * 256 + hh * 128: dc * 256 + (hh + 1) * 128],
                      XTq[:, dc * Q:(dc + 1) * Q]) for dc in range(DC)]
            bank, tmm = mm_acc(p, banks, 4 + hh, Q, parts)
            p.wait("scalar", tmm)
            tev = p.emit("scalar", lambda E, o=FTq[:, hh * Q:(hh + 1) * Q], i=bank[:],
                         b=FDB[:, hh:hh + 1]: E.activation(o, i, AF.Gelu, bias=b), inc=True)
            banks.release(4 + hh, tev)
        ff1_tok = p.tok("scalar")

        p.wait("tensor", ff1_tok)
        for tc in range(QC):
            gc = q * QC + tc
            for oh in range(2):
                parts = [(FTq[:, hh * Q + tc * 128: hh * Q + tc * 128 + 128],
                          FUT[:, hh * 1024 + oh * 512: hh * 1024 + (oh + 1) * 512])
                         for hh in range(2)]
                parts.append((ONES, ROWS[:, R_FUB + oh * 512:R_FUB + (oh + 1) * 512]))
                bank, tmm = mm_acc(p, banks, (tc * 2 + oh) % 4, 512, parts)
                tf = tmpf[0]
                zs = X1[:, tc * D + oh * 512: tc * D + (oh + 1) * 512]
                xs = X[:, gc * D + oh * 512: gc * D + (oh + 1) * 512]
                p.wait("vector", tmm)
                p.emit("vector", lambda E, o=tf[:], i=bank[:], z=zs:
                       E.scalar_tensor_tensor(o, i, 1.0, z, ALU.mult, ALU.add))
                p.vdrain()
                tev = p.emit("vector", lambda E, o=xs, i=tf[:]:
                             E.tensor_add(o, o, i), inc=True)
                banks.release((tc * 2 + oh) % 4, tev)



def prep_l0ffn(inputs):
    dwk = np.asarray(inputs["L0_ffn_dw_kernel"]).reshape(8, 128, 3).transpose(1, 0, 2)
    out = {
        "l0f_dwk": np.ascontiguousarray(dwk.reshape(128, 24), dtype=np.float32),
        "l0f_dwb": np.ascontiguousarray(
            np.asarray(inputs["L0_ffn_dw_bias"]).reshape(8, 128).T, dtype=np.float32),
        "l0f_pwt": np.ascontiguousarray(
            np.asarray(inputs["L0_ffn_pw_w"]).T.reshape(8, 128, 1024)).astype(BT),
        "l0f_pwb": np.asarray(inputs["L0_ffn_pw_b"]).reshape(1, 1024).astype(BT),
        "l0f_rwt": np.ascontiguousarray(
            np.asarray(inputs["L0_ffn_router_w"]).T.reshape(8, 128, 8)).astype(BT),
        "l0f_rb": np.asarray(inputs["L0_ffn_router_b"]).reshape(1, 8).astype(BT),
        "l0f_ewt": np.ascontiguousarray(
            np.asarray(inputs["L0_ffn_expert_w"]).transpose(0, 2, 1).reshape(8, 8, 128, 1024)).astype(BT),
        "l0f_eb": np.asarray(inputs["L0_ffn_expert_b"]).reshape(1, 8 * 1024).astype(BT),
        "l0f_owt": np.ascontiguousarray(
            np.asarray(inputs["L0_ffn_out_w"]).T.reshape(8, 128, 1024)).astype(BT),
        "l0f_outb": np.asarray(inputs["L0_ffn_out_b"]).reshape(1, 1024).astype(BT),
        "l0f_nwb": np.ascontiguousarray(
            np.broadcast_to(np.asarray(inputs["L0_ffn_norm_w"]), (128, 1024)),
            dtype=np.float32),
        "l0f_fdt": np.ascontiguousarray(
            np.asarray(inputs["L0_ffn_ff_down_w"]).T.reshape(8, 128, 256)).astype(BT),
        "l0f_fdb": np.ascontiguousarray(
            np.asarray(inputs["L0_ffn_ff_down_b"]).reshape(2, 128).T, dtype=np.float32),
        "l0f_fut": np.ascontiguousarray(
            np.asarray(inputs["L0_ffn_ff_up_w"]).T.reshape(2, 128, 1024)).astype(BT),
        "l0f_fub": np.asarray(inputs["L0_ffn_ff_up_b"]).reshape(1, 1024).astype(BT),
        "onesb": np.ones((1, 128), dtype=BT),
        "identb": np.eye(128, dtype=BT),
    }
    return out




TT0 = 256
ATW = 512


def emit_l0attn(nc, p, C):
    d_wq = nc.dram_tensor("l0a_wq", [32, 8, 128, 128], BF16, kind="ExternalInput").ap()
    d_wk = nc.dram_tensor("l0a_wk", [32, 8, 128, 128], BF16, kind="ExternalInput").ap()
    d_wv = nc.dram_tensor("l0a_wv", [32, 8, 128, 128], BF16, kind="ExternalInput").ap()
    d_wo = nc.dram_tensor("l0a_wo", [8, 128, 64, 128], BF16, kind="ExternalInput").ap()
    d_bo = nc.dram_tensor("l0a_bo", [128, 8], F32, kind="ExternalInput").ap()
    d_em = nc.dram_tensor("l0a_emat2", [128, 128], BF16, kind="ExternalInput").ap()

    X, IDB, ybf, stat = C["X"], C["IDB"], C["ybf"], C["stat"]
    epsb, pb, pbt, banks = C["epsb"], C["pb"], C["pbt"], C["banks"]
    abf, af32 = C["abf"], C["af32"]
    YTt = abf.take(DC * TT0)
    SQK = abf.take(64 * 128)
    VSL = abf.take(65 * 128)
    ATE = abf.take(256 * 64)
    WB = [abf.take(4 * 8 * 128), abf.take(4 * 8 * 128)]
    WOB = [abf.take(32 * 128), abf.take(32 * 128)]
    KSUM = af32.take(TT0)
    KSC = abf.take(TT0)
    QSC = abf.take(TT0)
    ATB = [abf.take(256), abf.take(256)]
    BOC = af32.take(8)
    EM2 = abf.take(128)
    rq = af32.take(16)

    p.dma("sync", EM2[:], d_em)
    p.dma("sync", BOC[:], d_bo)
    consts_tok = p.tok("sync")
    x_toks = C.get("x_toks")

    def load_w(dw, grp, buf):
        src = dw[grp * 4:(grp + 1) * 4].transpose([2, 0, 1, 3])
        p.dma("gpsimd", WB[buf][:].rearrange("p (a s f) -> p a s f", a=4, s=8), src)
        return p.tok("gpsimd")

    def load_wo(cc, jh, buf):
        p.dma("sync", WOB[buf][:].rearrange("p (j f) -> p j f", j=32),
              d_wo[cc][:, jh * 32:(jh + 1) * 32, :])
        return p.tok("sync")

    p.wait("vector", consts_tok)
    p.wait("tensor", consts_tok)
    p.wait("scalar", consts_tok)
    p.emit("vector", lambda E, o=VSL[:, 64 * 128:65 * 128]: E.memset(o, 1.0))
    p.vdrain()
    bufs = C["bufs"]

    ksc_war = [None]
    qsc_war = [None]
    wb_toks = [None, None]
    wb_pe_done = [None, None]
    wob_toks = [None, None]
    wob_pe_done = [None, None]
    gi = 0
    slab_pe_done = None
    ate_dve = None
    ksum_bank = 4

    for tile in range(T // TT0):
        r0 = tile * TT0
        chunks = [r0 // 128, r0 // 128 + 1]
        ln_done = emit_ln(p, X, YTt, bufs, x_toks, eps=1e-5, chunks=chunks, yt_T=TT0)
        p.wait("tensor", ln_done[-1])

        def qkv_path(dw, kind):
            nonlocal gi, slab_pe_done, ate_dve
            for grp in range(8):
                buf = gi % 2
                p.wait("gpsimd", wb_pe_done[buf])
                wb_toks[buf] = load_w(dw, grp, buf)
                p.wait("tensor", wb_toks[buf])
                for lc in range(4):
                    fc = grp * 4 + lc
                    bi = (fc % 2) + (4 if kind == 'k' and False else 0)
                    bi = fc % 2 if kind != 'k' else 2 + fc % 2
                    bank = banks.acquire(bi)
                    tmm = None
                    for dc in range(DC):
                        lhsT = WB[buf][:, (lc * 8 + dc) * 128:(lc * 8 + dc) * 128 + 128]
                        rhs = YTt[:, dc * TT0:(dc + 1) * TT0]
                        tmm = p.emit("tensor", lambda E, o=bank[:, 0:TT0], l=lhsT, r=rhs,
                                     st=(dc == 0), sp=(dc == DC - 1):
                                     E.matmul(o, l, r, start=st, stop=sp), inc=(dc == DC - 1))
                    if kind == 'k':
                        p.wait("scalar", tmm)
                        p.wait("scalar", ksc_war[0])
                        tev = p.emit("scalar", lambda E, o=KSC[:], i=bank[:, 0:TT0]:
                                     E.activation(o, i, AF.Relu), inc=True)
                        banks.release(bi, tev)
                        p.wait("tensor", tev)
                        kb = pb[ksum_bank]
                        p.emit("tensor", lambda E, o=kb[:, 0:TT0], l=EM2[:], r=KSC[:],
                               st=(fc == 0), sp=(fc == 31):
                               E.matmul(o, l, r, start=st, stop=sp), inc=True)
                        ksc_war[0] = p.tok("tensor")
                    elif kind == 'q':
                        p.wait("scalar", tmm)
                        p.wait("scalar", qsc_war[0])
                        tev = p.emit("scalar", lambda E, o=QSC[:], i=bank[:, 0:TT0]:
                                     E.activation(o, i, AF.Relu, scale=0.125), inc=True)
                        banks.release(bi, tev)
                        p.wait("vector", tev)
                        p.wait("vector", slab_pe_done)
                        for hh in range(2):
                            hd = 2 * fc + hh
                            for th in range(2):
                                tq = p.emit("vector", lambda E,
                                            o=SQK[th * 64:th * 64 + 64, hd * 128:(hd + 1) * 128],
                                            a=QSC[hh * 64:hh * 64 + 64, th * 128:(th + 1) * 128],
                                            b=KSUM[hh * 64:hh * 64 + 64, th * 128:(th + 1) * 128]:
                                            E.tensor_tensor(o, a, b, ALU.mult),
                                            inc=(hh == 1 and th == 1))
                        qsc_war[0] = p.tok("vector")
                    else:
                        p.wait("vector", tmm)
                        p.wait("vector", slab_pe_done)
                        for hh in range(2):
                            hd = 2 * fc + hh
                            for th in range(2):
                                tv = p.emit("vector", lambda E,
                                            o=VSL[th * 64:th * 64 + 64, hd * 128:(hd + 1) * 128],
                                            i=bank[hh * 64:hh * 64 + 64, th * 128:(th + 1) * 128]:
                                            E.tensor_copy(o, i), inc=(hh == 1 and th == 1))
                        banks.release(bi, p.tok("vector"))
                wb_pe_done[buf] = p.tok("tensor")
                gi += 1

        bank4 = banks.acquire(ksum_bank)
        qkv_path(d_wk, 'k')
        tks = p.tok("tensor")
        p.wait("vector", tks)
        tkev = p.emit("vector", lambda E, o=KSUM[:], i=pb[ksum_bank][:, 0:TT0]:
                      E.tensor_copy(o, i), inc=True)
        banks.release(ksum_bank, tkev)
        p.vdrain()
        qkv_path(d_wq, 'q')
        qkv_path(d_wv, 'v')
        slab_dve = p.tok("vector")
        p.vdrain()

        p.wait("tensor", slab_dve)
        sq_v = [SQK[h * 64:h * 64 + 64, :].rearrange("p (i t) -> p t i", t=128) for h in (0, 1)]
        vs_v = [VSL[h * 64:h * 64 + 64, :].rearrange("p (j t) -> p t j", t=128) for h in (0, 1)]
        npair = 128
        nbk = (npair + 6) // 7
        for bk in range(nbk):
            bi = bk % 4
            bank = banks.acquire(bi)
            pairs = range(bk * 7, min((bk + 1) * 7, npair))
            tmm = None
            for s, g in enumerate(pairs):
                for h in (0, 1):
                    tmm = p.emit("tensor", lambda E, o=bank[h * 64:h * 64 + 64, s * 65:s * 65 + 65],
                                 l=sq_v[h][:, g, :], r=vs_v[h][:, g, :],
                                 tp=(h * 64, h * 64):
                                 E.matmul(o, l, r, start=True, stop=True, tile_position=tp),
                                 inc=(h == 1 and s == len(list(pairs)) - 1))
            ns = len(list(pairs))
            p.wait("vector", tmm)
            qn_view = bank[:, 0:ns * 65].rearrange("p (s c) -> p s c", c=65)[:, :, 64]
            p.emit("vector", lambda E, o=rq[:, 0:ns], i=qn_view:
                   E.tensor_scalar(o, i, 1e-6, None, ALU.add))
            p.vdrain()
            p.emit("vector", lambda E, o=rq[:, 8:8 + ns], i=rq[:, 0:ns]: E.reciprocal(o, i))
            p.vdrain()
            tev = None
            for s, g in enumerate(pairs):
                dst = ATE[:, (tile % 2) * 8192 + g * 64: (tile % 2) * 8192 + g * 64 + 64]
                tev = p.emit("vector", lambda E, o=dst, i=bank[:, s * 65:s * 65 + 64],
                             r=rq[:, 8 + s:9 + s]:
                             E.tensor_scalar(o, i, r, None, ALU.mult),
                             inc=(s == len(list(pairs)) - 1))
            banks.release(bi, tev)
        slab_pe_done = p.tok("tensor")
        ate_dve = p.tok("vector")

        if tile % 2 == 1:
            p.wait("tensor", ate_dve)
            rr0 = (tile - 1) * TT0
            wi = 0
            for h in (0, 1):
                for cc in range(DC):
                    bi = 4 + h
                    bank = banks.acquire(bi)
                    tmm = None
                    for jh in (0, 1):
                        wbuf = wi % 2
                        p.wait("sync", wob_pe_done[wbuf])
                        wob_toks[wbuf] = load_wo(cc, jh, wbuf)
                        p.wait("tensor", wob_toks[wbuf])
                        for jj in range(32):
                            j = jh * 32 + jj
                            lhsT = WOB[wbuf][h * 64:h * 64 + 64, jj * 128:(jj + 1) * 128]
                            rhs = ATE[h * 64:h * 64 + 64, :].rearrange(
                                "p (g j) -> p j g", j=64)[:, j, :]
                            tmm = p.emit("tensor", lambda E, o=bank[:, 0:256], l=lhsT, r=rhs,
                                         st=(j == 0), sp=(j == 63):
                                         E.matmul(o, l, r, start=st, stop=sp), inc=(jj == 31))
                        wob_pe_done[wbuf] = p.tok("tensor")
                        wi += 1
                    p.wait("scalar", tmm)
                    tab = p.emit("scalar", lambda E, o=ATB[h][:], i=bank[:, 0:256],
                                 b=BOC[:, cc:cc + 1]:
                                 E.activation(o, i, AF.Identity, bias=b), inc=True)
                    banks.release(bi, tab)
                    p.wait("tensor", tab)
                    for gb in range(2):
                        gchunk = (rr0 + h * 128 + gb * 256) // 128
                        slot = gb
                        p.wait("tensor", bufs["_ln_evac"][slot])
                        tpe = p.emit("tensor", lambda E, o=pbt[slot][:, 0:128],
                                     i=ATB[h][:, gb * 128:(gb + 1) * 128], idn=IDB[:]:
                                     E.transpose(o, i, idn), inc=True)
                        p.wait("vector", tpe)
                        bufs["_ln_evac"][slot] = p.emit(
                            "vector", lambda E, o=X[:, gchunk * D + cc * 128: gchunk * D + cc * 128 + 128],
                            i=pbt[slot][:, 0:128]:
                            E.tensor_tensor(o, o, i, ALU.add), inc=True)



def prep_l0attn(inputs):
    wq = np.asarray(inputs["L0_attn_Wq"])
    wk = np.asarray(inputs["L0_attn_Wk"])
    wv = np.asarray(inputs["L0_attn_Wv"])
    wo = np.asarray(inputs["L0_attn_Wo"])
    bo = np.asarray(inputs["L0_attn_bo"])

    def chunk_qkv(w):
        return np.ascontiguousarray(
            w.reshape(8, 128, 32, 128).transpose(2, 0, 1, 3)).astype(BT)

    wop = wo.reshape(64, 64, 1024)
    wop = np.concatenate([wop, wop], axis=0)
    wopc = np.ascontiguousarray(
        wop.reshape(128, 64, 8, 128).transpose(2, 0, 1, 3)).astype(BT)
    em2 = np.fromfunction(lambda p_, m: (p_ % 64 == m % 64), (128, 128)).astype(BT)
    return {
        "l0a_wq": chunk_qkv(wq),
        "l0a_wk": chunk_qkv(wk),
        "l0a_wv": chunk_qkv(wv),
        "l0a_wo": wopc,
        "l0a_bo": np.ascontiguousarray(bo.reshape(8, 128).T, dtype=np.float32),
        "l0a_emat2": em2,
        "identb": np.eye(128, dtype=BT),
    }




NHP_L1A = 8
NHH_L1A = 2
STG_L1A = 10


def emit_l1attn(nc, p, C):
    d_wqkv = nc.dram_tensor("l1a_wqkv", [8, 3, 8, 128, 128], BF16, kind="ExternalInput").ap()
    d_wpj = nc.dram_tensor("l1a_wpj", [8, 128, 1024], BF16, kind="ExternalInput").ap()
    d_i7 = nc.dram_tensor("l1a_i7", [64, 3 * 64], F32, kind="ExternalInput").ap()

    X, IDB, IDF, ybf, stat = C["X"], C["IDB"], C["IDF"], C["ybf"], C["stat"]
    epsb, pb, pbt, banks = C["epsb"], C["pb"], C["pbt"], C["banks"]
    abf, af32 = C["abf"], C["af32"]
    YT = abf.take(DC * T)
    SVT = abf.take(DC * T)
    QKV = abf.take(3 * T)
    WQB = [abf.take(24 * 128)]
    WPJ = abf.take(DC * 1024)
    S3s = af32.take(T)[0:64, :]
    E3 = abf.take(T)[0:64, :]
    E3T = abf.take(16 * 64)
    VTM = abf.take(16 * 64)
    KN1T = abf.take(T)[0:64, :]
    WTs = abf.take(T)[0:64, :]
    KN1C = abf.take(64)
    SM = af32.take(3 * 64)
    QLT2 = abf.take(64)
    KLT2 = abf.take(64)
    _sm = af32.take(1056)
    K2 = _sm[0:64, 0:64]
    K2T = _sm[0:64, 64:128]
    Vcur = _sm[0:64, 128:192]
    VcurT = _sm[0:64, 192:256]
    KVs = _sm[0:64, 256:320]
    KVT = _sm[0:64, 320:384]
    T1 = _sm[0:64, 384:448]
    T2 = _sm[0:64, 448:512]
    T3 = _sm[0:64, 512:576]
    Vn = _sm[0:64, 576:640]
    I7 = _sm[0:64, 640:832]
    ONE64 = _sm[0:64, 832:833]
    ONER = _sm[0:1, 833:897]
    sden = _sm[0:64, 897:901]
    srow = _sm[0:1, 901:981]
    sm2 = _sm[:, 981:989]
    VnT = _sm[0:64, 989:1053]
    _zb = abf.take(128)
    Zb = _zb[0:64, 0:64]
    Bn = _zb[0:64, 64:128]

    p.dma("sync", I7[:], d_i7)
    p.dma("sync", WPJ[:].rearrange("p (s f) -> p s f", s=8), d_wpj.transpose([1, 0, 2]))
    consts_tok = p.tok("sync")
    x_toks = C.get("x_toks")

    def load_wq(hp):
        src = d_wqkv[hp].transpose([2, 0, 1, 3])
        p.dma("gpsimd", WQB[0][:].rearrange("p (a s f) -> p a s f", a=3, s=8), src)
        return p.tok("gpsimd")

    wq_tok = load_wq(0)

    p.wait("vector", consts_tok)
    p.wait("tensor", consts_tok)
    p.wait("scalar", consts_tok)
    p.emit("vector", lambda E, o=ONE64[:]: E.memset(o, 1.0))
    p.emit("vector", lambda E, o=ONER[:]: E.memset(o, 1.0))
    p.vdrain()
    bufs = C["bufs"]

    ln_done = emit_ln(p, X, YT, bufs, x_toks, eps=1e-5)
    p.wait("tensor", ln_done[-1])

    def mm64(bi, lhsT, rhs, n, extra=None, fp=False):
        bank = banks.acquire(bi)
        tmm = p.emit("tensor", lambda E, o=bank[0:lhsT.shape[1], 0:n], l=lhsT, r=rhs:
                     E.matmul(o, l, r, start=True, stop=True), inc=True)
        return bank, tmm

    qkv_dve = None
    for hp in range(NHP_L1A):
        buf = 0
        p.wait("tensor", wq_tok)
        for slot in range(3):
            for ts in range(4):
                bi = (slot * 4 + ts) % 4
                bank = banks.acquire(bi)
                tmm = None
                for dc in range(DC):
                    lhsT = WQB[buf][:, (slot * 8 + dc) * 128:(slot * 8 + dc) * 128 + 128]
                    rhs = YT[:, dc * T + ts * 512: dc * T + (ts + 1) * 512]
                    tmm = p.emit("tensor", lambda E, o=bank[:], l=lhsT, r=rhs,
                                 st=(dc == 0), sp=(dc == DC - 1):
                                 E.matmul(o, l, r, start=st, stop=sp), inc=(dc == DC - 1))
                dst = QKV[:, slot * T + ts * 512: slot * T + (ts + 1) * 512]
                if slot == 0:
                    p.wait("scalar", tmm)
                    p.wait("scalar", qkv_dve)
                    tev = p.emit("scalar", lambda E, o=dst, i=bank[:]:
                                 E.activation(o, i, AF.Identity, scale=0.125), inc=True)
                    eng = "scalar"
                else:
                    p.wait("vector", tmm)
                    p.wait("vector", qkv_dve)
                    tev = p.emit("vector", lambda E, o=dst, i=bank[:]:
                                 E.tensor_copy(o, i), inc=True)
                    eng = "vector"
                banks.release(bi, tev)
        qkv_done_v = p.tok("vector")
        qkv_done_s = p.tok("scalar")
        if hp < NHP_L1A - 1:
            p.wait("gpsimd", p.tok("tensor"))
            wq_tok = load_wq(hp + 1)

        for hh in range(NHH_L1A):
            h = 2 * hp + hh
            Qc = QKV[hh * 64:hh * 64 + 64, 0:T]
            Kc = QKV[hh * 64:hh * 64 + 64, T:2 * T]
            Vc = QKV[hh * 64:hh * 64 + 64, 2 * T:3 * T]
            QLT = QLT2[hh * 64:hh * 64 + 64, :]
            KLT = KLT2[hh * 64:hh * 64 + 64, :]
            p.wait("vector", qkv_done_v)
            p.wait("vector", qkv_done_s)
            p.wait("vector", p.tok("scalar"))
            p.emit("vector", lambda E, o=SM[0:64, 0:64], i=Qc.rearrange("p (m s) -> p m s", s=32):
                   E.tensor_reduce(o, i, AX.X, ALU.add))
            p.emit("vector", lambda E, o=SM[0:64, 64:128], i=Kc.rearrange("p (m s) -> p m s", s=32):
                   E.tensor_reduce(o, i, AX.X, ALU.add))
            p.vdrain()
            p.emit("vector", lambda E, o=QLT, i=SM[0:64, 0:64]:
                   E.tensor_scalar(o, i, 1.0 / 32, None, ALU.mult))
            tql = p.emit("vector", lambda E, o=KLT, i=SM[0:64, 64:128]:
                         E.tensor_scalar(o, i, 1.0 / 32, None, ALU.mult), inc=True)
            p.vdrain()
            if STG_L1A < 2:
                continue
            p.wait("tensor", tql)
            bank, tmm = mm64(4, QLT, KLT, 64)
            p.wait("vector", tmm)
            tng = p.emit("vector", lambda E, o=sden[:, 0:1], i=bank[0:64, 0:64]:
                         E.reduce_max(o, i, AX.X, negate=True), inc=True)
            p.vdrain()
            p.wait("scalar", tng)
            tex = p.emit("scalar", lambda E, o=K2[:], i=bank[0:64, 0:64], b=sden[:, 0:1],
                         a=sden[:, 1:2]:
                         E.activation(o, i, AF.Exp, bias=b, accum_out=a), inc=True)
            p.wait("vector", tex)
            banks.release(4, tex)
            p.emit("vector", lambda E, o=sden[:, 2:3], i=sden[:, 1:2]: E.reciprocal(o, i))
            p.vdrain()
            tk2 = p.emit("vector", lambda E, o=K2[:], i=K2[:], r=sden[:, 2:3]:
                         E.tensor_scalar(o, i, r, None, ALU.mult), inc=True)
            p.vdrain()
            p.wait("tensor", tk2)
            bank5 = banks.acquire(5)
            ttr = p.emit("tensor", lambda E, o=bank5[0:64, 0:64], i=K2[:], idn=IDF[0:64, 0:64]:
                         E.transpose(o, i, idn), inc=True)
            p.wait("vector", ttr)
            tk2t = p.emit("vector", lambda E, o=K2T[:], i=bank5[0:64, 0:64]:
                          E.tensor_copy(o, i), inc=True)
            banks.release(5, tk2t)
            if STG_L1A < 3:
                continue
            p.wait("tensor", tk2t)
            bank, tmm = mm64(4, K2[:], ONE64[:], 1)
            p.wait("vector", tmm)
            tcs = p.emit("vector", lambda E, o=sden[:, 3:4], i=bank[0:64, 0:1]:
                         E.tensor_copy(o, i), inc=True)
            banks.release(4, tcs)
            p.wait("tensor", tcs)
            bank5 = banks.acquire(5)
            ttr = p.emit("tensor", lambda E, o=bank5[0:1, 0:64], i=sden[:, 3:4],
                         idn=IDF[0:64, 0:64]: E.transpose(o, i, idn), inc=True)
            p.wait("vector", ttr)
            trow = p.emit("vector", lambda E, o=srow[:, 0:64], i=bank5[0:1, 0:64]:
                          E.tensor_copy(o, i), inc=True)
            banks.release(5, trow)
            p.vdrain()
            p.emit("vector", lambda E, o=srow[:, 64:65], i=srow[:, 0:64]:
                   E.reduce_max(o, i, AX.X))
            p.vdrain()
            trcp = p.emit("vector", lambda E, o=srow[:, 65:66], i=srow[:, 64:65]:
                          E.reciprocal(o, i), inc=True)
            p.wait("tensor", trcp)
            bank, tmm = mm64(4, ONER[:], srow[:, 65:66], 1)
            p.wait("vector", tmm)
            tden = p.emit("vector", lambda E, o=sden[:, 2:3], i=bank[0:64, 0:1]:
                          E.tensor_copy(o, i), inc=True)
            banks.release(4, tden)
            p.vdrain()
            p.emit("vector", lambda E, o=Vcur[:], i=K2T[:], r=sden[:, 2:3]:
                   E.tensor_scalar(o, i, r, None, ALU.mult))
            tv0 = p.emit("vector", lambda E, o=VcurT[:], i=K2[:], r=sden[:, 2:3]:
                         E.tensor_scalar(o, i, r, None, ALU.mult), inc=True)
            p.vdrain()
            if STG_L1A < 4:
                continue
            tprev = tv0
            for it in range(6):
                p.wait("tensor", tprev)
                bank, tmm = mm64(4, K2T[:], Vcur[:], 64)
                p.wait("vector", tmm)
                tkv = p.emit("vector", lambda E, o=KVs[:], i=bank[0:64, 0:64]:
                             E.tensor_copy(o, i), inc=True)
                banks.release(4, tkv)
                p.vdrain()
                t1t = p.emit("vector", lambda E, o=T1[:], i=KVs[:], s7=I7[:, 0:64]:
                             E.scalar_tensor_tensor(o, i, -1.0, s7, ALU.mult, ALU.add),
                             inc=True)
                p.wait("tensor", t1t)
                bank5 = banks.acquire(5)
                ttr = p.emit("tensor", lambda E, o=bank5[0:64, 0:64], i=KVs[:],
                             idn=IDF[0:64, 0:64]: E.transpose(o, i, idn), inc=True)
                p.wait("vector", ttr)
                tkvt = p.emit("vector", lambda E, o=KVT[:], i=bank5[0:64, 0:64]:
                              E.tensor_copy(o, i), inc=True)
                banks.release(5, tkvt)
                p.wait("tensor", tkvt)
                bank, tmm = mm64(4, KVT[:], T1[:], 64)
                p.wait("vector", tmm)
                tm2 = p.emit("vector", lambda E, o=T2[:], i=bank[0:64, 0:64],
                             s15=I7[:, 64:128]:
                             E.scalar_tensor_tensor(o, i, -1.0, s15, ALU.mult, ALU.add),
                             inc=True)
                banks.release(4, tm2)
                p.vdrain()
                p.wait("tensor", tm2)
                bank, tmm = mm64(4, KVT[:], T2[:], 64)
                p.wait("vector", tmm)
                tm3 = p.emit("vector", lambda E, o=T3[:], i=bank[0:64, 0:64],
                             s13=I7[:, 128:192]:
                             E.scalar_tensor_tensor(o, i, -1.0, s13, ALU.mult, ALU.add),
                             inc=True)
                banks.release(4, tm3)
                p.vdrain()
                p.wait("tensor", tm3)
                bank, tmm = mm64(4, VcurT[:], T3[:], 64)
                bank5, tmm2 = mm64(5, T3[:], VcurT[:], 64)
                p.wait("vector", tmm)
                p.wait("vector", tmm2)
                p.emit("vector", lambda E, o=Vn[:], i=bank[0:64, 0:64]:
                       E.tensor_scalar(o, i, 0.25, None, ALU.mult))
                tvn = p.emit("vector", lambda E, o=VnT[:], i=bank5[0:64, 0:64]:
                             E.tensor_scalar(o, i, 0.25, None, ALU.mult), inc=True)
                banks.release(4, tvn)
                banks.release(5, tvn)
                p.vdrain()
                p.emit("vector", lambda E, o=Vcur[:], i=Vn[:]: E.tensor_copy(o, i))
                tprev = p.emit("vector", lambda E, o=VcurT[:], i=VnT[:]:
                               E.tensor_copy(o, i), inc=True)
                p.vdrain()
            tzb = p.emit("vector", lambda E, o=Zb[:], i=Vcur[:]: E.tensor_copy(o, i),
                         inc=True)
            p.vdrain()

            if STG_L1A < 5:
                continue
            tex = None
            for tc in range(NCH):
                bi = tc % 2
                bank = banks.acquire(bi)
                lhsT = Qc[:, tc * 128:(tc + 1) * 128]
                tmm = p.emit("tensor", lambda E, o=bank[:, 0:64], l=lhsT, r=KLT:
                             E.matmul(o, l, r, start=True, stop=True), inc=True)
                p.wait("vector", tmm)
                p.wait("vector", tex)
                tng = p.emit("vector", lambda E, o=sm2[:, 0:1], i=bank[:, 0:64]:
                             E.reduce_max(o, i, AX.X, negate=True), inc=True)
                p.vdrain()
                p.wait("scalar", tng)
                tex = p.emit("scalar", lambda E, o=KN1C[:], i=bank[:, 0:64],
                             b=sm2[:, 0:1], a=sm2[:, 1:2]:
                             E.activation(o, i, AF.Exp, bias=b, accum_out=a), inc=True)
                banks.release(bi, tex)
                p.wait("vector", tex)
                p.emit("vector", lambda E, o=sm2[:, 2:3], i=sm2[:, 1:2]: E.reciprocal(o, i))
                p.vdrain()
                tkn = p.emit("vector", lambda E, o=KN1C[:], i=KN1C[:], r=sm2[:, 2:3]:
                             E.tensor_scalar(o, i, r, None, ALU.mult), inc=True)
                p.wait("tensor", tkn)
                slot = tc % 2
                p.wait("tensor", bufs["_ln_evac"][slot])
                tpe = p.emit("tensor", lambda E, o=pbt[slot][0:64, 0:128], i=KN1C[:],
                             idn=IDB[:]: E.transpose(o, i, idn), inc=True)
                p.wait("vector", tpe)
                bufs["_ln_evac"][slot] = p.emit(
                    "vector", lambda E, o=KN1T[:, tc * 128:(tc + 1) * 128],
                    i=pbt[slot][0:64, 0:128]: E.tensor_copy(o, i), inc=True)
            kn1_done = p.tok("vector")

            if STG_L1A < 6:
                continue
            for ts in range(4):
                bi = 2 + ts % 2
                bank = banks.acquire(bi)
                tmm = p.emit("tensor", lambda E, o=bank[0:64, 0:512], l=QLT,
                             r=Kc[:, ts * 512:(ts + 1) * 512]:
                             E.matmul(o, l, r, start=True, stop=True), inc=True)
                p.wait("vector", tmm)
                tcp = p.emit("vector", lambda E, o=S3s[:, ts * 512:(ts + 1) * 512],
                             i=bank[0:64, 0:512]: E.tensor_copy(o, i), inc=True)
                banks.release(bi, tcp)
            p.vdrain()
            p.wait("vector", p.tok("scalar"))
            tng = p.emit("vector", lambda E, o=sden[:, 0:1], i=S3s[:]:
                         E.reduce_max(o, i, AX.X, negate=True), inc=True)
            p.vdrain()
            p.wait("scalar", tng)
            tex = p.emit("scalar", lambda E, o=E3[:], i=S3s[:], b=sden[:, 0:1],
                         a=sden[:, 1:2]: E.activation(o, i, AF.Exp, bias=b,
                                                      accum_out=a), inc=True)
            p.wait("vector", tex)
            p.emit("vector", lambda E, o=sden[:, 2:3], i=sden[:, 1:2]: E.reciprocal(o, i))
            p.vdrain()
            if STG_L1A < 7:
                continue
            p.wait("tensor", tex)
            idv = IDB[0:64, 0:64] if hh == 0 else IDB[64:128, 64:128]
            for tc in range(NCH):
                slot = tc % 2
                p.wait("tensor", bufs["_ln_evac"][slot])
                tpe = p.emit("tensor", lambda E, o=pbt[slot][:, 0:64],
                             i=E3[:, tc * 128:(tc + 1) * 128], idn=IDB[0:64, 0:64]:
                             E.transpose(o, i, idn), inc=True)
                p.wait("vector", tpe)
                bufs["_ln_evac"][slot] = p.emit(
                    "vector", lambda E, o=E3T[:, tc * 64:(tc + 1) * 64],
                    i=pbt[slot][:, 0:64]: E.tensor_copy(o, i), inc=True)
                p.wait("tensor", bufs["_ln_evac"][slot])
                tpe = p.emit("tensor", lambda E, o=pbt[slot][:, 0:64],
                             i=Vc[:, tc * 128:(tc + 1) * 128], idn=idv:
                             E.transpose(o, i, idn), inc=True)
                p.wait("vector", tpe)
                bufs["_ln_evac"][slot] = p.emit(
                    "vector", lambda E, o=VTM[:, tc * 64:(tc + 1) * 64],
                    i=pbt[slot][:, 0:64]: E.tensor_copy(o, i), inc=True)
            tev3 = p.tok("vector")

            if STG_L1A < 8:
                continue
            p.wait("tensor", tev3)
            bank = banks.acquire(4)
            tmm = None
            for tc in range(NCH):
                tmm = p.emit("tensor", lambda E, o=bank[0:64, 0:64],
                             l=E3T[:, tc * 64:(tc + 1) * 64],
                             r=VTM[:, tc * 64:(tc + 1) * 64],
                             st=(tc == 0), sp=(tc == NCH - 1):
                             E.matmul(o, l, r, start=st, stop=sp), inc=(tc == NCH - 1))
            p.wait("vector", tmm)
            tb = p.emit("vector", lambda E, o=Bn[:], i=bank[0:64, 0:64], r=sden[:, 2:3]:
                        E.tensor_scalar(o, i, r, None, ALU.mult), inc=True)
            banks.release(4, tb)
            p.vdrain()

            if STG_L1A < 9:
                continue
            p.wait("tensor", tzb)
            p.wait("tensor", kn1_done)
            p.wait("tensor", tb)
            for ts in range(4):
                bi = 2 + ts % 2
                bank = banks.acquire(bi)
                tmm = p.emit("tensor", lambda E, o=bank[0:64, 0:512], l=Zb[:],
                             r=KN1T[:, ts * 512:(ts + 1) * 512]:
                             E.matmul(o, l, r, start=True, stop=True), inc=True)
                p.wait("vector", tmm)
                tcp = p.emit("vector", lambda E, o=WTs[:, ts * 512:(ts + 1) * 512],
                             i=bank[0:64, 0:512]: E.tensor_copy(o, i), inc=True)
                banks.release(bi, tcp)
            wt_done = p.tok("vector")
            if STG_L1A < 10:
                continue
            p.wait("tensor", wt_done)
            for ts in range(4):
                bi = 2 + ts % 2
                bank = banks.acquire(bi)
                tmm = p.emit("tensor", lambda E, o=bank[0:64, 0:512], l=Bn[:],
                             r=WTs[:, ts * 512:(ts + 1) * 512]:
                             E.matmul(o, l, r, start=True, stop=True), inc=True)
                dst = SVT[hh * 64:hh * 64 + 64, hp * T + ts * 512: hp * T + (ts + 1) * 512]
                p.wait("vector", tmm)
                tcp = p.emit("vector", lambda E, o=dst, i=bank[0:64, 0:512]:
                             E.tensor_copy(o, i), inc=True)
                banks.release(bi, tcp)
        qkv_dve = p.tok("vector")

    svt_done = p.tok("vector")
    p.wait("tensor", svt_done)
    for tc in range(NCH):
        for oh in range(2):
            bi = (tc * 2 + oh) % 4
            bank = banks.acquire(bi)
            tmm = None
            for dc in range(DC):
                lhsT = SVT[:, dc * T + tc * 128: dc * T + tc * 128 + 128]
                rhs = WPJ[:, dc * 1024 + oh * 512: dc * 1024 + (oh + 1) * 512]
                tmm = p.emit("tensor", lambda E, o=bank[:], l=lhsT, r=rhs,
                             st=(dc == 0), sp=(dc == DC - 1):
                             E.matmul(o, l, r, start=st, stop=sp), inc=(dc == DC - 1))
            xs = X[:, tc * D + oh * 512: tc * D + (oh + 1) * 512]
            p.wait("vector", tmm)
            tev = p.emit("vector", lambda E, o=xs, i=bank[:]:
                         E.tensor_add(o, o, i), inc=True)
            banks.release(bi, tev)



def prep_l1attn(inputs):
    wqkv = np.asarray(inputs["L1_attn_w_qkv"])
    wpj = np.asarray(inputs["L1_attn_w_proj"])
    wq = wqkv.reshape(8, 128, 3, 8, 128)
    wq = np.ascontiguousarray(wq.transpose(3, 2, 0, 1, 4)).astype(BT)
    i7 = np.concatenate([7 * np.eye(64), 15 * np.eye(64), 13 * np.eye(64)],
                        axis=1).astype(np.float32)
    return {
        "l1a_wqkv": wq,
        "l1a_wpj": np.ascontiguousarray(wpj.reshape(8, 128, 1024)).astype(BT),
        "l1a_i7": i7,
        "identb": np.eye(128, dtype=BT),
        "identf": np.eye(128, dtype=np.float32),
    }



NEXP_L1F = 16
STG_L1F = 3


def emit_l1ffn(nc, p, C):
    d_gate = nc.dram_tensor("l1f_gate", [D, 16], BF16, kind="ExternalInput").ap()
    d_w1 = nc.dram_tensor("l1f_w1", [16, 2, 4, 128, 2048], BF16, kind="ExternalInput").ap()
    d_w2 = nc.dram_tensor("l1f_w2", [16, 2, 8, 128, 1024], BF16, kind="ExternalInput").ap()
    d_b1 = nc.dram_tensor("l1f_b1c", [128, 256], F32, kind="ExternalInput").ap()
    d_b2 = nc.dram_tensor("l1f_b2", [16, 1024], F32, kind="ExternalInput").ap()

    X, IDB, IDF, ybf, stat = C["X"], C["IDB"], C["IDF"], C["ybf"], C["stat"]
    epsb, pb, pbt, banks = C["epsb"], C["pb"], C["pbt"], C["banks"]
    abf, af32 = C["abf"], C["af32"]
    YT = abf.take(DC * T)
    HID = abf.take(16 * 512)
    W1buf = [abf.take(4 * 2048), abf.take(4 * 2048)]
    W2buf = [abf.take(8 * 1024), abf.take(8 * 1024)]
    GATE = abf.take(DC * 16)
    B1C = af32.take(256)
    B2 = af32.take(1024)[0:16, :]
    GT = af32.take(NCH * 16)
    GTT = af32.take(2 * 128)[0:16, :]
    gsc = af32.take(NCH * 16)
    gm = af32.take(16)

    for ap, dr in ((B1C[:], d_b1), (B2[:], d_b2)):
        p.dma("sync", ap, dr)
    for dc in range(DC):
        p.dma("sync", GATE[:, dc * 16:(dc + 1) * 16], d_gate[dc * 128:(dc + 1) * 128, :])
    consts_tok = p.tok("sync")
    x_toks = C.get("x_toks")

    def load_w1(ei):
        for h in range(2):
            src = d_w1[ei, h].transpose([1, 0, 2])
            p.dma("gpsimd", W1buf[h][:].rearrange("p (s f) -> p s f", s=4), src)
        return p.tok("gpsimd")

    def load_w2(ei):
        for h in range(2):
            src = d_w2[ei, h].transpose([1, 0, 2])
            p.dma("gpsimd", W2buf[h][:].rearrange("p (s f) -> p s f", s=8), src)
        return p.tok("gpsimd")

    w1_tok = load_w1(0)
    w2_tok = load_w2(0)

    p.wait("vector", consts_tok)
    p.wait("tensor", consts_tok)
    p.wait("scalar", consts_tok)
    lnbufs = C["bufs"]
    ln_done = emit_ln(p, X, YT, lnbufs, x_toks, eps=1e-5)

    gtt_war = [None, None]
    for c in (range(NCH) if STG_L1F >= 2 else []):
        bi = c % 2
        bank = banks.acquire(bi)
        p.wait("tensor", ln_done[c])
        tpe = None
        for dc in range(DC):
            lhsT = YT[:, dc * T + c * 128: dc * T + c * 128 + 128]
            rhs = GATE[:, dc * 16:(dc + 1) * 16]
            tpe = p.emit("tensor", lambda E, o=bank[:, 0:16], l=lhsT, r=rhs,
                         st=(dc == 0), sp=(dc == DC - 1):
                         E.matmul(o, l, r, start=st, stop=sp),
                         inc=(dc == DC - 1))
        gs = gsc[:, c * 16:(c + 1) * 16]
        g8 = gs[:, 0:8]
        gtc = GT[:, c * 16:(c + 1) * 16]
        p.wait("vector", tpe)
        p.emit("vector", lambda E, o=g8, i=bank[:, 0:16]: E.max(o, i))
        p.vdrain()
        p.emit("vector", lambda E, o=gs[:, 8:9], i=gs[:, 0:1]:
               E.tensor_scalar(o, i, -1.0, None, ALU.mult))
        tmask = p.emit("vector", lambda E, o=gm[:, 0:16], i=bank[:, 0:16], m3=gs[:, 2:3]:
                       E.tensor_scalar(o, i, m3, None, ALU.is_ge), inc=True)
        p.vdrain()
        p.wait("scalar", tmask)
        texp = p.emit("scalar", lambda E, o=gtc, i=bank[:, 0:16], b=gs[:, 8:9]:
                      E.activation(o, i, AF.Exp, bias=b), inc=True)
        banks.release(bi, texp)
        p.wait("vector", texp)
        p.emit("vector", lambda E, o=gtc, i=gtc, m=gm[:, 0:16], s=gs[:, 9:10]:
               E.scalar_tensor_tensor(o, i, 1.0, m, ALU.mult, ALU.mult, accum_out=s))
        p.vdrain()
        p.emit("vector", lambda E, o=gs[:, 10:11], i=gs[:, 9:10]: E.reciprocal(o, i))
        p.vdrain()
        tgt = p.emit("vector", lambda E, o=gtc, i=gtc, r=gs[:, 10:11]:
                     E.tensor_scalar(o, i, r, None, ALU.mult), inc=True)
        if STG_L1F < 3:
            continue
        b2i = 2 + (c % 2)
        bank2 = banks.acquire(b2i)
        gslot = GTT[:, (c % 2) * 128:(c % 2) * 128 + 128]
        p.wait("tensor", tgt)
        ttr = p.emit("tensor", lambda E, o=bank2[0:16, 0:128], i=gtc, idn=IDF[:]:
                     E.transpose(o, i, idn), inc=True)
        p.wait("vector", ttr)
        p.wait("vector", gtt_war[c % 2])
        tcp = p.emit("vector", lambda E, o=gslot,
                     i=bank2[0:16, 0:128]: E.tensor_copy(o, i), inc=True)
        p.wait("tensor", tcp)
        tfree = None
        for oh in range(2):
            p.wait("tensor", tfree)
            tmm = p.emit("tensor", lambda E, o=bank2[:, 0:512], l=gslot,
                         r=B2[:, oh * 512:(oh + 1) * 512]:
                         E.matmul(o, l, r, start=True, stop=True), inc=True)
            p.wait("vector", tmm)
            tfree = p.emit("vector", lambda E, o=X[:, c * D + oh * 512: c * D + (oh + 1) * 512],
                           a=bank2[:, 0:512]: E.tensor_add(o, o, a), inc=True)
        banks.release(b2i, tfree)
        gtt_war[c % 2] = p.tok("tensor")
    p.vdrain()
    init_done_dve = p.tok("vector")

    hid_free_pe = None
    evac_done = {}
    for ei in range(NEXP_L1F):
        p.wait("tensor", w1_tok)
        mm2_last_pe = None
        for th in range(4):
            g = 0
            hid_evac_act = None
            hid_evac_dve = None
            for hc in range(16):
                bi = g % 3
                bank = banks.acquire(bi)
                tmm = None
                for dc in range(DC):
                    lhsT = W1buf[dc // 4][:, (dc % 4) * 2048 + hc * 128: (dc % 4) * 2048 + hc * 128 + 128]
                    rhs = YT[:, dc * T + th * 512: dc * T + (th + 1) * 512]
                    tmm = p.emit("tensor", lambda E, o=bank[:], l=lhsT, r=rhs,
                                 st=(dc == 0), sp=(dc == DC - 1):
                                 E.matmul(o, l, r, start=st, stop=sp),
                                 inc=(dc == DC - 1))
                dst = HID[:, hc * 512:(hc + 1) * 512]
                bcol = B1C[:, ei * 16 + hc: ei * 16 + hc + 1]
                eng = "scalar" if bi < 2 else "vector"
                p.wait(eng, tmm)
                p.wait(eng, hid_free_pe)
                if eng == "scalar":
                    tev = p.emit("scalar", lambda E, o=dst, i=bank[:], b=bcol:
                                 E.activation(o, i, AF.Relu, bias=b), inc=True)
                    hid_evac_act = tev
                else:
                    tev = p.emit("vector", lambda E, o=dst, i=bank[:], b=bcol:
                                 E.tensor_scalar(o, i, b, 0.0, ALU.add, ALU.max), inc=True)
                    hid_evac_dve = tev
                banks.release(bi, tev)
                g += 1
            if th == 3 and ei < NEXP_L1F - 1:
                mm1_pe_done = p.tok("tensor")
                p.wait("gpsimd", mm1_pe_done)
                w1_next = load_w1(ei + 1)
            if th == 0:
                p.wait("tensor", w2_tok)
            p.wait("tensor", hid_evac_act)
            p.wait("tensor", hid_evac_dve)
            g = 0
            for tc in range(4):
                gc = th * 4 + tc
                for oh in range(2):
                    bi = 3 + (g % 3)
                    bank = banks.acquire(bi)
                    tmm = None
                    for hc in range(16):
                        lhsT = HID[:, hc * 512 + tc * 128: hc * 512 + tc * 128 + 128]
                        rhs = W2buf[hc // 8][:, (hc % 8) * 1024 + oh * 512: (hc % 8) * 1024 + (oh + 1) * 512]
                        tmm = p.emit("tensor", lambda E, o=bank[:], l=lhsT, r=rhs,
                                     st=(hc == 0), sp=(hc == 15):
                                     E.matmul(o, l, r, start=st, stop=sp),
                                     inc=(hc == 15))
                    Xs = X[:, gc * D + oh * 512: gc * D + (oh + 1) * 512]
                    gcol = GT[:, gc * 16 + ei: gc * 16 + ei + 1]
                    p.wait("vector", tmm)
                    tev = p.emit("vector", lambda E, o=Xs, i=bank[:], s=gcol:
                                 E.scalar_tensor_tensor(o, i, s, o, ALU.mult, ALU.add),
                                 inc=True)
                    banks.release(bi, tev)
                    g += 1
            mm2_last_pe = p.tok("tensor")
            hid_free_pe = mm2_last_pe
        if ei < NEXP_L1F - 1:
            p.wait("gpsimd", mm2_last_pe)
            w2_tok = load_w2(ei + 1)
            w1_tok = w1_next




def prep_l1ffn(inputs):
    w1 = np.asarray(inputs["L1_ffn_W1"])
    w2 = np.asarray(inputs["L1_ffn_W2"])
    b1 = np.asarray(inputs["L1_ffn_b1"])
    b2 = np.asarray(inputs["L1_ffn_b2"])
    gate = np.asarray(inputs["L1_ffn_gate"])
    return {
        "l1f_gate": gate.astype(BT),
        "l1f_w1": np.ascontiguousarray(w1.reshape(16, 2, 4, 128, 2048)).astype(BT),
        "l1f_w2": np.ascontiguousarray(w2.reshape(16, 2, 8, 128, 1024)).astype(BT),
        "l1f_b1c": np.ascontiguousarray(
            b1.reshape(16, 16, 128).transpose(2, 0, 1).reshape(128, 256)).astype(np.float32),
        "l1f_b2": b2.astype(np.float32),
        "identb": np.eye(128, dtype=BT),
        "identf": np.eye(128, dtype=np.float32),
    }


def build_nc(build_fn):
    nc = bass.Bass("TRN2", target_bir_lowering=False)
    with ExitStack() as ctx:
        build_fn(nc, ctx)
    return nc


def run_launch(nc, prep, x_percore, n_cores=8):
    in_maps = []
    for c in range(n_cores):
        m = dict(prep)
        m["x_in"] = np.ascontiguousarray(x_percore[c], dtype=np.float32)
        in_maps.append(m)
    res = run_bass_kernel_spmd(nc, in_maps, core_ids=list(range(n_cores)))
    return np.stack([r["x_out"] for r in res.results])


def _standalone(emit_fn):
    def build(nc, ctx):
        p = Prog(nc, ctx)
        C = make_common(nc, ctx, p)
        d_x = nc.dram_tensor("x_in", [T, D], F32, kind="ExternalInput").ap()
        d_out = nc.dram_tensor("x_out", [T, D], F32, kind="ExternalOutput").ap()
        C["x_toks"] = emit_load_x(p, C["X"], d_x)
        emit_fn(nc, p, C)
        emit_store_x(p, d_out, C["X"], [p.tok("vector")])
        p.finalize()
    return build


build_l0attn = _standalone(emit_l0attn)
build_l0ffn = _standalone(emit_l0ffn)
build_l1attn = _standalone(emit_l1attn)
build_l1ffn = _standalone(emit_l1ffn)


FUSE_N = 4
FUSE_LIST = ["a0", "f0", "a1", "f1"]
PAD_NOPS = 0
FORCE_BARRIER = False


def build_fused(nc, ctx):
    p = Prog(nc, ctx)
    C = make_common(nc, ctx, p)
    d_x = nc.dram_tensor("x_in", [T, D], F32, kind="ExternalInput").ap()
    d_out = nc.dram_tensor("x_out", [T, D], F32, kind="ExternalOutput").ap()
    C["x_toks"] = emit_load_x(p, C["X"], d_x)
    _all = {"a0": emit_l0attn, "f0": emit_l0ffn, "a1": emit_l1attn, "f1": emit_l1ffn}
    fns = [_all[k] for k in FUSE_LIST[:FUSE_N]]
    for i, fn in enumerate(fns):
        C["abf"].reset()
        C["af32"].reset()
        fn(nc, p, C)
        if i < len(fns) - 1 or FORCE_BARRIER:
            emit_barrier(p, C)
            C["x_toks"] = None
    for _ in range(PAD_NOPS):
        p.emit("vector", lambda E: E.nop())
    emit_store_x(p, d_out, C["X"], [p.tok("vector")])
    p.finalize()


def prep_all(inputs):
    prep = prep_common()
    for pf in (prep_l0attn, prep_l0ffn, prep_l1attn, prep_l1ffn):
        prep.update(pf(inputs))
    return prep


_NC_CACHE = {}


def _get_nc(name, build_fn):
    if name not in _NC_CACHE:
        _NC_CACHE[name] = build_nc(build_fn)
    return _NC_CACHE[name]


def kernel(**inputs):
    x = np.ascontiguousarray(np.asarray(inputs["L0_attn_x"], dtype=np.float32))
    nc = _get_nc("fused", build_fused)
    prep = prep_all(inputs)
    return run_launch(nc, prep, x)


# version 05
# speedup vs baseline: 1.0082x; 1.0022x over previous
import numpy as np
import ml_dtypes
from contextlib import ExitStack

import concourse.bass as bass
import concourse.mybir as mybir
from concourse.bass_utils import run_bass_kernel_spmd

F32 = mybir.dt.float32
BF16 = mybir.dt.bfloat16
AF = mybir.ActivationFunctionType
ALU = mybir.AluOpType
AX = mybir.AxisListType
BT = ml_dtypes.bfloat16

T = 2048
D = 1024
NCH = T // 128
DC = D // 128


ENGINES = ("tensor", "vector", "scalar", "sync", "gpsimd")


class Prog:

    def __init__(self, nc, ctx):
        self.nc = nc
        self.ctx = ctx
        self.q = {e: [] for e in ENGINES}
        self.cnt = {}
        self.s = {}
        self.s_small = {}
        for e in ENGINES:
            h = ctx.enter_context(nc.semaphore("s_" + e))
            self.cnt[h] = 0
            self.s[e] = h
        for e in ("sync", "gpsimd"):
            h = ctx.enter_context(nc.semaphore("sm_" + e))
            self.cnt[h] = 0
            self.s_small[e] = h
        self.s_barrier = {}
        for e in ENGINES:
            h = ctx.enter_context(nc.semaphore("sb_" + e))
            self.cnt[h] = 0
            self.s_barrier[e] = h

    def emit(self, eng, fn, inc=False):
        incs = []
        if inc:
            incs = [(self.s[eng], 1)]
            self.cnt[self.s[eng]] += 1
        self.q[eng].append((fn, incs))
        return self.tok(eng)

    def tok(self, eng):
        s = self.s[eng]
        return (s, self.cnt[s])

    def wait(self, eng, tok):
        if tok is None:
            return
        if isinstance(tok, list):
            for t in tok:
                self.wait(eng, t)
            return
        s, v = tok
        if v <= 0:
            return
        self.q[eng].append((lambda E, s=s, v=v: E.wait_ge(s, v), []))

    def vdrain(self):
        self.q["vector"].append((lambda E: E.drain(), []))

    def dma(self, queue, out_ap, in_ap):
        sb = out_ap if str(out_ap.space).lower().find("sb") >= 0 else in_ap
        try:
            parts = sb.shape[0]
            perpart = sb.free_nbytes()
        except Exception:
            parts, perpart = 128, 4096
        small = parts < 128 or perpart < 512
        s = self.s_small[queue] if small else self.s[queue]
        self.cnt[s] += 16
        self.q[queue].append(
            (lambda E, o=out_ap, i=in_ap: E.dma_start(o, i), [(s, 16)]))
        tok = (s, self.cnt[s])
        if small:
            self.wait(queue, tok)
        return tok

    def finalize(self):
        block = self.ctx.enter_context(self.nc.Block())
        for ename in ENGINES:
            items = self.q[ename]

            def body(E, items=items):
                for fn, incs in items:
                    inst = fn(E)
                    for s, k in incs:
                        inst.then_inc(s, k)

            getattr(block, ename)(body)


class Banks:

    def __init__(self, p, pb):
        self.p = p
        self.pb = pb
        self.free_tok = [None] * len(pb)

    def acquire(self, i):
        self.p.wait("tensor", self.free_tok[i])
        return self.pb[i]

    def release(self, i, tok):
        self.free_tok[i] = tok



def emit_load_x(p, X, d_x):
    return [p.dma("sync", X[:, c * D:(c + 1) * D], d_x[c * 128:(c + 1) * 128, :])
            for c in range(NCH)]


def emit_store_x(p, d_out, X, wait_toks):
    for tk in wait_toks:
        p.wait("sync", tk)
    last = None
    for c in range(NCH):
        last = p.dma("sync", d_out[c * 128:(c + 1) * 128, :], X[:, c * D:(c + 1) * D])
    p.wait("sync", last)


def emit_ln(p, X, YT, bufs, x_toks, eps=1e-5, y_tm=None, chunks=None, yt_T=T):
    if chunks is None:
        chunks = list(range(NCH))
    ybf, stat, pbt, identb = bufs["ybf"], bufs["stat"], bufs["pbt"], bufs["identb"]
    evac_tok = bufs.setdefault("_ln_evac", [None, None])
    ybf_war = bufs.setdefault("_ln_ybf_war", [None, None])
    act_tok = bufs.setdefault("_ln_act", [None, None])
    chunk_done = []
    for li, c in enumerate(chunks):
        xc = X[:, c * D:(c + 1) * D]
        sl = stat[:, (li % 2) * 16:(li % 2) * 16 + 16]
        if x_toks is not None:
            p.wait("vector", x_toks[c])
        p.wait("vector", act_tok[li % 2])
        p.emit("vector", lambda E, a=sl[:, 0:6], b=xc[:, 0:512]: E.bn_stats(a, b))
        p.emit("vector", lambda E, a=sl[:, 6:12], b=xc[:, 512:1024]: E.bn_stats(a, b))
        p.vdrain()
        tdve = p.emit("vector", lambda E, a=sl[:, 12:14], b=sl[:, 0:12]:
                      E.bn_aggr(a, b), inc=True)
        p.wait("scalar", tdve)
        p.emit("scalar", lambda E, o=sl[:, 14:15], i=sl[:, 13:14], ee=bufs["epsb"]:
               E.activation(o, i, AF.Ln, bias=ee))
        tact = p.emit("scalar", lambda E, o=sl[:, 15:16], i=sl[:, 14:15]:
                      E.activation(o, i, AF.Exp, scale=-0.5), inc=True)
        act_tok[li % 2] = tact
        yb = ybf[li % 2]
        p.wait("vector", tact)
        p.wait("vector", ybf_war[li % 2])
        p.vdrain()
        if y_tm is not None:
            p.emit("vector", lambda E, o=y_tm[:, li * D:(li + 1) * D], i=xc,
                   m=sl[:, 12:13], r=sl[:, 15:16]:
                   E.tensor_scalar(o, i, m, r, ALU.subtract, ALU.mult))
            p.vdrain()
            tapply = p.emit("vector", lambda E, o=yb[:], i=y_tm[:, li * D:(li + 1) * D]:
                            E.tensor_copy(o, i), inc=True)
        else:
            tapply = p.emit("vector", lambda E, o=yb[:], i=xc,
                            m=sl[:, 12:13], r=sl[:, 15:16]:
                            E.tensor_scalar(o, i, m, r, ALU.subtract, ALU.mult), inc=True)
        p.wait("tensor", tapply)
        emit_transposes(p, bufs, yb, YT, li, yt_T)
        ybf_war[li % 2] = p.tok("tensor")
        chunk_done.append(p.tok("vector"))
    return chunk_done


def emit_transposes(p, bufs, yb, YT, li, yt_T):
    pbt, identb = bufs["pbt"], bufs["identb"]
    evac_tok = bufs.setdefault("_ln_evac", [None, None])
    for dc in range(DC):
        slot = dc % 2
        p.wait("tensor", evac_tok[slot])
        tpe = p.emit("tensor", lambda E, o=pbt[slot][:, 0:128],
                     i=yb[:, dc * 128:dc * 128 + 128], idn=identb[:]:
                     E.transpose(o, i, idn), inc=True)
        p.wait("vector", tpe)
        evac_tok[slot] = p.emit(
            "vector", lambda E, o=YT[:, dc * yt_T + li * 128: dc * yt_T + li * 128 + 128],
            i=pbt[slot][:, 0:128]: E.tensor_copy(o, i), inc=True)


def emit_cast_transpose(p, bufs, src_f32, YT, li, yt_T):
    ybf = bufs["ybf"]
    ybf_war = bufs.setdefault("_ln_ybf_war", [None, None])
    yb = ybf[li % 2]
    p.wait("vector", ybf_war[li % 2])
    p.vdrain()
    tcast = p.emit("vector", lambda E, o=yb[:], i=src_f32: E.tensor_copy(o, i), inc=True)
    p.wait("tensor", tcast)
    emit_transposes(p, bufs, yb, YT, li, yt_T)
    ybf_war[li % 2] = p.tok("tensor")



class Arena:

    def __init__(self, t):
        self.t = t
        self.n = t.shape[1]
        self.off = 0

    def take(self, cols):
        v = self.t[:, self.off:self.off + cols]
        self.off += cols
        assert self.off <= self.n, f"arena overflow {self.off} > {self.n}"
        return v

    def reset(self):
        self.off = 0


ABF_COLS = 58700
AF32_COLS = 5830


def make_common(nc, ctx, p):
    e = ctx.enter_context
    C = {}
    C["X"] = e(nc.sbuf_tensor("X", [128, NCH * D], F32))
    C["IDB"] = e(nc.sbuf_tensor("IDB", [128, 128], BF16))
    C["IDF"] = e(nc.sbuf_tensor("IDF", [128, 128], F32))
    C["ybf"] = [e(nc.sbuf_tensor("ybf0", [128, D], BF16)),
                e(nc.sbuf_tensor("ybf1", [128, D], BF16))]
    C["stat"] = e(nc.sbuf_tensor("stat", [128, 32], F32))
    C["epsb"] = e(nc.sbuf_tensor("epsb", [128, 1], F32))
    C["eps6"] = e(nc.sbuf_tensor("eps6", [128, 1], F32))
    C["abf"] = Arena(e(nc.sbuf_tensor("ABF", [128, ABF_COLS], BF16)))
    C["af32"] = Arena(e(nc.sbuf_tensor("AF32", [128, AF32_COLS], F32)))
    C["pb"] = [e(nc.psum_tensor(f"pb{i}", [128, 512], F32)) for i in range(6)]
    C["pbt"] = [e(nc.psum_tensor("pbt0", [128, 256], BF16)),
                e(nc.psum_tensor("pbt1", [128, 256], BF16))]
    C["banks"] = Banks(p, C["pb"])
    d_idb = nc.dram_tensor("identb", [128, 128], BF16, kind="ExternalInput").ap()
    d_idf = nc.dram_tensor("identf", [128, 128], F32, kind="ExternalInput").ap()
    p.dma("sync", C["IDB"][:], d_idb)
    p.dma("sync", C["IDF"][:], d_idf)
    ct = p.tok("sync")
    for en in ("vector", "tensor", "scalar"):
        p.wait(en, ct)
    p.emit("vector", lambda E, o=C["epsb"][:]: E.memset(o, 1e-5))
    p.emit("vector", lambda E, o=C["eps6"][:]: E.memset(o, 1e-6))
    p.vdrain()
    C["bufs"] = dict(ybf=C["ybf"], stat=C["stat"], pbt=C["pbt"], identb=C["IDB"],
                     epsb=C["epsb"][:])
    return C


BARRIER_WAIT_ON = list(ENGINES)


def emit_barrier(p, C):
    p.vdrain()
    toks = {}
    for en in ENGINES:
        sb = p.s_barrier[en]
        p.cnt[sb] += 1
        if en == "tensor":
            bank0 = C["banks"].acquire(0)
            p.q[en].append((lambda E, o=bank0[0:128, 0:128], l=C["IDB"][:],
                            r=C["IDB"][:]: E.matmul(o, l, r, start=True, stop=True),
                            [(sb, 1)]))
        else:
            p.q[en].append((lambda E: E.nop(nofuse=True), [(sb, 1)]))
        toks[en] = (sb, p.cnt[sb])
    for en in ENGINES:
        for other in ENGINES:
            if other != en and other in BARRIER_WAIT_ON:
                p.wait(en, toks[other])
    bt = p.tok("tensor")
    for i in range(len(C["banks"].free_tok)):
        C["banks"].free_tok[i] = toks["vector"]
    C["bufs"]["_ln_evac"] = [toks["vector"], toks["vector"]]
    C["bufs"]["_ln_ybf_war"] = [toks["tensor"], toks["tensor"]]
    C["bufs"]["_ln_act"] = [toks["scalar"], toks["scalar"]]


def prep_common():
    return {"identb": np.eye(128, dtype=BT), "identf": np.eye(128, dtype=np.float32)}


def mm_acc(p, banks, bi, ncols, parts, out_base=0, npart=128):
    bank = banks.acquire(bi)
    n = len(parts)
    tmm = None
    for i, (lhsT, rhs) in enumerate(parts):
        tmm = p.emit("tensor", lambda E, o=bank[out_base:out_base + npart, 0:ncols],
                     l=lhsT, r=rhs, st=(i == 0), sp=(i == n - 1):
                     E.matmul(o, l, r, start=st, stop=sp), inc=(i == n - 1))
    return bank, tmm


def emit_top_gate(p, gs, gm, gout, bank_ap, nexp, kth, drain=True):
    p.emit("vector", lambda E, o=gs[:, 0:8], i=bank_ap: E.max(o, i))
    p.vdrain()
    p.emit("vector", lambda E, o=gs[:, 8:9], i=gs[:, 0:1]:
           E.tensor_scalar(o, i, -1.0, None, ALU.mult))
    tmask = p.emit("vector", lambda E, o=gm[:, 0:nexp], i=bank_ap, mk=gs[:, kth:kth + 1]:
                   E.tensor_scalar(o, i, mk, None, ALU.is_ge), inc=True)
    p.vdrain()
    p.wait("scalar", tmask)
    texp = p.emit("scalar", lambda E, o=gout, i=bank_ap, b=gs[:, 8:9]:
                  E.activation(o, i, AF.Exp, bias=b), inc=True)
    p.wait("vector", texp)
    p.emit("vector", lambda E, o=gout, i=gout, m=gm[:, 0:nexp], s=gs[:, 9:10]:
           E.scalar_tensor_tensor(o, i, 1.0, m, ALU.mult, ALU.mult, accum_out=s))
    p.vdrain()
    p.emit("vector", lambda E, o=gs[:, 10:11], i=gs[:, 9:10]: E.reciprocal(o, i))
    p.vdrain()
    tgt = p.emit("vector", lambda E, o=gout, i=gout, r=gs[:, 10:11]:
                 E.tensor_scalar(o, i, r, None, ALU.mult), inc=True)
    if drain:
        p.vdrain()
    return tgt, texp



Q = 512
QC = Q // 128
NQ = T // Q


def emit_l0ffn(nc, p, C):
    d_dwk = nc.dram_tensor("l0f_dwk", [128, 24], F32, kind="ExternalInput").ap()
    d_dwb = nc.dram_tensor("l0f_dwb", [128, 8], F32, kind="ExternalInput").ap()
    d_pwt = nc.dram_tensor("l0f_pwt", [8, 128, 1024], BF16, kind="ExternalInput").ap()
    d_pwb = nc.dram_tensor("l0f_pwb", [1, 1024], BF16, kind="ExternalInput").ap()
    d_rwt = nc.dram_tensor("l0f_rwt", [8, 128, 8], BF16, kind="ExternalInput").ap()
    d_rb = nc.dram_tensor("l0f_rb", [1, 8], BF16, kind="ExternalInput").ap()
    d_ewt = nc.dram_tensor("l0f_ewt", [8, 8, 128, 1024], BF16, kind="ExternalInput").ap()
    d_eb = nc.dram_tensor("l0f_eb", [1, 8 * 1024], BF16, kind="ExternalInput").ap()
    d_owt = nc.dram_tensor("l0f_owt", [8, 128, 1024], BF16, kind="ExternalInput").ap()
    d_outb = nc.dram_tensor("l0f_outb", [1, 1024], BF16, kind="ExternalInput").ap()
    d_nwb = nc.dram_tensor("l0f_nwb", [128, 1024], F32, kind="ExternalInput").ap()
    d_fdt = nc.dram_tensor("l0f_fdt", [8, 128, 256], BF16, kind="ExternalInput").ap()
    d_fdb = nc.dram_tensor("l0f_fdb", [128, 2], F32, kind="ExternalInput").ap()
    d_fut = nc.dram_tensor("l0f_fut", [2, 128, 1024], BF16, kind="ExternalInput").ap()
    d_fub = nc.dram_tensor("l0f_fub", [1, 1024], BF16, kind="ExternalInput").ap()
    d_ones = nc.dram_tensor("onesb", [1, 128], BF16, kind="ExternalInput").ap()

    X, IDB, ybf, stat = C["X"], C["IDB"], C["ybf"], C["stat"]
    epsb, eps6, pb, pbt, banks = C["epsb"], C["eps6"], C["pb"], C["pbt"], C["banks"]
    abf, af32 = C["abf"], C["af32"]
    X1 = af32.take(QC * D)
    YTq = abf.take(DC * Q)
    CTq = abf.take(DC * Q)
    XTq = YTq
    CMB = abf.take(QC * D)
    PWT = abf.take(DC * 1024)
    OWT = abf.take(DC * 1024)
    FDT = abf.take(DC * 256)
    FUT = abf.take(2 * 1024)
    EWT = [abf.take(DC * 1024), abf.take(DC * 1024)]
    RWT = abf.take(DC * 8)
    NWB = af32.take(1024)
    DWK = af32.take(24)
    DWB = af32.take(8)
    FDB = af32.take(2)
    ROWS = abf.take(3 * 1024 + 8 + 128)[0:1, :]
    EBR = [abf.take(1024)[0:1, :], abf.take(1024)[0:1, :]]
    RW = af32.take(NCH * 8)
    gsc = af32.take(16)
    gm = af32.take(16)
    tmb = [abf.take(512), abf.take(512)]
    yet = [abf.take(512), abf.take(512)]
    TCIN = abf.take(Q + 2)
    DISC = abf.take(1024)
    tail = abf.take(DC * 2)
    cacc = af32.take(Q)
    tmpf = [cacc]

    R_PWB, R_OUTB, R_FUB, R_RB, R_ONES = 0, 1024, 2048, 3072, 3080
    p.dma("sync", DWK[:], d_dwk)
    p.dma("sync", DWB[:], d_dwb)
    p.dma("sync", FDB[:], d_fdb)
    p.dma("sync", NWB[:], d_nwb)
    p.dma("sync", ROWS[:, R_PWB:R_PWB + 1024], d_pwb)
    p.dma("sync", ROWS[:, R_OUTB:R_OUTB + 1024], d_outb)
    p.dma("sync", ROWS[:, R_FUB:R_FUB + 1024], d_fub)
    p.dma("sync", ROWS[:, R_RB:R_RB + 8], d_rb)
    p.dma("sync", ROWS[:, R_ONES:R_ONES + 128], d_ones)
    for dc in range(DC):
        p.dma("sync", RWT[:, dc * 8:(dc + 1) * 8], d_rwt[dc])
    p.dma("sync", PWT[:].rearrange("p (s f) -> p s f", s=8), d_pwt.transpose([1, 0, 2]))
    p.dma("sync", OWT[:].rearrange("p (s f) -> p s f", s=8), d_owt.transpose([1, 0, 2]))
    p.dma("sync", FDT[:].rearrange("p (s f) -> p s f", s=8), d_fdt.transpose([1, 0, 2]))
    p.dma("sync", FUT[:].rearrange("p (s f) -> p s f", s=2), d_fut.transpose([1, 0, 2]))
    consts_tok = p.tok("sync")
    x_toks = C.get("x_toks")

    def load_ewt(ei, buf):
        p.dma("gpsimd", EWT[buf][:].rearrange("p (s f) -> p s f", s=8),
              d_ewt[ei].transpose([1, 0, 2]))
        p.dma("gpsimd", EBR[buf][:], d_eb[:, ei * 1024:(ei + 1) * 1024])
        return p.tok("gpsimd")

    ONES = ROWS[:, R_ONES:R_ONES + 128]
    ewt_tok = [load_ewt(0, 0), None]

    p.wait("vector", consts_tok)
    p.wait("tensor", consts_tok)
    p.wait("scalar", consts_tok)
    p.emit("vector", lambda E, o=tail[:]: E.memset(o, 0.0))
    p.vdrain()
    bufs = C["bufs"]

    tmb_war = [None, None]
    yet_war = [None, None]
    ewt_pe_done = [None, None]
    for q in range(NQ):
        chunks = list(range(q * QC, (q + 1) * QC))
        ln_done = emit_ln(p, X, YTq, bufs, x_toks, eps=1e-5, y_tm=X1,
                          chunks=chunks, yt_T=Q)
        for dc in range(DC):
            w0 = DWK[:, dc * 3:dc * 3 + 1]
            w1 = DWK[:, dc * 3 + 1:dc * 3 + 2]
            w2 = DWK[:, dc * 3 + 2:dc * 3 + 3]
            bcol = DWB[:, dc:dc + 1]
            ytc = YTq[:, dc * Q:(dc + 1) * Q]
            p.wait("vector", ln_done[-1])
            p.vdrain()
            p.emit("vector", lambda E, o=TCIN[:, 0:2], i=tail[:, dc * 2:dc * 2 + 2]:
                   E.tensor_copy(o, i))
            p.emit("vector", lambda E, o=TCIN[:, 2:Q + 2], i=ytc: E.tensor_copy(o, i))
            p.emit("vector", lambda E, o=tail[:, dc * 2:dc * 2 + 2], i=ytc[:, Q - 2:Q]:
                   E.tensor_copy(o, i))
            p.vdrain()
            p.emit("vector", lambda E, o=cacc[:], i=TCIN[:, 2:Q + 2], a=w2, b=bcol:
                   E.tensor_scalar(o, i, a, b, ALU.mult, ALU.add))
            p.vdrain()
            p.emit("vector", lambda E, o=cacc[:], i=TCIN[:, 1:Q + 1], a=w1, x=cacc[:]:
                   E.scalar_tensor_tensor(o, i, a, x, ALU.mult, ALU.add))
            p.vdrain()
            p.emit("vector", lambda E, o=CTq[:, dc * Q:(dc + 1) * Q], i=TCIN[:, 0:Q],
                   a=w0, x=cacc[:]:
                   E.scalar_tensor_tensor(o, i, a, x, ALU.mult, ALU.add), inc=True)
        p.vdrain()
        conv_done = p.tok("vector")

        p.wait("tensor", conv_done)
        for tc in range(QC):
            gc = q * QC + tc
            for eh in range(2):
                parts = [(CTq[:, dc * Q + tc * 128: dc * Q + tc * 128 + 128],
                          PWT[:, dc * 1024 + eh * 512: dc * 1024 + (eh + 1) * 512])
                         for dc in range(DC)]
                parts.append((ONES, ROWS[:, R_PWB + eh * 512:R_PWB + (eh + 1) * 512]))
                bank, tmm = mm_acc(p, banks, (tc * 2 + eh) % 4, 512, parts)
                tb = tmb[eh]
                p.wait("scalar", tmm)
                p.wait("scalar", tmb_war[eh])
                tev = p.emit("scalar", lambda E, o=tb[:], i=bank[:]:
                             E.activation(o, i, AF.Silu), inc=True)
                banks.release((tc * 2 + eh) % 4, tev)
                x1s = X1[:, tc * D + eh * 512: tc * D + (eh + 1) * 512]
                p.wait("vector", tev)
                tadd = p.emit("vector", lambda E, o=x1s, i=x1s, b=tb[:]:
                              E.tensor_add(o, i, b), inc=True)
                tmb_war[eh] = tadd
            p.vdrain()
            emit_cast_transpose(p, bufs, X1[:, tc * D:(tc + 1) * D], XTq, tc, Q)
            parts = [(XTq[:, dc * Q + tc * 128: dc * Q + tc * 128 + 128],
                      RWT[:, dc * 8:(dc + 1) * 8]) for dc in range(DC)]
            parts.append((ONES, ROWS[:, R_RB:R_RB + 8]))
            xt_tok = p.tok("vector")
            p.wait("tensor", xt_tok)
            bank, tmm = mm_acc(p, banks, 4 + tc % 2, 8, parts)
            p.wait("vector", tmm)
            rwc = RW[:, gc * 8:(gc + 1) * 8]
            tgt, texp = emit_top_gate(p, gsc, gm, rwc, bank[:, 0:8], 8, 1)
            banks.release(4 + tc % 2, texp)

        gb = 0
        for ei in range(8):
            cur = ei % 2
            p.wait("tensor", ewt_tok[cur])
            for tc in range(QC):
                gc = q * QC + tc
                for oh in range(2):
                    parts = [(XTq[:, dc * Q + tc * 128: dc * Q + tc * 128 + 128],
                              EWT[cur][:, dc * 1024 + oh * 512: dc * 1024 + (oh + 1) * 512])
                             for dc in range(DC)]
                    parts.append((ONES, EBR[cur][:, oh * 512:(oh + 1) * 512]))
                    bank, tmm = mm_acc(p, banks, gb % 4, 512, parts)
                    ys = yet[gb % 2]
                    p.wait("scalar", tmm)
                    p.wait("scalar", yet_war[gb % 2])
                    tev = p.emit("scalar", lambda E, o=ys[:], i=bank[:]:
                                 E.activation(o, i, AF.Silu), inc=True)
                    banks.release(gb % 4, tev)
                    dst = CMB[:, tc * D + oh * 512: tc * D + (oh + 1) * 512]
                    rcol = RW[:, gc * 8 + ei: gc * 8 + ei + 1]
                    p.wait("vector", tev)
                    if ei == 0:
                        tacc = p.emit("vector", lambda E, o=dst, i=ys[:], r=rcol:
                                      E.tensor_scalar(o, i, r, None, ALU.mult), inc=True)
                    else:
                        tacc = p.emit("vector", lambda E, o=dst, i=ys[:], r=rcol:
                                      E.scalar_tensor_tensor(o, i, r, o, ALU.mult, ALU.add),
                                      inc=True)
                    yet_war[gb % 2] = tacc
                    gb += 1
            ewt_pe_done[cur] = p.tok("tensor")
            nxt = ei + 1 if ei < 7 else (0 if q < NQ - 1 else None)
            if nxt is not None:
                p.wait("gpsimd", ewt_pe_done[1 - cur])
                ewt_tok[1 - cur] = load_ewt(nxt, 1 - cur)
        p.vdrain()

        for tc in range(QC):
            emit_cast_transpose(p, bufs, CMB[:, tc * D:(tc + 1) * D], CTq, tc, Q)
        cmbt_tok = p.tok("vector")

        p.wait("tensor", cmbt_tok)
        for tc in range(QC):
            for oh in range(2):
                parts = [(CTq[:, dc * Q + tc * 128: dc * Q + tc * 128 + 128],
                          OWT[:, dc * 1024 + oh * 512: dc * 1024 + (oh + 1) * 512])
                         for dc in range(DC)]
                parts.append((ONES, ROWS[:, R_OUTB + oh * 512:R_OUTB + (oh + 1) * 512]))
                bank, tmm = mm_acc(p, banks, (tc * 2 + oh) % 4, 512, parts)
                dst = X1[:, tc * D + oh * 512: tc * D + (oh + 1) * 512]
                p.wait("vector", tmm)
                tev = p.emit("vector", lambda E, o=dst, i=bank[:]:
                             E.tensor_add(o, o, i), inc=True)
                banks.release((tc * 2 + oh) % 4, tev)

        for tc in range(QC):
            zc = X1[:, tc * D:(tc + 1) * D]
            sl = stat[:, (tc % 2) * 16:(tc % 2) * 16 + 16]
            tz = p.tok("vector")
            p.wait("scalar", tz)
            p.wait("scalar", bufs["_ln_act"][tc % 2])
            tsq = p.emit("scalar", lambda E, o=DISC[:], i=zc, a=sl[:, 0:1]:
                         E.activation(o, i, AF.Square, accum_out=a))
            p.emit("scalar", lambda E, o=sl[:, 1:2], i=sl[:, 0:1], ee=eps6[:]:
                   E.activation(o, i, AF.Ln, bias=ee, scale=1.0 / D))
            tact = p.emit("scalar", lambda E, o=sl[:, 2:3], i=sl[:, 1:2]:
                          E.activation(o, i, AF.Exp, scale=-0.5), inc=True)
            bufs["_ln_act"][tc % 2] = tact
            p.wait("vector", tact)
            p.vdrain()
            th = p.emit("vector", lambda E, o=zc, i=zc, r=sl[:, 2:3], n=NWB[:]:
                        E.scalar_tensor_tensor(o, i, r, n, ALU.mult, ALU.mult), inc=True)
            p.vdrain()
            emit_cast_transpose(p, bufs, zc, XTq, tc, Q)
        ht_tok = p.tok("vector")

        FTq = CTq
        p.wait("tensor", ht_tok)
        for hh in range(2):
            parts = [(FDT[:, dc * 256 + hh * 128: dc * 256 + (hh + 1) * 128],
                      XTq[:, dc * Q:(dc + 1) * Q]) for dc in range(DC)]
            bank, tmm = mm_acc(p, banks, 4 + hh, Q, parts)
            p.wait("scalar", tmm)
            tev = p.emit("scalar", lambda E, o=FTq[:, hh * Q:(hh + 1) * Q], i=bank[:],
                         b=FDB[:, hh:hh + 1]: E.activation(o, i, AF.Gelu, bias=b), inc=True)
            banks.release(4 + hh, tev)
        ff1_tok = p.tok("scalar")

        p.wait("tensor", ff1_tok)
        for tc in range(QC):
            gc = q * QC + tc
            for oh in range(2):
                parts = [(FTq[:, hh * Q + tc * 128: hh * Q + tc * 128 + 128],
                          FUT[:, hh * 1024 + oh * 512: hh * 1024 + (oh + 1) * 512])
                         for hh in range(2)]
                parts.append((ONES, ROWS[:, R_FUB + oh * 512:R_FUB + (oh + 1) * 512]))
                bank, tmm = mm_acc(p, banks, (tc * 2 + oh) % 4, 512, parts)
                tf = tmpf[0]
                zs = X1[:, tc * D + oh * 512: tc * D + (oh + 1) * 512]
                xs = X[:, gc * D + oh * 512: gc * D + (oh + 1) * 512]
                p.wait("vector", tmm)
                p.emit("vector", lambda E, o=tf[:], i=bank[:], z=zs:
                       E.scalar_tensor_tensor(o, i, 1.0, z, ALU.mult, ALU.add))
                p.vdrain()
                tev = p.emit("vector", lambda E, o=xs, i=tf[:]:
                             E.tensor_add(o, o, i), inc=True)
                banks.release((tc * 2 + oh) % 4, tev)



def prep_l0ffn(inputs):
    dwk = np.asarray(inputs["L0_ffn_dw_kernel"]).reshape(8, 128, 3).transpose(1, 0, 2)
    out = {
        "l0f_dwk": np.ascontiguousarray(dwk.reshape(128, 24), dtype=np.float32),
        "l0f_dwb": np.ascontiguousarray(
            np.asarray(inputs["L0_ffn_dw_bias"]).reshape(8, 128).T, dtype=np.float32),
        "l0f_pwt": np.ascontiguousarray(
            np.asarray(inputs["L0_ffn_pw_w"]).T.reshape(8, 128, 1024)).astype(BT),
        "l0f_pwb": np.asarray(inputs["L0_ffn_pw_b"]).reshape(1, 1024).astype(BT),
        "l0f_rwt": np.ascontiguousarray(
            np.asarray(inputs["L0_ffn_router_w"]).T.reshape(8, 128, 8)).astype(BT),
        "l0f_rb": np.asarray(inputs["L0_ffn_router_b"]).reshape(1, 8).astype(BT),
        "l0f_ewt": np.ascontiguousarray(
            np.asarray(inputs["L0_ffn_expert_w"]).transpose(0, 2, 1).reshape(8, 8, 128, 1024)).astype(BT),
        "l0f_eb": np.asarray(inputs["L0_ffn_expert_b"]).reshape(1, 8 * 1024).astype(BT),
        "l0f_owt": np.ascontiguousarray(
            np.asarray(inputs["L0_ffn_out_w"]).T.reshape(8, 128, 1024)).astype(BT),
        "l0f_outb": np.asarray(inputs["L0_ffn_out_b"]).reshape(1, 1024).astype(BT),
        "l0f_nwb": np.ascontiguousarray(
            np.broadcast_to(np.asarray(inputs["L0_ffn_norm_w"]), (128, 1024)),
            dtype=np.float32),
        "l0f_fdt": np.ascontiguousarray(
            np.asarray(inputs["L0_ffn_ff_down_w"]).T.reshape(8, 128, 256)).astype(BT),
        "l0f_fdb": np.ascontiguousarray(
            np.asarray(inputs["L0_ffn_ff_down_b"]).reshape(2, 128).T, dtype=np.float32),
        "l0f_fut": np.ascontiguousarray(
            np.asarray(inputs["L0_ffn_ff_up_w"]).T.reshape(2, 128, 1024)).astype(BT),
        "l0f_fub": np.asarray(inputs["L0_ffn_ff_up_b"]).reshape(1, 1024).astype(BT),
        "onesb": np.ones((1, 128), dtype=BT),
        "identb": np.eye(128, dtype=BT),
    }
    return out




TT0 = 256
ATW = 512


def emit_l0attn(nc, p, C):
    d_wq = nc.dram_tensor("l0a_wq", [32, 8, 128, 128], BF16, kind="ExternalInput").ap()
    d_wk = nc.dram_tensor("l0a_wk", [32, 8, 128, 128], BF16, kind="ExternalInput").ap()
    d_wv = nc.dram_tensor("l0a_wv", [32, 8, 128, 128], BF16, kind="ExternalInput").ap()
    d_wo = nc.dram_tensor("l0a_wo", [8, 128, 64, 128], BF16, kind="ExternalInput").ap()
    d_bo = nc.dram_tensor("l0a_bo", [128, 8], F32, kind="ExternalInput").ap()
    d_em = nc.dram_tensor("l0a_emat2", [128, 128], BF16, kind="ExternalInput").ap()

    X, IDB, ybf, stat = C["X"], C["IDB"], C["ybf"], C["stat"]
    epsb, pb, pbt, banks = C["epsb"], C["pb"], C["pbt"], C["banks"]
    abf, af32 = C["abf"], C["af32"]
    YTt = abf.take(DC * TT0)
    SQK = abf.take(64 * 128)
    VSL = abf.take(65 * 128)
    ATE = abf.take(256 * 64)
    WB = [abf.take(4 * 8 * 128), abf.take(4 * 8 * 128)]
    WOB = [abf.take(32 * 128), abf.take(32 * 128)]
    KSUM = af32.take(TT0)
    KSC = abf.take(TT0)
    QSC = abf.take(TT0)
    ATB = [abf.take(256), abf.take(256)]
    BOC = af32.take(8)
    EM2 = abf.take(128)
    rq = af32.take(16)

    p.dma("sync", EM2[:], d_em)
    p.dma("sync", BOC[:], d_bo)
    consts_tok = p.tok("sync")
    x_toks = C.get("x_toks")

    def load_w(dw, grp, buf):
        src = dw[grp * 4:(grp + 1) * 4].transpose([2, 0, 1, 3])
        p.dma("gpsimd", WB[buf][:].rearrange("p (a s f) -> p a s f", a=4, s=8), src)
        return p.tok("gpsimd")

    def load_wo(cc, jh, buf):
        p.dma("sync", WOB[buf][:].rearrange("p (j f) -> p j f", j=32),
              d_wo[cc][:, jh * 32:(jh + 1) * 32, :])
        return p.tok("sync")

    p.wait("vector", consts_tok)
    p.wait("tensor", consts_tok)
    p.wait("scalar", consts_tok)
    p.emit("vector", lambda E, o=VSL[:, 64 * 128:65 * 128]: E.memset(o, 1.0))
    p.vdrain()
    bufs = C["bufs"]

    ksc_war = [None]
    qsc_war = [None]
    wb_toks = [None, None]
    wb_pe_done = [None, None]
    wob_toks = [None, None]
    wob_pe_done = [None, None]
    gi = 0
    slab_pe_done = None
    ate_dve = None
    ksum_bank = 4

    for tile in range(T // TT0):
        r0 = tile * TT0
        chunks = [r0 // 128, r0 // 128 + 1]
        ln_done = emit_ln(p, X, YTt, bufs, x_toks, eps=1e-5, chunks=chunks, yt_T=TT0)
        p.wait("tensor", ln_done[-1])

        def qkv_path(dw, kind):
            nonlocal gi, slab_pe_done, ate_dve
            for grp in range(8):
                buf = gi % 2
                p.wait("gpsimd", wb_pe_done[buf])
                wb_toks[buf] = load_w(dw, grp, buf)
                p.wait("tensor", wb_toks[buf])
                for lc in range(4):
                    fc = grp * 4 + lc
                    bi = (fc % 2) + (4 if kind == 'k' and False else 0)
                    bi = fc % 2 if kind != 'k' else 2 + fc % 2
                    bank = banks.acquire(bi)
                    tmm = None
                    for dc in range(DC):
                        lhsT = WB[buf][:, (lc * 8 + dc) * 128:(lc * 8 + dc) * 128 + 128]
                        rhs = YTt[:, dc * TT0:(dc + 1) * TT0]
                        tmm = p.emit("tensor", lambda E, o=bank[:, 0:TT0], l=lhsT, r=rhs,
                                     st=(dc == 0), sp=(dc == DC - 1):
                                     E.matmul(o, l, r, start=st, stop=sp), inc=(dc == DC - 1))
                    if kind == 'k':
                        p.wait("scalar", tmm)
                        p.wait("scalar", ksc_war[0])
                        tev = p.emit("scalar", lambda E, o=KSC[:], i=bank[:, 0:TT0]:
                                     E.activation(o, i, AF.Relu), inc=True)
                        banks.release(bi, tev)
                        p.wait("tensor", tev)
                        kb = pb[ksum_bank]
                        p.emit("tensor", lambda E, o=kb[:, 0:TT0], l=EM2[:], r=KSC[:],
                               st=(fc == 0), sp=(fc == 31):
                               E.matmul(o, l, r, start=st, stop=sp), inc=True)
                        ksc_war[0] = p.tok("tensor")
                    elif kind == 'q':
                        p.wait("scalar", tmm)
                        p.wait("scalar", qsc_war[0])
                        tev = p.emit("scalar", lambda E, o=QSC[:], i=bank[:, 0:TT0]:
                                     E.activation(o, i, AF.Relu, scale=0.125), inc=True)
                        banks.release(bi, tev)
                        p.wait("vector", tev)
                        p.wait("vector", slab_pe_done)
                        for hh in range(2):
                            hd = 2 * fc + hh
                            for th in range(2):
                                tq = p.emit("vector", lambda E,
                                            o=SQK[th * 64:th * 64 + 64, hd * 128:(hd + 1) * 128],
                                            a=QSC[hh * 64:hh * 64 + 64, th * 128:(th + 1) * 128],
                                            b=KSUM[hh * 64:hh * 64 + 64, th * 128:(th + 1) * 128]:
                                            E.tensor_tensor(o, a, b, ALU.mult),
                                            inc=(hh == 1 and th == 1))
                        qsc_war[0] = p.tok("vector")
                    else:
                        p.wait("vector", tmm)
                        p.wait("vector", slab_pe_done)
                        for hh in range(2):
                            hd = 2 * fc + hh
                            for th in range(2):
                                tv = p.emit("vector", lambda E,
                                            o=VSL[th * 64:th * 64 + 64, hd * 128:(hd + 1) * 128],
                                            i=bank[hh * 64:hh * 64 + 64, th * 128:(th + 1) * 128]:
                                            E.tensor_copy(o, i), inc=(hh == 1 and th == 1))
                        banks.release(bi, p.tok("vector"))
                wb_pe_done[buf] = p.tok("tensor")
                gi += 1

        bank4 = banks.acquire(ksum_bank)
        qkv_path(d_wk, 'k')
        tks = p.tok("tensor")
        p.wait("vector", tks)
        tkev = p.emit("vector", lambda E, o=KSUM[:], i=pb[ksum_bank][:, 0:TT0]:
                      E.tensor_copy(o, i), inc=True)
        banks.release(ksum_bank, tkev)
        p.vdrain()
        qkv_path(d_wq, 'q')
        qkv_path(d_wv, 'v')
        slab_dve = p.tok("vector")
        p.vdrain()

        p.wait("tensor", slab_dve)
        sq_v = [SQK[h * 64:h * 64 + 64, :].rearrange("p (i t) -> p t i", t=128) for h in (0, 1)]
        vs_v = [VSL[h * 64:h * 64 + 64, :].rearrange("p (j t) -> p t j", t=128) for h in (0, 1)]
        npair = 128
        nbk = (npair + 6) // 7
        for bk in range(nbk):
            bi = bk % 4
            bank = banks.acquire(bi)
            pairs = range(bk * 7, min((bk + 1) * 7, npair))
            tmm = None
            for s, g in enumerate(pairs):
                for h in (0, 1):
                    tmm = p.emit("tensor", lambda E, o=bank[h * 64:h * 64 + 64, s * 65:s * 65 + 65],
                                 l=sq_v[h][:, g, :], r=vs_v[h][:, g, :],
                                 tp=(h * 64, h * 64):
                                 E.matmul(o, l, r, start=True, stop=True, tile_position=tp),
                                 inc=(h == 1 and s == len(list(pairs)) - 1))
            ns = len(list(pairs))
            p.wait("vector", tmm)
            qn_view = bank[:, 0:ns * 65].rearrange("p (s c) -> p s c", c=65)[:, :, 64]
            p.emit("vector", lambda E, o=rq[:, 0:ns], i=qn_view:
                   E.tensor_scalar(o, i, 1e-6, None, ALU.add))
            p.vdrain()
            p.emit("vector", lambda E, o=rq[:, 8:8 + ns], i=rq[:, 0:ns]: E.reciprocal(o, i))
            p.vdrain()
            tev = None
            for s, g in enumerate(pairs):
                dst = ATE[:, (tile % 2) * 8192 + g * 64: (tile % 2) * 8192 + g * 64 + 64]
                tev = p.emit("vector", lambda E, o=dst, i=bank[:, s * 65:s * 65 + 64],
                             r=rq[:, 8 + s:9 + s]:
                             E.tensor_scalar(o, i, r, None, ALU.mult),
                             inc=(s == len(list(pairs)) - 1))
            banks.release(bi, tev)
        slab_pe_done = p.tok("tensor")
        ate_dve = p.tok("vector")

        if tile % 2 == 1:
            p.wait("tensor", ate_dve)
            rr0 = (tile - 1) * TT0
            wi = 0
            for h in (0, 1):
                for cc in range(DC):
                    bi = 4 + h
                    bank = banks.acquire(bi)
                    tmm = None
                    for jh in (0, 1):
                        wbuf = wi % 2
                        p.wait("sync", wob_pe_done[wbuf])
                        wob_toks[wbuf] = load_wo(cc, jh, wbuf)
                        p.wait("tensor", wob_toks[wbuf])
                        for jj in range(32):
                            j = jh * 32 + jj
                            lhsT = WOB[wbuf][h * 64:h * 64 + 64, jj * 128:(jj + 1) * 128]
                            rhs = ATE[h * 64:h * 64 + 64, :].rearrange(
                                "p (g j) -> p j g", j=64)[:, j, :]
                            tmm = p.emit("tensor", lambda E, o=bank[:, 0:256], l=lhsT, r=rhs,
                                         st=(j == 0), sp=(j == 63):
                                         E.matmul(o, l, r, start=st, stop=sp), inc=(jj == 31))
                        wob_pe_done[wbuf] = p.tok("tensor")
                        wi += 1
                    p.wait("scalar", tmm)
                    tab = p.emit("scalar", lambda E, o=ATB[h][:], i=bank[:, 0:256],
                                 b=BOC[:, cc:cc + 1]:
                                 E.activation(o, i, AF.Identity, bias=b), inc=True)
                    banks.release(bi, tab)
                    p.wait("tensor", tab)
                    for gb in range(2):
                        gchunk = (rr0 + h * 128 + gb * 256) // 128
                        slot = gb
                        p.wait("tensor", bufs["_ln_evac"][slot])
                        tpe = p.emit("tensor", lambda E, o=pbt[slot][:, 0:128],
                                     i=ATB[h][:, gb * 128:(gb + 1) * 128], idn=IDB[:]:
                                     E.transpose(o, i, idn), inc=True)
                        p.wait("vector", tpe)
                        bufs["_ln_evac"][slot] = p.emit(
                            "vector", lambda E, o=X[:, gchunk * D + cc * 128: gchunk * D + cc * 128 + 128],
                            i=pbt[slot][:, 0:128]:
                            E.tensor_tensor(o, o, i, ALU.add), inc=True)



def prep_l0attn(inputs):
    wq = np.asarray(inputs["L0_attn_Wq"])
    wk = np.asarray(inputs["L0_attn_Wk"])
    wv = np.asarray(inputs["L0_attn_Wv"])
    wo = np.asarray(inputs["L0_attn_Wo"])
    bo = np.asarray(inputs["L0_attn_bo"])

    def chunk_qkv(w):
        return np.ascontiguousarray(
            w.reshape(8, 128, 32, 128).transpose(2, 0, 1, 3)).astype(BT)

    wop = wo.reshape(64, 64, 1024)
    wop = np.concatenate([wop, wop], axis=0)
    wopc = np.ascontiguousarray(
        wop.reshape(128, 64, 8, 128).transpose(2, 0, 1, 3)).astype(BT)
    em2 = np.fromfunction(lambda p_, m: (p_ % 64 == m % 64), (128, 128)).astype(BT)
    return {
        "l0a_wq": chunk_qkv(wq),
        "l0a_wk": chunk_qkv(wk),
        "l0a_wv": chunk_qkv(wv),
        "l0a_wo": wopc,
        "l0a_bo": np.ascontiguousarray(bo.reshape(8, 128).T, dtype=np.float32),
        "l0a_emat2": em2,
        "identb": np.eye(128, dtype=BT),
    }




NHP_L1A = 8
NHH_L1A = 2
STG_L1A = 10


def emit_l1attn(nc, p, C):
    d_wqkv = nc.dram_tensor("l1a_wqkv", [8, 3, 8, 128, 128], BF16, kind="ExternalInput").ap()
    d_wpj = nc.dram_tensor("l1a_wpj", [8, 128, 1024], BF16, kind="ExternalInput").ap()
    d_i7 = nc.dram_tensor("l1a_i7", [64, 3 * 64], F32, kind="ExternalInput").ap()

    X, IDB, IDF, ybf, stat = C["X"], C["IDB"], C["IDF"], C["ybf"], C["stat"]
    epsb, pb, pbt, banks = C["epsb"], C["pb"], C["pbt"], C["banks"]
    abf, af32 = C["abf"], C["af32"]
    YT = abf.take(DC * T)
    SVT = abf.take(DC * T)
    QKV = abf.take(3 * T)
    WQB = [abf.take(24 * 128)]
    WPJ = abf.take(DC * 1024)
    S3s = af32.take(T)[0:64, :]
    E3 = abf.take(T)[0:64, :]
    E3T = abf.take(16 * 64)
    VTM = abf.take(16 * 64)
    KN1T = abf.take(T)[0:64, :]
    WTs = abf.take(T)[0:64, :]
    KN1C = abf.take(64)
    SM = af32.take(3 * 64)
    QLT2 = abf.take(64)
    KLT2 = abf.take(64)
    _sm = af32.take(1056)
    K2 = _sm[0:64, 0:64]
    K2T = _sm[0:64, 64:128]
    Vcur = _sm[0:64, 128:192]
    VcurT = _sm[0:64, 192:256]
    KVs = _sm[0:64, 256:320]
    KVT = _sm[0:64, 320:384]
    T1 = _sm[0:64, 384:448]
    T2 = _sm[0:64, 448:512]
    T3 = _sm[0:64, 512:576]
    Vn = _sm[0:64, 576:640]
    I7 = _sm[0:64, 640:832]
    ONE64 = _sm[0:64, 832:833]
    ONER = _sm[0:1, 833:897]
    sden = _sm[0:64, 897:901]
    srow = _sm[0:1, 901:981]
    sm2 = _sm[:, 981:989]
    VnT = _sm[0:64, 989:1053]
    _zb = abf.take(128)
    Zb = _zb[0:64, 0:64]
    Bn = _zb[0:64, 64:128]

    p.dma("sync", I7[:], d_i7)
    p.dma("sync", WPJ[:].rearrange("p (s f) -> p s f", s=8), d_wpj.transpose([1, 0, 2]))
    consts_tok = p.tok("sync")
    x_toks = C.get("x_toks")

    def load_wq(hp):
        src = d_wqkv[hp].transpose([2, 0, 1, 3])
        p.dma("gpsimd", WQB[0][:].rearrange("p (a s f) -> p a s f", a=3, s=8), src)
        return p.tok("gpsimd")

    wq_tok = load_wq(0)

    p.wait("vector", consts_tok)
    p.wait("tensor", consts_tok)
    p.wait("scalar", consts_tok)
    p.emit("vector", lambda E, o=ONE64[:]: E.memset(o, 1.0))
    p.emit("vector", lambda E, o=ONER[:]: E.memset(o, 1.0))
    p.vdrain()
    bufs = C["bufs"]

    ln_done = emit_ln(p, X, YT, bufs, x_toks, eps=1e-5)
    p.wait("tensor", ln_done[-1])

    def mm64(bi, lhsT, rhs, n, extra=None, fp=False):
        bank = banks.acquire(bi)
        tmm = p.emit("tensor", lambda E, o=bank[0:lhsT.shape[1], 0:n], l=lhsT, r=rhs:
                     E.matmul(o, l, r, start=True, stop=True), inc=True)
        return bank, tmm

    qkv_dve = None
    for hp in range(NHP_L1A):
        buf = 0
        p.wait("tensor", wq_tok)
        for slot in range(3):
            for ts in range(4):
                bi = (slot * 4 + ts) % 4
                bank = banks.acquire(bi)
                tmm = None
                for dc in range(DC):
                    lhsT = WQB[buf][:, (slot * 8 + dc) * 128:(slot * 8 + dc) * 128 + 128]
                    rhs = YT[:, dc * T + ts * 512: dc * T + (ts + 1) * 512]
                    tmm = p.emit("tensor", lambda E, o=bank[:], l=lhsT, r=rhs,
                                 st=(dc == 0), sp=(dc == DC - 1):
                                 E.matmul(o, l, r, start=st, stop=sp), inc=(dc == DC - 1))
                dst = QKV[:, slot * T + ts * 512: slot * T + (ts + 1) * 512]
                if slot == 0:
                    p.wait("scalar", tmm)
                    p.wait("scalar", qkv_dve)
                    tev = p.emit("scalar", lambda E, o=dst, i=bank[:]:
                                 E.activation(o, i, AF.Identity, scale=0.125), inc=True)
                    eng = "scalar"
                else:
                    p.wait("vector", tmm)
                    p.wait("vector", qkv_dve)
                    tev = p.emit("vector", lambda E, o=dst, i=bank[:]:
                                 E.tensor_copy(o, i), inc=True)
                    eng = "vector"
                banks.release(bi, tev)
        qkv_done_v = p.tok("vector")
        qkv_done_s = p.tok("scalar")
        if hp < NHP_L1A - 1:
            p.wait("gpsimd", p.tok("tensor"))
            wq_tok = load_wq(hp + 1)

        for hh in range(NHH_L1A):
            h = 2 * hp + hh
            Qc = QKV[hh * 64:hh * 64 + 64, 0:T]
            Kc = QKV[hh * 64:hh * 64 + 64, T:2 * T]
            Vc = QKV[hh * 64:hh * 64 + 64, 2 * T:3 * T]
            QLT = QLT2[hh * 64:hh * 64 + 64, :]
            KLT = KLT2[hh * 64:hh * 64 + 64, :]
            p.wait("vector", qkv_done_v)
            p.wait("vector", qkv_done_s)
            p.wait("vector", p.tok("scalar"))
            p.emit("vector", lambda E, o=SM[0:64, 0:64], i=Qc.rearrange("p (m s) -> p m s", s=32):
                   E.tensor_reduce(o, i, AX.X, ALU.add))
            p.emit("vector", lambda E, o=SM[0:64, 64:128], i=Kc.rearrange("p (m s) -> p m s", s=32):
                   E.tensor_reduce(o, i, AX.X, ALU.add))
            p.vdrain()
            p.emit("vector", lambda E, o=QLT, i=SM[0:64, 0:64]:
                   E.tensor_scalar(o, i, 1.0 / 32, None, ALU.mult))
            tql = p.emit("vector", lambda E, o=KLT, i=SM[0:64, 64:128]:
                         E.tensor_scalar(o, i, 1.0 / 32, None, ALU.mult), inc=True)
            p.vdrain()
            if STG_L1A < 2:
                continue
            p.wait("tensor", tql)
            bank, tmm = mm64(4, QLT, KLT, 64)
            p.wait("vector", tmm)
            tng = p.emit("vector", lambda E, o=sden[:, 0:1], i=bank[0:64, 0:64]:
                         E.reduce_max(o, i, AX.X, negate=True), inc=True)
            p.vdrain()
            p.wait("scalar", tng)
            tex = p.emit("scalar", lambda E, o=K2[:], i=bank[0:64, 0:64], b=sden[:, 0:1],
                         a=sden[:, 1:2]:
                         E.activation(o, i, AF.Exp, bias=b, accum_out=a), inc=True)
            p.wait("vector", tex)
            banks.release(4, tex)
            p.emit("vector", lambda E, o=sden[:, 2:3], i=sden[:, 1:2]: E.reciprocal(o, i))
            p.vdrain()
            tk2 = p.emit("vector", lambda E, o=K2[:], i=K2[:], r=sden[:, 2:3]:
                         E.tensor_scalar(o, i, r, None, ALU.mult), inc=True)
            p.vdrain()
            p.wait("tensor", tk2)
            bank5 = banks.acquire(5)
            ttr = p.emit("tensor", lambda E, o=bank5[0:64, 0:64], i=K2[:], idn=IDF[0:64, 0:64]:
                         E.transpose(o, i, idn), inc=True)
            p.wait("vector", ttr)
            tk2t = p.emit("vector", lambda E, o=K2T[:], i=bank5[0:64, 0:64]:
                          E.tensor_copy(o, i), inc=True)
            banks.release(5, tk2t)
            if STG_L1A < 3:
                continue
            p.wait("tensor", tk2t)
            bank, tmm = mm64(4, K2[:], ONE64[:], 1)
            p.wait("vector", tmm)
            tcs = p.emit("vector", lambda E, o=sden[:, 3:4], i=bank[0:64, 0:1]:
                         E.tensor_copy(o, i), inc=True)
            banks.release(4, tcs)
            p.wait("tensor", tcs)
            bank5 = banks.acquire(5)
            ttr = p.emit("tensor", lambda E, o=bank5[0:1, 0:64], i=sden[:, 3:4],
                         idn=IDF[0:64, 0:64]: E.transpose(o, i, idn), inc=True)
            p.wait("vector", ttr)
            trow = p.emit("vector", lambda E, o=srow[:, 0:64], i=bank5[0:1, 0:64]:
                          E.tensor_copy(o, i), inc=True)
            banks.release(5, trow)
            p.vdrain()
            p.emit("vector", lambda E, o=srow[:, 64:65], i=srow[:, 0:64]:
                   E.reduce_max(o, i, AX.X))
            p.vdrain()
            trcp = p.emit("vector", lambda E, o=srow[:, 65:66], i=srow[:, 64:65]:
                          E.reciprocal(o, i), inc=True)
            p.wait("tensor", trcp)
            bank, tmm = mm64(4, ONER[:], srow[:, 65:66], 1)
            p.wait("vector", tmm)
            tden = p.emit("vector", lambda E, o=sden[:, 2:3], i=bank[0:64, 0:1]:
                          E.tensor_copy(o, i), inc=True)
            banks.release(4, tden)
            p.vdrain()
            p.emit("vector", lambda E, o=Vcur[:], i=K2T[:], r=sden[:, 2:3]:
                   E.tensor_scalar(o, i, r, None, ALU.mult))
            tv0 = p.emit("vector", lambda E, o=VcurT[:], i=K2[:], r=sden[:, 2:3]:
                         E.tensor_scalar(o, i, r, None, ALU.mult), inc=True)
            p.vdrain()
            if STG_L1A < 4:
                continue
            tprev = tv0
            for it in range(6):
                p.wait("tensor", tprev)
                bank, tmm = mm64(4, K2T[:], Vcur[:], 64)
                p.wait("vector", tmm)
                tkv = p.emit("vector", lambda E, o=KVs[:], i=bank[0:64, 0:64]:
                             E.tensor_copy(o, i), inc=True)
                banks.release(4, tkv)
                p.vdrain()
                t1t = p.emit("vector", lambda E, o=T1[:], i=KVs[:], s7=I7[:, 0:64]:
                             E.scalar_tensor_tensor(o, i, -1.0, s7, ALU.mult, ALU.add),
                             inc=True)
                p.wait("tensor", t1t)
                bank5 = banks.acquire(5)
                ttr = p.emit("tensor", lambda E, o=bank5[0:64, 0:64], i=KVs[:],
                             idn=IDF[0:64, 0:64]: E.transpose(o, i, idn), inc=True)
                p.wait("vector", ttr)
                tkvt = p.emit("vector", lambda E, o=KVT[:], i=bank5[0:64, 0:64]:
                              E.tensor_copy(o, i), inc=True)
                banks.release(5, tkvt)
                p.wait("tensor", tkvt)
                bank, tmm = mm64(4, KVT[:], T1[:], 64)
                p.wait("vector", tmm)
                tm2 = p.emit("vector", lambda E, o=T2[:], i=bank[0:64, 0:64],
                             s15=I7[:, 64:128]:
                             E.scalar_tensor_tensor(o, i, -1.0, s15, ALU.mult, ALU.add),
                             inc=True)
                banks.release(4, tm2)
                p.vdrain()
                p.wait("tensor", tm2)
                bank, tmm = mm64(4, KVT[:], T2[:], 64)
                p.wait("vector", tmm)
                tm3 = p.emit("vector", lambda E, o=T3[:], i=bank[0:64, 0:64],
                             s13=I7[:, 128:192]:
                             E.scalar_tensor_tensor(o, i, -1.0, s13, ALU.mult, ALU.add),
                             inc=True)
                banks.release(4, tm3)
                p.vdrain()
                p.wait("tensor", tm3)
                bank, tmm = mm64(4, VcurT[:], T3[:], 64)
                bank5, tmm2 = mm64(5, T3[:], VcurT[:], 64)
                p.wait("vector", tmm)
                p.wait("vector", tmm2)
                p.emit("vector", lambda E, o=Vn[:], i=bank[0:64, 0:64]:
                       E.tensor_scalar(o, i, 0.25, None, ALU.mult))
                tvn = p.emit("vector", lambda E, o=VnT[:], i=bank5[0:64, 0:64]:
                             E.tensor_scalar(o, i, 0.25, None, ALU.mult), inc=True)
                banks.release(4, tvn)
                banks.release(5, tvn)
                p.vdrain()
                p.emit("vector", lambda E, o=Vcur[:], i=Vn[:]: E.tensor_copy(o, i))
                tprev = p.emit("vector", lambda E, o=VcurT[:], i=VnT[:]:
                               E.tensor_copy(o, i), inc=True)
                p.vdrain()
            tzb = p.emit("vector", lambda E, o=Zb[:], i=Vcur[:]: E.tensor_copy(o, i),
                         inc=True)
            p.vdrain()

            if STG_L1A < 5:
                continue
            tex = None
            for tc in range(NCH):
                bi = tc % 2
                bank = banks.acquire(bi)
                lhsT = Qc[:, tc * 128:(tc + 1) * 128]
                tmm = p.emit("tensor", lambda E, o=bank[:, 0:64], l=lhsT, r=KLT:
                             E.matmul(o, l, r, start=True, stop=True), inc=True)
                p.wait("vector", tmm)
                p.wait("vector", tex)
                tng = p.emit("vector", lambda E, o=sm2[:, 0:1], i=bank[:, 0:64]:
                             E.reduce_max(o, i, AX.X, negate=True), inc=True)
                p.vdrain()
                p.wait("scalar", tng)
                tex = p.emit("scalar", lambda E, o=KN1C[:], i=bank[:, 0:64],
                             b=sm2[:, 0:1], a=sm2[:, 1:2]:
                             E.activation(o, i, AF.Exp, bias=b, accum_out=a), inc=True)
                banks.release(bi, tex)
                p.wait("vector", tex)
                p.emit("vector", lambda E, o=sm2[:, 2:3], i=sm2[:, 1:2]: E.reciprocal(o, i))
                p.vdrain()
                tkn = p.emit("vector", lambda E, o=KN1C[:], i=KN1C[:], r=sm2[:, 2:3]:
                             E.tensor_scalar(o, i, r, None, ALU.mult), inc=True)
                p.wait("tensor", tkn)
                slot = tc % 2
                p.wait("tensor", bufs["_ln_evac"][slot])
                tpe = p.emit("tensor", lambda E, o=pbt[slot][0:64, 0:128], i=KN1C[:],
                             idn=IDB[:]: E.transpose(o, i, idn), inc=True)
                p.wait("vector", tpe)
                bufs["_ln_evac"][slot] = p.emit(
                    "vector", lambda E, o=KN1T[:, tc * 128:(tc + 1) * 128],
                    i=pbt[slot][0:64, 0:128]: E.tensor_copy(o, i), inc=True)
            kn1_done = p.tok("vector")

            if STG_L1A < 6:
                continue
            for ts in range(4):
                bi = ts % 4
                bank = banks.acquire(bi)
                tmm = p.emit("tensor", lambda E, o=bank[0:64, 0:512], l=QLT,
                             r=Kc[:, ts * 512:(ts + 1) * 512]:
                             E.matmul(o, l, r, start=True, stop=True), inc=True)
                p.wait("vector", tmm)
                tcp = p.emit("vector", lambda E, o=S3s[:, ts * 512:(ts + 1) * 512],
                             i=bank[0:64, 0:512]: E.tensor_copy(o, i), inc=True)
                banks.release(bi, tcp)
            p.vdrain()
            p.wait("vector", p.tok("scalar"))
            tng = p.emit("vector", lambda E, o=sden[:, 0:1], i=S3s[:]:
                         E.reduce_max(o, i, AX.X, negate=True), inc=True)
            p.vdrain()
            p.wait("scalar", tng)
            tex = p.emit("scalar", lambda E, o=E3[:], i=S3s[:], b=sden[:, 0:1],
                         a=sden[:, 1:2]: E.activation(o, i, AF.Exp, bias=b,
                                                      accum_out=a), inc=True)
            p.wait("vector", tex)
            p.emit("vector", lambda E, o=sden[:, 2:3], i=sden[:, 1:2]: E.reciprocal(o, i))
            p.vdrain()
            if STG_L1A < 7:
                continue
            p.wait("tensor", tex)
            idv = IDB[0:64, 0:64] if hh == 0 else IDB[64:128, 64:128]
            for tc in range(NCH):
                slot = tc % 2
                p.wait("tensor", bufs["_ln_evac"][slot])
                tpe = p.emit("tensor", lambda E, o=pbt[slot][:, 0:64],
                             i=E3[:, tc * 128:(tc + 1) * 128], idn=IDB[0:64, 0:64]:
                             E.transpose(o, i, idn), inc=True)
                p.wait("vector", tpe)
                bufs["_ln_evac"][slot] = p.emit(
                    "vector", lambda E, o=E3T[:, tc * 64:(tc + 1) * 64],
                    i=pbt[slot][:, 0:64]: E.tensor_copy(o, i), inc=True)
                p.wait("tensor", bufs["_ln_evac"][slot])
                tpe = p.emit("tensor", lambda E, o=pbt[slot][:, 0:64],
                             i=Vc[:, tc * 128:(tc + 1) * 128], idn=idv:
                             E.transpose(o, i, idn), inc=True)
                p.wait("vector", tpe)
                bufs["_ln_evac"][slot] = p.emit(
                    "vector", lambda E, o=VTM[:, tc * 64:(tc + 1) * 64],
                    i=pbt[slot][:, 0:64]: E.tensor_copy(o, i), inc=True)
            tev3 = p.tok("vector")

            if STG_L1A < 8:
                continue
            p.wait("tensor", tev3)
            bank = banks.acquire(4)
            tmm = None
            for tc in range(NCH):
                tmm = p.emit("tensor", lambda E, o=bank[0:64, 0:64],
                             l=E3T[:, tc * 64:(tc + 1) * 64],
                             r=VTM[:, tc * 64:(tc + 1) * 64],
                             st=(tc == 0), sp=(tc == NCH - 1):
                             E.matmul(o, l, r, start=st, stop=sp), inc=(tc == NCH - 1))
            p.wait("vector", tmm)
            tb = p.emit("vector", lambda E, o=Bn[:], i=bank[0:64, 0:64], r=sden[:, 2:3]:
                        E.tensor_scalar(o, i, r, None, ALU.mult), inc=True)
            banks.release(4, tb)
            p.vdrain()

            if STG_L1A < 9:
                continue
            p.wait("tensor", tzb)
            p.wait("tensor", kn1_done)
            p.wait("tensor", tb)
            for ts in range(4):
                bi = ts % 4
                bank = banks.acquire(bi)
                tmm = p.emit("tensor", lambda E, o=bank[0:64, 0:512], l=Zb[:],
                             r=KN1T[:, ts * 512:(ts + 1) * 512]:
                             E.matmul(o, l, r, start=True, stop=True), inc=True)
                p.wait("vector", tmm)
                tcp = p.emit("vector", lambda E, o=WTs[:, ts * 512:(ts + 1) * 512],
                             i=bank[0:64, 0:512]: E.tensor_copy(o, i), inc=True)
                banks.release(bi, tcp)
            wt_done = p.tok("vector")
            if STG_L1A < 10:
                continue
            p.wait("tensor", wt_done)
            for ts in range(4):
                bi = ts % 4
                bank = banks.acquire(bi)
                tmm = p.emit("tensor", lambda E, o=bank[0:64, 0:512], l=Bn[:],
                             r=WTs[:, ts * 512:(ts + 1) * 512]:
                             E.matmul(o, l, r, start=True, stop=True), inc=True)
                dst = SVT[hh * 64:hh * 64 + 64, hp * T + ts * 512: hp * T + (ts + 1) * 512]
                p.wait("vector", tmm)
                tcp = p.emit("vector", lambda E, o=dst, i=bank[0:64, 0:512]:
                             E.tensor_copy(o, i), inc=True)
                banks.release(bi, tcp)
        qkv_dve = p.tok("vector")

    svt_done = p.tok("vector")
    p.wait("tensor", svt_done)
    for tc in range(NCH):
        for oh in range(2):
            bi = (tc * 2 + oh) % 4
            bank = banks.acquire(bi)
            tmm = None
            for dc in range(DC):
                lhsT = SVT[:, dc * T + tc * 128: dc * T + tc * 128 + 128]
                rhs = WPJ[:, dc * 1024 + oh * 512: dc * 1024 + (oh + 1) * 512]
                tmm = p.emit("tensor", lambda E, o=bank[:], l=lhsT, r=rhs,
                             st=(dc == 0), sp=(dc == DC - 1):
                             E.matmul(o, l, r, start=st, stop=sp), inc=(dc == DC - 1))
            xs = X[:, tc * D + oh * 512: tc * D + (oh + 1) * 512]
            p.wait("vector", tmm)
            tev = p.emit("vector", lambda E, o=xs, i=bank[:]:
                         E.tensor_add(o, o, i), inc=True)
            banks.release(bi, tev)



def prep_l1attn(inputs):
    wqkv = np.asarray(inputs["L1_attn_w_qkv"])
    wpj = np.asarray(inputs["L1_attn_w_proj"])
    wq = wqkv.reshape(8, 128, 3, 8, 128)
    wq = np.ascontiguousarray(wq.transpose(3, 2, 0, 1, 4)).astype(BT)
    i7 = np.concatenate([7 * np.eye(64), 15 * np.eye(64), 13 * np.eye(64)],
                        axis=1).astype(np.float32)
    return {
        "l1a_wqkv": wq,
        "l1a_wpj": np.ascontiguousarray(wpj.reshape(8, 128, 1024)).astype(BT),
        "l1a_i7": i7,
        "identb": np.eye(128, dtype=BT),
        "identf": np.eye(128, dtype=np.float32),
    }



NEXP_L1F = 16
STG_L1F = 3


def emit_l1ffn(nc, p, C):
    d_gate = nc.dram_tensor("l1f_gate", [D, 16], BF16, kind="ExternalInput").ap()
    d_w1 = nc.dram_tensor("l1f_w1", [16, 2, 4, 128, 2048], BF16, kind="ExternalInput").ap()
    d_w2 = nc.dram_tensor("l1f_w2", [16, 2, 8, 128, 1024], BF16, kind="ExternalInput").ap()
    d_b1 = nc.dram_tensor("l1f_b1c", [128, 256], F32, kind="ExternalInput").ap()
    d_b2 = nc.dram_tensor("l1f_b2", [16, 1024], F32, kind="ExternalInput").ap()

    X, IDB, IDF, ybf, stat = C["X"], C["IDB"], C["IDF"], C["ybf"], C["stat"]
    epsb, pb, pbt, banks = C["epsb"], C["pb"], C["pbt"], C["banks"]
    abf, af32 = C["abf"], C["af32"]
    YT = abf.take(DC * T)
    HID = abf.take(16 * 512)
    W1buf = [abf.take(4 * 2048), abf.take(4 * 2048)]
    W2buf = [abf.take(8 * 1024), abf.take(8 * 1024)]
    GATE = abf.take(DC * 16)
    B1C = af32.take(256)
    B2 = af32.take(1024)[0:16, :]
    GT = af32.take(NCH * 16)
    GTT = af32.take(2 * 128)[0:16, :]
    gsc = af32.take(NCH * 16)
    gm = af32.take(16)

    for ap, dr in ((B1C[:], d_b1), (B2[:], d_b2)):
        p.dma("sync", ap, dr)
    for dc in range(DC):
        p.dma("sync", GATE[:, dc * 16:(dc + 1) * 16], d_gate[dc * 128:(dc + 1) * 128, :])
    consts_tok = p.tok("sync")
    x_toks = C.get("x_toks")

    def load_w1(ei):
        for h in range(2):
            src = d_w1[ei, h].transpose([1, 0, 2])
            p.dma("gpsimd", W1buf[h][:].rearrange("p (s f) -> p s f", s=4), src)
        return p.tok("gpsimd")

    def load_w2(ei):
        for h in range(2):
            src = d_w2[ei, h].transpose([1, 0, 2])
            p.dma("gpsimd", W2buf[h][:].rearrange("p (s f) -> p s f", s=8), src)
        return p.tok("gpsimd")

    w1_tok = load_w1(0)
    w2_tok = load_w2(0)

    p.wait("vector", consts_tok)
    p.wait("tensor", consts_tok)
    p.wait("scalar", consts_tok)
    lnbufs = C["bufs"]
    ln_done = emit_ln(p, X, YT, lnbufs, x_toks, eps=1e-5)

    gtt_war = [None, None]
    for c in (range(NCH) if STG_L1F >= 2 else []):
        bi = c % 2
        bank = banks.acquire(bi)
        p.wait("tensor", ln_done[c])
        tpe = None
        for dc in range(DC):
            lhsT = YT[:, dc * T + c * 128: dc * T + c * 128 + 128]
            rhs = GATE[:, dc * 16:(dc + 1) * 16]
            tpe = p.emit("tensor", lambda E, o=bank[:, 0:16], l=lhsT, r=rhs,
                         st=(dc == 0), sp=(dc == DC - 1):
                         E.matmul(o, l, r, start=st, stop=sp),
                         inc=(dc == DC - 1))
        gs = gsc[:, c * 16:(c + 1) * 16]
        g8 = gs[:, 0:8]
        gtc = GT[:, c * 16:(c + 1) * 16]
        p.wait("vector", tpe)
        p.emit("vector", lambda E, o=g8, i=bank[:, 0:16]: E.max(o, i))
        p.vdrain()
        p.emit("vector", lambda E, o=gs[:, 8:9], i=gs[:, 0:1]:
               E.tensor_scalar(o, i, -1.0, None, ALU.mult))
        tmask = p.emit("vector", lambda E, o=gm[:, 0:16], i=bank[:, 0:16], m3=gs[:, 2:3]:
                       E.tensor_scalar(o, i, m3, None, ALU.is_ge), inc=True)
        p.vdrain()
        p.wait("scalar", tmask)
        texp = p.emit("scalar", lambda E, o=gtc, i=bank[:, 0:16], b=gs[:, 8:9]:
                      E.activation(o, i, AF.Exp, bias=b), inc=True)
        banks.release(bi, texp)
        p.wait("vector", texp)
        p.emit("vector", lambda E, o=gtc, i=gtc, m=gm[:, 0:16], s=gs[:, 9:10]:
               E.scalar_tensor_tensor(o, i, 1.0, m, ALU.mult, ALU.mult, accum_out=s))
        p.vdrain()
        p.emit("vector", lambda E, o=gs[:, 10:11], i=gs[:, 9:10]: E.reciprocal(o, i))
        p.vdrain()
        tgt = p.emit("vector", lambda E, o=gtc, i=gtc, r=gs[:, 10:11]:
                     E.tensor_scalar(o, i, r, None, ALU.mult), inc=True)
        if STG_L1F < 3:
            continue
        b2i = 2 + (c % 2)
        bank2 = banks.acquire(b2i)
        gslot = GTT[:, (c % 2) * 128:(c % 2) * 128 + 128]
        p.wait("tensor", tgt)
        ttr = p.emit("tensor", lambda E, o=bank2[0:16, 0:128], i=gtc, idn=IDF[:]:
                     E.transpose(o, i, idn), inc=True)
        p.wait("vector", ttr)
        p.wait("vector", gtt_war[c % 2])
        tcp = p.emit("vector", lambda E, o=gslot,
                     i=bank2[0:16, 0:128]: E.tensor_copy(o, i), inc=True)
        p.wait("tensor", tcp)
        tfree = None
        for oh in range(2):
            p.wait("tensor", tfree)
            tmm = p.emit("tensor", lambda E, o=bank2[:, 0:512], l=gslot,
                         r=B2[:, oh * 512:(oh + 1) * 512]:
                         E.matmul(o, l, r, start=True, stop=True), inc=True)
            p.wait("vector", tmm)
            tfree = p.emit("vector", lambda E, o=X[:, c * D + oh * 512: c * D + (oh + 1) * 512],
                           a=bank2[:, 0:512]: E.tensor_add(o, o, a), inc=True)
        banks.release(b2i, tfree)
        gtt_war[c % 2] = p.tok("tensor")
    p.vdrain()
    init_done_dve = p.tok("vector")

    hid_free_pe = None
    evac_done = {}
    for ei in range(NEXP_L1F):
        p.wait("tensor", w1_tok)
        mm2_last_pe = None
        for th in range(4):
            g = 0
            hid_evac_act = None
            hid_evac_dve = None
            for hc in range(16):
                bi = g % 3
                bank = banks.acquire(bi)
                tmm = None
                for dc in range(DC):
                    lhsT = W1buf[dc // 4][:, (dc % 4) * 2048 + hc * 128: (dc % 4) * 2048 + hc * 128 + 128]
                    rhs = YT[:, dc * T + th * 512: dc * T + (th + 1) * 512]
                    tmm = p.emit("tensor", lambda E, o=bank[:], l=lhsT, r=rhs,
                                 st=(dc == 0), sp=(dc == DC - 1):
                                 E.matmul(o, l, r, start=st, stop=sp),
                                 inc=(dc == DC - 1))
                dst = HID[:, hc * 512:(hc + 1) * 512]
                bcol = B1C[:, ei * 16 + hc: ei * 16 + hc + 1]
                eng = "scalar" if bi < 2 else "vector"
                p.wait(eng, tmm)
                p.wait(eng, hid_free_pe)
                if eng == "scalar":
                    tev = p.emit("scalar", lambda E, o=dst, i=bank[:], b=bcol:
                                 E.activation(o, i, AF.Relu, bias=b), inc=True)
                    hid_evac_act = tev
                else:
                    tev = p.emit("vector", lambda E, o=dst, i=bank[:], b=bcol:
                                 E.tensor_scalar(o, i, b, 0.0, ALU.add, ALU.max), inc=True)
                    hid_evac_dve = tev
                banks.release(bi, tev)
                g += 1
            if th == 3 and ei < NEXP_L1F - 1:
                mm1_pe_done = p.tok("tensor")
                p.wait("gpsimd", mm1_pe_done)
                w1_next = load_w1(ei + 1)
            if th == 0:
                p.wait("tensor", w2_tok)
            p.wait("tensor", hid_evac_act)
            p.wait("tensor", hid_evac_dve)
            g = 0
            for tc in range(4):
                gc = th * 4 + tc
                for oh in range(2):
                    bi = 3 + (g % 3)
                    bank = banks.acquire(bi)
                    tmm = None
                    for hc in range(16):
                        lhsT = HID[:, hc * 512 + tc * 128: hc * 512 + tc * 128 + 128]
                        rhs = W2buf[hc // 8][:, (hc % 8) * 1024 + oh * 512: (hc % 8) * 1024 + (oh + 1) * 512]
                        tmm = p.emit("tensor", lambda E, o=bank[:], l=lhsT, r=rhs,
                                     st=(hc == 0), sp=(hc == 15):
                                     E.matmul(o, l, r, start=st, stop=sp),
                                     inc=(hc == 15))
                    Xs = X[:, gc * D + oh * 512: gc * D + (oh + 1) * 512]
                    gcol = GT[:, gc * 16 + ei: gc * 16 + ei + 1]
                    p.wait("vector", tmm)
                    tev = p.emit("vector", lambda E, o=Xs, i=bank[:], s=gcol:
                                 E.scalar_tensor_tensor(o, i, s, o, ALU.mult, ALU.add),
                                 inc=True)
                    banks.release(bi, tev)
                    g += 1
            mm2_last_pe = p.tok("tensor")
            hid_free_pe = mm2_last_pe
        if ei < NEXP_L1F - 1:
            p.wait("gpsimd", mm2_last_pe)
            w2_tok = load_w2(ei + 1)
            w1_tok = w1_next




def prep_l1ffn(inputs):
    w1 = np.asarray(inputs["L1_ffn_W1"])
    w2 = np.asarray(inputs["L1_ffn_W2"])
    b1 = np.asarray(inputs["L1_ffn_b1"])
    b2 = np.asarray(inputs["L1_ffn_b2"])
    gate = np.asarray(inputs["L1_ffn_gate"])
    return {
        "l1f_gate": gate.astype(BT),
        "l1f_w1": np.ascontiguousarray(w1.reshape(16, 2, 4, 128, 2048)).astype(BT),
        "l1f_w2": np.ascontiguousarray(w2.reshape(16, 2, 8, 128, 1024)).astype(BT),
        "l1f_b1c": np.ascontiguousarray(
            b1.reshape(16, 16, 128).transpose(2, 0, 1).reshape(128, 256)).astype(np.float32),
        "l1f_b2": b2.astype(np.float32),
        "identb": np.eye(128, dtype=BT),
        "identf": np.eye(128, dtype=np.float32),
    }


def build_nc(build_fn):
    nc = bass.Bass("TRN2", target_bir_lowering=False)
    with ExitStack() as ctx:
        build_fn(nc, ctx)
    return nc


def run_launch(nc, prep, x_percore, n_cores=8):
    in_maps = []
    for c in range(n_cores):
        m = dict(prep)
        m["x_in"] = np.ascontiguousarray(x_percore[c], dtype=np.float32)
        in_maps.append(m)
    res = run_bass_kernel_spmd(nc, in_maps, core_ids=list(range(n_cores)))
    return np.stack([r["x_out"] for r in res.results])


def _standalone(emit_fn):
    def build(nc, ctx):
        p = Prog(nc, ctx)
        C = make_common(nc, ctx, p)
        d_x = nc.dram_tensor("x_in", [T, D], F32, kind="ExternalInput").ap()
        d_out = nc.dram_tensor("x_out", [T, D], F32, kind="ExternalOutput").ap()
        C["x_toks"] = emit_load_x(p, C["X"], d_x)
        emit_fn(nc, p, C)
        emit_store_x(p, d_out, C["X"], [p.tok("vector")])
        p.finalize()
    return build


build_l0attn = _standalone(emit_l0attn)
build_l0ffn = _standalone(emit_l0ffn)
build_l1attn = _standalone(emit_l1attn)
build_l1ffn = _standalone(emit_l1ffn)


FUSE_N = 4
FUSE_LIST = ["a0", "f0", "a1", "f1"]
PAD_NOPS = 0
FORCE_BARRIER = False


def build_fused(nc, ctx):
    p = Prog(nc, ctx)
    C = make_common(nc, ctx, p)
    d_x = nc.dram_tensor("x_in", [T, D], F32, kind="ExternalInput").ap()
    d_out = nc.dram_tensor("x_out", [T, D], F32, kind="ExternalOutput").ap()
    C["x_toks"] = emit_load_x(p, C["X"], d_x)
    _all = {"a0": emit_l0attn, "f0": emit_l0ffn, "a1": emit_l1attn, "f1": emit_l1ffn}
    fns = [_all[k] for k in FUSE_LIST[:FUSE_N]]
    for i, fn in enumerate(fns):
        C["abf"].reset()
        C["af32"].reset()
        fn(nc, p, C)
        if i < len(fns) - 1 or FORCE_BARRIER:
            emit_barrier(p, C)
            C["x_toks"] = None
    for _ in range(PAD_NOPS):
        p.emit("vector", lambda E: E.nop())
    emit_store_x(p, d_out, C["X"], [p.tok("vector")])
    p.finalize()


def prep_all(inputs):
    prep = prep_common()
    for pf in (prep_l0attn, prep_l0ffn, prep_l1attn, prep_l1ffn):
        prep.update(pf(inputs))
    return prep


_NC_CACHE = {}


def _get_nc(name, build_fn):
    if name not in _NC_CACHE:
        _NC_CACHE[name] = build_nc(build_fn)
    return _NC_CACHE[name]


def kernel(**inputs):
    x = np.ascontiguousarray(np.asarray(inputs["L0_attn_x"], dtype=np.float32))
    nc = _get_nc("fused", build_fused)
    prep = prep_all(inputs)
    return run_launch(nc, prep, x)


# version 06
# speedup vs baseline: 1.0089x; 1.0007x over previous
import numpy as np
import ml_dtypes
from contextlib import ExitStack

import concourse.bass as bass
import concourse.mybir as mybir
from concourse.bass_utils import run_bass_kernel_spmd

F32 = mybir.dt.float32
BF16 = mybir.dt.bfloat16
AF = mybir.ActivationFunctionType
ALU = mybir.AluOpType
AX = mybir.AxisListType
BT = ml_dtypes.bfloat16

T = 2048
D = 1024
NCH = T // 128
DC = D // 128


ENGINES = ("tensor", "vector", "scalar", "sync", "gpsimd")


class Prog:

    def __init__(self, nc, ctx):
        self.nc = nc
        self.ctx = ctx
        self.q = {e: [] for e in ENGINES}
        self.cnt = {}
        self.s = {}
        self.s_small = {}
        for e in ENGINES:
            h = ctx.enter_context(nc.semaphore("s_" + e))
            self.cnt[h] = 0
            self.s[e] = h
        for e in ("sync", "gpsimd"):
            h = ctx.enter_context(nc.semaphore("sm_" + e))
            self.cnt[h] = 0
            self.s_small[e] = h
        self.s_barrier = {}
        for e in ENGINES:
            h = ctx.enter_context(nc.semaphore("sb_" + e))
            self.cnt[h] = 0
            self.s_barrier[e] = h

    def emit(self, eng, fn, inc=False):
        incs = []
        if inc:
            incs = [(self.s[eng], 1)]
            self.cnt[self.s[eng]] += 1
        self.q[eng].append((fn, incs))
        return self.tok(eng)

    def tok(self, eng):
        s = self.s[eng]
        return (s, self.cnt[s])

    def wait(self, eng, tok):
        if tok is None:
            return
        if isinstance(tok, list):
            for t in tok:
                self.wait(eng, t)
            return
        s, v = tok
        if v <= 0:
            return
        self.q[eng].append((lambda E, s=s, v=v: E.wait_ge(s, v), []))

    def vdrain(self):
        self.q["vector"].append((lambda E: E.drain(), []))

    def dma(self, queue, out_ap, in_ap):
        sb = out_ap if str(out_ap.space).lower().find("sb") >= 0 else in_ap
        try:
            parts = sb.shape[0]
            perpart = sb.free_nbytes()
        except Exception:
            parts, perpart = 128, 4096
        small = parts < 128 or perpart < 512
        s = self.s_small[queue] if small else self.s[queue]
        self.cnt[s] += 16
        self.q[queue].append(
            (lambda E, o=out_ap, i=in_ap: E.dma_start(o, i), [(s, 16)]))
        tok = (s, self.cnt[s])
        if small:
            self.wait(queue, tok)
        return tok

    def finalize(self):
        block = self.ctx.enter_context(self.nc.Block())
        for ename in ENGINES:
            items = self.q[ename]

            def body(E, items=items):
                for fn, incs in items:
                    inst = fn(E)
                    for s, k in incs:
                        inst.then_inc(s, k)

            getattr(block, ename)(body)


class Banks:

    def __init__(self, p, pb):
        self.p = p
        self.pb = pb
        self.free_tok = [None] * len(pb)

    def acquire(self, i):
        self.p.wait("tensor", self.free_tok[i])
        return self.pb[i]

    def release(self, i, tok):
        self.free_tok[i] = tok



def emit_load_x(p, X, d_x):
    return [p.dma("sync", X[:, c * D:(c + 1) * D], d_x[c * 128:(c + 1) * 128, :])
            for c in range(NCH)]


def emit_store_x(p, d_out, X, wait_toks):
    for tk in wait_toks:
        p.wait("sync", tk)
    last = None
    for c in range(NCH):
        last = p.dma("sync", d_out[c * 128:(c + 1) * 128, :], X[:, c * D:(c + 1) * D])
    p.wait("sync", last)


def emit_ln(p, X, YT, bufs, x_toks, eps=1e-5, y_tm=None, chunks=None, yt_T=T):
    if chunks is None:
        chunks = list(range(NCH))
    ybf, stat, pbt, identb = bufs["ybf"], bufs["stat"], bufs["pbt"], bufs["identb"]
    evac_tok = bufs.setdefault("_ln_evac", [None, None])
    ybf_war = bufs.setdefault("_ln_ybf_war", [None, None])
    act_tok = bufs.setdefault("_ln_act", [None, None])
    chunk_done = []
    for li, c in enumerate(chunks):
        xc = X[:, c * D:(c + 1) * D]
        sl = stat[:, (li % 2) * 16:(li % 2) * 16 + 16]
        if x_toks is not None:
            p.wait("vector", x_toks[c])
        p.wait("vector", act_tok[li % 2])
        p.emit("vector", lambda E, a=sl[:, 0:6], b=xc[:, 0:512]: E.bn_stats(a, b))
        p.emit("vector", lambda E, a=sl[:, 6:12], b=xc[:, 512:1024]: E.bn_stats(a, b))
        p.vdrain()
        tdve = p.emit("vector", lambda E, a=sl[:, 12:14], b=sl[:, 0:12]:
                      E.bn_aggr(a, b), inc=True)
        p.wait("scalar", tdve)
        p.emit("scalar", lambda E, o=sl[:, 14:15], i=sl[:, 13:14], ee=bufs["epsb"]:
               E.activation(o, i, AF.Ln, bias=ee))
        tact = p.emit("scalar", lambda E, o=sl[:, 15:16], i=sl[:, 14:15]:
                      E.activation(o, i, AF.Exp, scale=-0.5), inc=True)
        act_tok[li % 2] = tact
        yb = ybf[li % 2]
        p.wait("vector", tact)
        p.wait("vector", ybf_war[li % 2])
        p.vdrain()
        if y_tm is not None:
            p.emit("vector", lambda E, o=y_tm[:, li * D:(li + 1) * D], i=xc,
                   m=sl[:, 12:13], r=sl[:, 15:16]:
                   E.tensor_scalar(o, i, m, r, ALU.subtract, ALU.mult))
            p.vdrain()
            tapply = p.emit("vector", lambda E, o=yb[:], i=y_tm[:, li * D:(li + 1) * D]:
                            E.tensor_copy(o, i), inc=True)
        else:
            tapply = p.emit("vector", lambda E, o=yb[:], i=xc,
                            m=sl[:, 12:13], r=sl[:, 15:16]:
                            E.tensor_scalar(o, i, m, r, ALU.subtract, ALU.mult), inc=True)
        p.wait("tensor", tapply)
        emit_transposes(p, bufs, yb, YT, li, yt_T)
        ybf_war[li % 2] = p.tok("tensor")
        chunk_done.append(p.tok("vector"))
    return chunk_done


def emit_transposes(p, bufs, yb, YT, li, yt_T):
    pbt, identb = bufs["pbt"], bufs["identb"]
    evac_tok = bufs.setdefault("_ln_evac", [None, None])
    for dc in range(DC):
        slot = dc % 2
        p.wait("tensor", evac_tok[slot])
        tpe = p.emit("tensor", lambda E, o=pbt[slot][:, 0:128],
                     i=yb[:, dc * 128:dc * 128 + 128], idn=identb[:]:
                     E.transpose(o, i, idn), inc=True)
        p.wait("vector", tpe)
        evac_tok[slot] = p.emit(
            "vector", lambda E, o=YT[:, dc * yt_T + li * 128: dc * yt_T + li * 128 + 128],
            i=pbt[slot][:, 0:128]: E.tensor_copy(o, i), inc=True)


def emit_cast_transpose(p, bufs, src_f32, YT, li, yt_T):
    ybf = bufs["ybf"]
    ybf_war = bufs.setdefault("_ln_ybf_war", [None, None])
    yb = ybf[li % 2]
    p.wait("vector", ybf_war[li % 2])
    p.vdrain()
    tcast = p.emit("vector", lambda E, o=yb[:], i=src_f32: E.tensor_copy(o, i), inc=True)
    p.wait("tensor", tcast)
    emit_transposes(p, bufs, yb, YT, li, yt_T)
    ybf_war[li % 2] = p.tok("tensor")



class Arena:

    def __init__(self, t):
        self.t = t
        self.n = t.shape[1]
        self.off = 0

    def take(self, cols):
        v = self.t[:, self.off:self.off + cols]
        self.off += cols
        assert self.off <= self.n, f"arena overflow {self.off} > {self.n}"
        return v

    def reset(self):
        self.off = 0


ABF_COLS = 58700
AF32_COLS = 5830


def make_common(nc, ctx, p):
    e = ctx.enter_context
    C = {}
    C["X"] = e(nc.sbuf_tensor("X", [128, NCH * D], F32))
    C["IDB"] = e(nc.sbuf_tensor("IDB", [128, 128], BF16))
    C["IDF"] = e(nc.sbuf_tensor("IDF", [128, 128], F32))
    C["ybf"] = [e(nc.sbuf_tensor("ybf0", [128, D], BF16)),
                e(nc.sbuf_tensor("ybf1", [128, D], BF16))]
    C["stat"] = e(nc.sbuf_tensor("stat", [128, 32], F32))
    C["epsb"] = e(nc.sbuf_tensor("epsb", [128, 1], F32))
    C["eps6"] = e(nc.sbuf_tensor("eps6", [128, 1], F32))
    C["abf"] = Arena(e(nc.sbuf_tensor("ABF", [128, ABF_COLS], BF16)))
    C["af32"] = Arena(e(nc.sbuf_tensor("AF32", [128, AF32_COLS], F32)))
    C["pb"] = [e(nc.psum_tensor(f"pb{i}", [128, 512], F32)) for i in range(6)]
    C["pbt"] = [e(nc.psum_tensor("pbt0", [128, 256], BF16)),
                e(nc.psum_tensor("pbt1", [128, 256], BF16))]
    C["banks"] = Banks(p, C["pb"])
    d_idb = nc.dram_tensor("identb", [128, 128], BF16, kind="ExternalInput").ap()
    d_idf = nc.dram_tensor("identf", [128, 128], F32, kind="ExternalInput").ap()
    p.dma("sync", C["IDB"][:], d_idb)
    p.dma("sync", C["IDF"][:], d_idf)
    ct = p.tok("sync")
    for en in ("vector", "tensor", "scalar"):
        p.wait(en, ct)
    p.emit("vector", lambda E, o=C["epsb"][:]: E.memset(o, 1e-5))
    p.emit("vector", lambda E, o=C["eps6"][:]: E.memset(o, 1e-6))
    p.vdrain()
    C["bufs"] = dict(ybf=C["ybf"], stat=C["stat"], pbt=C["pbt"], identb=C["IDB"],
                     epsb=C["epsb"][:])
    return C


BARRIER_WAIT_ON = list(ENGINES)


def emit_barrier(p, C):
    p.vdrain()
    toks = {}
    for en in ENGINES:
        sb = p.s_barrier[en]
        p.cnt[sb] += 1
        if en == "tensor":
            bank0 = C["banks"].acquire(0)
            p.q[en].append((lambda E, o=bank0[0:128, 0:128], l=C["IDB"][:],
                            r=C["IDB"][:]: E.matmul(o, l, r, start=True, stop=True),
                            [(sb, 1)]))
        else:
            p.q[en].append((lambda E: E.nop(nofuse=True), [(sb, 1)]))
        toks[en] = (sb, p.cnt[sb])
    for en in ENGINES:
        for other in ENGINES:
            if other != en and other in BARRIER_WAIT_ON:
                p.wait(en, toks[other])
    bt = p.tok("tensor")
    for i in range(len(C["banks"].free_tok)):
        C["banks"].free_tok[i] = toks["vector"]
    C["bufs"]["_ln_evac"] = [toks["vector"], toks["vector"]]
    C["bufs"]["_ln_ybf_war"] = [toks["tensor"], toks["tensor"]]
    C["bufs"]["_ln_act"] = [toks["scalar"], toks["scalar"]]


def prep_common():
    return {"identb": np.eye(128, dtype=BT), "identf": np.eye(128, dtype=np.float32)}


def mm_acc(p, banks, bi, ncols, parts, out_base=0, npart=128):
    bank = banks.acquire(bi)
    n = len(parts)
    tmm = None
    for i, (lhsT, rhs) in enumerate(parts):
        tmm = p.emit("tensor", lambda E, o=bank[out_base:out_base + npart, 0:ncols],
                     l=lhsT, r=rhs, st=(i == 0), sp=(i == n - 1):
                     E.matmul(o, l, r, start=st, stop=sp), inc=(i == n - 1))
    return bank, tmm


def emit_top_gate(p, gs, gm, gout, bank_ap, nexp, kth, drain=True):
    p.emit("vector", lambda E, o=gs[:, 0:8], i=bank_ap: E.max(o, i))
    p.vdrain()
    p.emit("vector", lambda E, o=gs[:, 8:9], i=gs[:, 0:1]:
           E.tensor_scalar(o, i, -1.0, None, ALU.mult))
    tmask = p.emit("vector", lambda E, o=gm[:, 0:nexp], i=bank_ap, mk=gs[:, kth:kth + 1]:
                   E.tensor_scalar(o, i, mk, None, ALU.is_ge), inc=True)
    p.vdrain()
    p.wait("scalar", tmask)
    texp = p.emit("scalar", lambda E, o=gout, i=bank_ap, b=gs[:, 8:9]:
                  E.activation(o, i, AF.Exp, bias=b), inc=True)
    p.wait("vector", texp)
    p.emit("vector", lambda E, o=gout, i=gout, m=gm[:, 0:nexp], s=gs[:, 9:10]:
           E.scalar_tensor_tensor(o, i, 1.0, m, ALU.mult, ALU.mult, accum_out=s))
    p.vdrain()
    p.emit("vector", lambda E, o=gs[:, 10:11], i=gs[:, 9:10]: E.reciprocal(o, i))
    p.vdrain()
    tgt = p.emit("vector", lambda E, o=gout, i=gout, r=gs[:, 10:11]:
                 E.tensor_scalar(o, i, r, None, ALU.mult), inc=True)
    if drain:
        p.vdrain()
    return tgt, texp



Q = 512
QC = Q // 128
NQ = T // Q


def emit_l0ffn(nc, p, C):
    d_dwk = nc.dram_tensor("l0f_dwk", [128, 24], F32, kind="ExternalInput").ap()
    d_dwb = nc.dram_tensor("l0f_dwb", [128, 8], F32, kind="ExternalInput").ap()
    d_pwt = nc.dram_tensor("l0f_pwt", [8, 128, 1024], BF16, kind="ExternalInput").ap()
    d_pwb = nc.dram_tensor("l0f_pwb", [1, 1024], BF16, kind="ExternalInput").ap()
    d_rwt = nc.dram_tensor("l0f_rwt", [8, 128, 8], BF16, kind="ExternalInput").ap()
    d_rb = nc.dram_tensor("l0f_rb", [1, 8], BF16, kind="ExternalInput").ap()
    d_ewt = nc.dram_tensor("l0f_ewt", [8, 8, 128, 1024], BF16, kind="ExternalInput").ap()
    d_eb = nc.dram_tensor("l0f_eb", [1, 8 * 1024], BF16, kind="ExternalInput").ap()
    d_owt = nc.dram_tensor("l0f_owt", [8, 128, 1024], BF16, kind="ExternalInput").ap()
    d_outb = nc.dram_tensor("l0f_outb", [1, 1024], BF16, kind="ExternalInput").ap()
    d_nwb = nc.dram_tensor("l0f_nwb", [128, 1024], F32, kind="ExternalInput").ap()
    d_fdt = nc.dram_tensor("l0f_fdt", [8, 128, 256], BF16, kind="ExternalInput").ap()
    d_fdb = nc.dram_tensor("l0f_fdb", [128, 2], F32, kind="ExternalInput").ap()
    d_fut = nc.dram_tensor("l0f_fut", [2, 128, 1024], BF16, kind="ExternalInput").ap()
    d_fub = nc.dram_tensor("l0f_fub", [1, 1024], BF16, kind="ExternalInput").ap()
    d_ones = nc.dram_tensor("onesb", [1, 128], BF16, kind="ExternalInput").ap()

    X, IDB, ybf, stat = C["X"], C["IDB"], C["ybf"], C["stat"]
    epsb, eps6, pb, pbt, banks = C["epsb"], C["eps6"], C["pb"], C["pbt"], C["banks"]
    abf, af32 = C["abf"], C["af32"]
    X1 = af32.take(QC * D)
    YTq = abf.take(DC * Q)
    CTq = abf.take(DC * Q)
    XTq = YTq
    CMB = abf.take(QC * D)
    PWT = abf.take(DC * 1024)
    OWT = abf.take(DC * 1024)
    FDT = abf.take(DC * 256)
    FUT = abf.take(2 * 1024)
    EWT = [abf.take(DC * 1024), abf.take(DC * 1024)]
    RWT = abf.take(DC * 8)
    NWB = af32.take(1024)
    DWK = af32.take(24)
    DWB = af32.take(8)
    FDB = af32.take(2)
    ROWS = abf.take(3 * 1024 + 8 + 128)[0:1, :]
    EBR = [abf.take(1024)[0:1, :], abf.take(1024)[0:1, :]]
    RW = af32.take(NCH * 8)
    gsc = af32.take(16)
    gm = af32.take(16)
    tmb = [abf.take(512), abf.take(512)]
    yet = [abf.take(512), abf.take(512)]
    TCIN = abf.take(Q + 2)
    DISC = abf.take(1024)
    tail = abf.take(DC * 2)
    cacc = af32.take(Q)
    tmpf = [cacc]

    R_PWB, R_OUTB, R_FUB, R_RB, R_ONES = 0, 1024, 2048, 3072, 3080
    p.dma("sync", DWK[:], d_dwk)
    p.dma("sync", DWB[:], d_dwb)
    p.dma("sync", FDB[:], d_fdb)
    p.dma("sync", NWB[:], d_nwb)
    p.dma("sync", ROWS[:, R_PWB:R_PWB + 1024], d_pwb)
    p.dma("sync", ROWS[:, R_OUTB:R_OUTB + 1024], d_outb)
    p.dma("sync", ROWS[:, R_FUB:R_FUB + 1024], d_fub)
    p.dma("sync", ROWS[:, R_RB:R_RB + 8], d_rb)
    p.dma("sync", ROWS[:, R_ONES:R_ONES + 128], d_ones)
    for dc in range(DC):
        p.dma("sync", RWT[:, dc * 8:(dc + 1) * 8], d_rwt[dc])
    p.dma("sync", PWT[:].rearrange("p (s f) -> p s f", s=8), d_pwt.transpose([1, 0, 2]))
    p.dma("sync", OWT[:].rearrange("p (s f) -> p s f", s=8), d_owt.transpose([1, 0, 2]))
    p.dma("sync", FDT[:].rearrange("p (s f) -> p s f", s=8), d_fdt.transpose([1, 0, 2]))
    p.dma("sync", FUT[:].rearrange("p (s f) -> p s f", s=2), d_fut.transpose([1, 0, 2]))
    consts_tok = p.tok("sync")
    x_toks = C.get("x_toks")

    def load_ewt(ei, buf):
        p.dma("gpsimd", EWT[buf][:].rearrange("p (s f) -> p s f", s=8),
              d_ewt[ei].transpose([1, 0, 2]))
        p.dma("gpsimd", EBR[buf][:], d_eb[:, ei * 1024:(ei + 1) * 1024])
        return p.tok("gpsimd")

    ONES = ROWS[:, R_ONES:R_ONES + 128]
    ewt_tok = [load_ewt(0, 0), None]

    p.wait("vector", consts_tok)
    p.wait("tensor", consts_tok)
    p.wait("scalar", consts_tok)
    p.emit("vector", lambda E, o=tail[:]: E.memset(o, 0.0))
    p.vdrain()
    bufs = C["bufs"]

    tmb_war = [None, None]
    yet_war = [None, None]
    ewt_pe_done = [None, None]
    for q in range(NQ):
        chunks = list(range(q * QC, (q + 1) * QC))
        ln_done = emit_ln(p, X, YTq, bufs, x_toks, eps=1e-5, y_tm=X1,
                          chunks=chunks, yt_T=Q)
        for dc in range(DC):
            w0 = DWK[:, dc * 3:dc * 3 + 1]
            w1 = DWK[:, dc * 3 + 1:dc * 3 + 2]
            w2 = DWK[:, dc * 3 + 2:dc * 3 + 3]
            bcol = DWB[:, dc:dc + 1]
            ytc = YTq[:, dc * Q:(dc + 1) * Q]
            p.wait("vector", ln_done[-1])
            p.vdrain()
            p.emit("vector", lambda E, o=TCIN[:, 0:2], i=tail[:, dc * 2:dc * 2 + 2]:
                   E.tensor_copy(o, i))
            p.emit("vector", lambda E, o=TCIN[:, 2:Q + 2], i=ytc: E.tensor_copy(o, i))
            p.emit("vector", lambda E, o=tail[:, dc * 2:dc * 2 + 2], i=ytc[:, Q - 2:Q]:
                   E.tensor_copy(o, i))
            p.vdrain()
            p.emit("vector", lambda E, o=cacc[:], i=TCIN[:, 2:Q + 2], a=w2, b=bcol:
                   E.tensor_scalar(o, i, a, b, ALU.mult, ALU.add))
            p.vdrain()
            p.emit("vector", lambda E, o=cacc[:], i=TCIN[:, 1:Q + 1], a=w1, x=cacc[:]:
                   E.scalar_tensor_tensor(o, i, a, x, ALU.mult, ALU.add))
            p.vdrain()
            p.emit("vector", lambda E, o=CTq[:, dc * Q:(dc + 1) * Q], i=TCIN[:, 0:Q],
                   a=w0, x=cacc[:]:
                   E.scalar_tensor_tensor(o, i, a, x, ALU.mult, ALU.add), inc=True)
        p.vdrain()
        conv_done = p.tok("vector")

        p.wait("tensor", conv_done)
        for tc in range(QC):
            gc = q * QC + tc
            for eh in range(2):
                parts = [(CTq[:, dc * Q + tc * 128: dc * Q + tc * 128 + 128],
                          PWT[:, dc * 1024 + eh * 512: dc * 1024 + (eh + 1) * 512])
                         for dc in range(DC)]
                parts.append((ONES, ROWS[:, R_PWB + eh * 512:R_PWB + (eh + 1) * 512]))
                bank, tmm = mm_acc(p, banks, (tc * 2 + eh) % 4, 512, parts)
                tb = tmb[eh]
                p.wait("scalar", tmm)
                p.wait("scalar", tmb_war[eh])
                tev = p.emit("scalar", lambda E, o=tb[:], i=bank[:]:
                             E.activation(o, i, AF.Silu), inc=True)
                banks.release((tc * 2 + eh) % 4, tev)
                x1s = X1[:, tc * D + eh * 512: tc * D + (eh + 1) * 512]
                p.wait("vector", tev)
                tadd = p.emit("vector", lambda E, o=x1s, i=x1s, b=tb[:]:
                              E.tensor_add(o, i, b), inc=True)
                tmb_war[eh] = tadd
            p.vdrain()
            emit_cast_transpose(p, bufs, X1[:, tc * D:(tc + 1) * D], XTq, tc, Q)
            parts = [(XTq[:, dc * Q + tc * 128: dc * Q + tc * 128 + 128],
                      RWT[:, dc * 8:(dc + 1) * 8]) for dc in range(DC)]
            parts.append((ONES, ROWS[:, R_RB:R_RB + 8]))
            xt_tok = p.tok("vector")
            p.wait("tensor", xt_tok)
            bank, tmm = mm_acc(p, banks, 4 + tc % 2, 8, parts)
            p.wait("vector", tmm)
            rwc = RW[:, gc * 8:(gc + 1) * 8]
            tgt, texp = emit_top_gate(p, gsc, gm, rwc, bank[:, 0:8], 8, 1)
            banks.release(4 + tc % 2, texp)

        gb = 0
        for ei in range(8):
            cur = ei % 2
            p.wait("tensor", ewt_tok[cur])
            for tc in range(QC):
                gc = q * QC + tc
                for oh in range(2):
                    parts = [(XTq[:, dc * Q + tc * 128: dc * Q + tc * 128 + 128],
                              EWT[cur][:, dc * 1024 + oh * 512: dc * 1024 + (oh + 1) * 512])
                             for dc in range(DC)]
                    parts.append((ONES, EBR[cur][:, oh * 512:(oh + 1) * 512]))
                    bank, tmm = mm_acc(p, banks, gb % 4, 512, parts)
                    ys = yet[gb % 2]
                    p.wait("scalar", tmm)
                    p.wait("scalar", yet_war[gb % 2])
                    tev = p.emit("scalar", lambda E, o=ys[:], i=bank[:]:
                                 E.activation(o, i, AF.Silu), inc=True)
                    banks.release(gb % 4, tev)
                    dst = CMB[:, tc * D + oh * 512: tc * D + (oh + 1) * 512]
                    rcol = RW[:, gc * 8 + ei: gc * 8 + ei + 1]
                    p.wait("vector", tev)
                    if ei == 0:
                        tacc = p.emit("vector", lambda E, o=dst, i=ys[:], r=rcol:
                                      E.tensor_scalar(o, i, r, None, ALU.mult), inc=True)
                    else:
                        tacc = p.emit("vector", lambda E, o=dst, i=ys[:], r=rcol:
                                      E.scalar_tensor_tensor(o, i, r, o, ALU.mult, ALU.add),
                                      inc=True)
                    yet_war[gb % 2] = tacc
                    gb += 1
            ewt_pe_done[cur] = p.tok("tensor")
            nxt = ei + 1 if ei < 7 else (0 if q < NQ - 1 else None)
            if nxt is not None:
                p.wait("gpsimd", ewt_pe_done[1 - cur])
                ewt_tok[1 - cur] = load_ewt(nxt, 1 - cur)
        p.vdrain()

        for tc in range(QC):
            emit_cast_transpose(p, bufs, CMB[:, tc * D:(tc + 1) * D], CTq, tc, Q)
        cmbt_tok = p.tok("vector")

        p.wait("tensor", cmbt_tok)
        for tc in range(QC):
            for oh in range(2):
                parts = [(CTq[:, dc * Q + tc * 128: dc * Q + tc * 128 + 128],
                          OWT[:, dc * 1024 + oh * 512: dc * 1024 + (oh + 1) * 512])
                         for dc in range(DC)]
                parts.append((ONES, ROWS[:, R_OUTB + oh * 512:R_OUTB + (oh + 1) * 512]))
                bank, tmm = mm_acc(p, banks, (tc * 2 + oh) % 4, 512, parts)
                dst = X1[:, tc * D + oh * 512: tc * D + (oh + 1) * 512]
                p.wait("vector", tmm)
                tev = p.emit("vector", lambda E, o=dst, i=bank[:]:
                             E.tensor_add(o, o, i), inc=True)
                banks.release((tc * 2 + oh) % 4, tev)

        for tc in range(QC):
            zc = X1[:, tc * D:(tc + 1) * D]
            sl = stat[:, (tc % 2) * 16:(tc % 2) * 16 + 16]
            tz = p.tok("vector")
            p.wait("scalar", tz)
            p.wait("scalar", bufs["_ln_act"][tc % 2])
            tsq = p.emit("scalar", lambda E, o=DISC[:], i=zc, a=sl[:, 0:1]:
                         E.activation(o, i, AF.Square, accum_out=a))
            p.emit("scalar", lambda E, o=sl[:, 1:2], i=sl[:, 0:1], ee=eps6[:]:
                   E.activation(o, i, AF.Ln, bias=ee, scale=1.0 / D))
            tact = p.emit("scalar", lambda E, o=sl[:, 2:3], i=sl[:, 1:2]:
                          E.activation(o, i, AF.Exp, scale=-0.5), inc=True)
            bufs["_ln_act"][tc % 2] = tact
            p.wait("vector", tact)
            p.vdrain()
            th = p.emit("vector", lambda E, o=zc, i=zc, r=sl[:, 2:3], n=NWB[:]:
                        E.scalar_tensor_tensor(o, i, r, n, ALU.mult, ALU.mult), inc=True)
            p.vdrain()
            emit_cast_transpose(p, bufs, zc, XTq, tc, Q)
        ht_tok = p.tok("vector")

        FTq = CTq
        p.wait("tensor", ht_tok)
        for hh in range(2):
            parts = [(FDT[:, dc * 256 + hh * 128: dc * 256 + (hh + 1) * 128],
                      XTq[:, dc * Q:(dc + 1) * Q]) for dc in range(DC)]
            bank, tmm = mm_acc(p, banks, 4 + hh, Q, parts)
            p.wait("scalar", tmm)
            tev = p.emit("scalar", lambda E, o=FTq[:, hh * Q:(hh + 1) * Q], i=bank[:],
                         b=FDB[:, hh:hh + 1]: E.activation(o, i, AF.Gelu, bias=b), inc=True)
            banks.release(4 + hh, tev)
        ff1_tok = p.tok("scalar")

        p.wait("tensor", ff1_tok)
        for tc in range(QC):
            gc = q * QC + tc
            for oh in range(2):
                parts = [(FTq[:, hh * Q + tc * 128: hh * Q + tc * 128 + 128],
                          FUT[:, hh * 1024 + oh * 512: hh * 1024 + (oh + 1) * 512])
                         for hh in range(2)]
                parts.append((ONES, ROWS[:, R_FUB + oh * 512:R_FUB + (oh + 1) * 512]))
                bank, tmm = mm_acc(p, banks, (tc * 2 + oh) % 4, 512, parts)
                tf = tmpf[0]
                zs = X1[:, tc * D + oh * 512: tc * D + (oh + 1) * 512]
                xs = X[:, gc * D + oh * 512: gc * D + (oh + 1) * 512]
                p.wait("vector", tmm)
                p.emit("vector", lambda E, o=tf[:], i=bank[:], z=zs:
                       E.scalar_tensor_tensor(o, i, 1.0, z, ALU.mult, ALU.add))
                p.vdrain()
                tev = p.emit("vector", lambda E, o=xs, i=tf[:]:
                             E.tensor_add(o, o, i), inc=True)
                banks.release((tc * 2 + oh) % 4, tev)



def prep_l0ffn(inputs):
    dwk = np.asarray(inputs["L0_ffn_dw_kernel"]).reshape(8, 128, 3).transpose(1, 0, 2)
    out = {
        "l0f_dwk": np.ascontiguousarray(dwk.reshape(128, 24), dtype=np.float32),
        "l0f_dwb": np.ascontiguousarray(
            np.asarray(inputs["L0_ffn_dw_bias"]).reshape(8, 128).T, dtype=np.float32),
        "l0f_pwt": np.ascontiguousarray(
            np.asarray(inputs["L0_ffn_pw_w"]).T.reshape(8, 128, 1024)).astype(BT),
        "l0f_pwb": np.asarray(inputs["L0_ffn_pw_b"]).reshape(1, 1024).astype(BT),
        "l0f_rwt": np.ascontiguousarray(
            np.asarray(inputs["L0_ffn_router_w"]).T.reshape(8, 128, 8)).astype(BT),
        "l0f_rb": np.asarray(inputs["L0_ffn_router_b"]).reshape(1, 8).astype(BT),
        "l0f_ewt": np.ascontiguousarray(
            np.asarray(inputs["L0_ffn_expert_w"]).transpose(0, 2, 1).reshape(8, 8, 128, 1024)).astype(BT),
        "l0f_eb": np.asarray(inputs["L0_ffn_expert_b"]).reshape(1, 8 * 1024).astype(BT),
        "l0f_owt": np.ascontiguousarray(
            np.asarray(inputs["L0_ffn_out_w"]).T.reshape(8, 128, 1024)).astype(BT),
        "l0f_outb": np.asarray(inputs["L0_ffn_out_b"]).reshape(1, 1024).astype(BT),
        "l0f_nwb": np.ascontiguousarray(
            np.broadcast_to(np.asarray(inputs["L0_ffn_norm_w"]), (128, 1024)),
            dtype=np.float32),
        "l0f_fdt": np.ascontiguousarray(
            np.asarray(inputs["L0_ffn_ff_down_w"]).T.reshape(8, 128, 256)).astype(BT),
        "l0f_fdb": np.ascontiguousarray(
            np.asarray(inputs["L0_ffn_ff_down_b"]).reshape(2, 128).T, dtype=np.float32),
        "l0f_fut": np.ascontiguousarray(
            np.asarray(inputs["L0_ffn_ff_up_w"]).T.reshape(2, 128, 1024)).astype(BT),
        "l0f_fub": np.asarray(inputs["L0_ffn_ff_up_b"]).reshape(1, 1024).astype(BT),
        "onesb": np.ones((1, 128), dtype=BT),
        "identb": np.eye(128, dtype=BT),
    }
    return out




TT0 = 256
ATW = 512


def emit_l0attn(nc, p, C):
    d_wq = nc.dram_tensor("l0a_wq", [32, 8, 128, 128], BF16, kind="ExternalInput").ap()
    d_wk = nc.dram_tensor("l0a_wk", [32, 8, 128, 128], BF16, kind="ExternalInput").ap()
    d_wv = nc.dram_tensor("l0a_wv", [32, 8, 128, 128], BF16, kind="ExternalInput").ap()
    d_wo = nc.dram_tensor("l0a_wo", [8, 128, 64, 128], BF16, kind="ExternalInput").ap()
    d_bo = nc.dram_tensor("l0a_bo", [128, 8], F32, kind="ExternalInput").ap()
    d_em = nc.dram_tensor("l0a_emat2", [128, 128], BF16, kind="ExternalInput").ap()

    X, IDB, ybf, stat = C["X"], C["IDB"], C["ybf"], C["stat"]
    epsb, pb, pbt, banks = C["epsb"], C["pb"], C["pbt"], C["banks"]
    abf, af32 = C["abf"], C["af32"]
    YTt = abf.take(DC * TT0)
    SQK = abf.take(64 * 128)
    VSL = abf.take(65 * 128)
    ATE = abf.take(256 * 64)
    WB = [abf.take(4 * 8 * 128), abf.take(4 * 8 * 128)]
    WOB = [abf.take(32 * 128), abf.take(32 * 128)]
    KSUM = af32.take(TT0)
    KSC = abf.take(TT0)
    QSC = abf.take(TT0)
    ATB = [abf.take(256), abf.take(256)]
    BOC = af32.take(8)
    EM2 = abf.take(128)
    rq = af32.take(16)

    p.dma("sync", EM2[:], d_em)
    p.dma("sync", BOC[:], d_bo)
    consts_tok = p.tok("sync")
    x_toks = C.get("x_toks")

    def load_w(dw, grp, buf):
        src = dw[grp * 4:(grp + 1) * 4].transpose([2, 0, 1, 3])
        p.dma("gpsimd", WB[buf][:].rearrange("p (a s f) -> p a s f", a=4, s=8), src)
        return p.tok("gpsimd")

    def load_wo(cc, jh, buf):
        p.dma("sync", WOB[buf][:].rearrange("p (j f) -> p j f", j=32),
              d_wo[cc][:, jh * 32:(jh + 1) * 32, :])
        return p.tok("sync")

    p.wait("vector", consts_tok)
    p.wait("tensor", consts_tok)
    p.wait("scalar", consts_tok)
    p.emit("vector", lambda E, o=VSL[:, 64 * 128:65 * 128]: E.memset(o, 1.0))
    p.vdrain()
    bufs = C["bufs"]

    ksc_war = [None]
    qsc_war = [None]
    wb_toks = [None, None]
    wb_pe_done = [None, None]
    wob_toks = [None, None]
    wob_pe_done = [None, None]
    gi = 0
    slab_pe_done = None
    ate_dve = None
    ksum_bank = 4

    for tile in range(T // TT0):
        r0 = tile * TT0
        chunks = [r0 // 128, r0 // 128 + 1]
        ln_done = emit_ln(p, X, YTt, bufs, x_toks, eps=1e-5, chunks=chunks, yt_T=TT0)
        p.wait("tensor", ln_done[-1])

        def qkv_path(dw, kind):
            nonlocal gi, slab_pe_done, ate_dve
            for grp in range(8):
                buf = gi % 2
                p.wait("gpsimd", wb_pe_done[buf])
                wb_toks[buf] = load_w(dw, grp, buf)
                p.wait("tensor", wb_toks[buf])
                for lc in range(4):
                    fc = grp * 4 + lc
                    bi = fc % 4
                    bank = banks.acquire(bi)
                    tmm = None
                    for dc in range(DC):
                        lhsT = WB[buf][:, (lc * 8 + dc) * 128:(lc * 8 + dc) * 128 + 128]
                        rhs = YTt[:, dc * TT0:(dc + 1) * TT0]
                        tmm = p.emit("tensor", lambda E, o=bank[:, 0:TT0], l=lhsT, r=rhs,
                                     st=(dc == 0), sp=(dc == DC - 1):
                                     E.matmul(o, l, r, start=st, stop=sp), inc=(dc == DC - 1))
                    if kind == 'k':
                        p.wait("scalar", tmm)
                        p.wait("scalar", ksc_war[0])
                        tev = p.emit("scalar", lambda E, o=KSC[:], i=bank[:, 0:TT0]:
                                     E.activation(o, i, AF.Relu), inc=True)
                        banks.release(bi, tev)
                        p.wait("tensor", tev)
                        kb = pb[ksum_bank]
                        p.emit("tensor", lambda E, o=kb[:, 0:TT0], l=EM2[:], r=KSC[:],
                               st=(fc == 0), sp=(fc == 31):
                               E.matmul(o, l, r, start=st, stop=sp), inc=True)
                        ksc_war[0] = p.tok("tensor")
                    elif kind == 'q':
                        p.wait("scalar", tmm)
                        p.wait("scalar", qsc_war[0])
                        tev = p.emit("scalar", lambda E, o=QSC[:], i=bank[:, 0:TT0]:
                                     E.activation(o, i, AF.Relu, scale=0.125), inc=True)
                        banks.release(bi, tev)
                        p.wait("vector", tev)
                        p.wait("vector", slab_pe_done)
                        for hh in range(2):
                            hd = 2 * fc + hh
                            for th in range(2):
                                tq = p.emit("vector", lambda E,
                                            o=SQK[th * 64:th * 64 + 64, hd * 128:(hd + 1) * 128],
                                            a=QSC[hh * 64:hh * 64 + 64, th * 128:(th + 1) * 128],
                                            b=KSUM[hh * 64:hh * 64 + 64, th * 128:(th + 1) * 128]:
                                            E.tensor_tensor(o, a, b, ALU.mult),
                                            inc=(hh == 1 and th == 1))
                        qsc_war[0] = p.tok("vector")
                    else:
                        p.wait("vector", tmm)
                        p.wait("vector", slab_pe_done)
                        for hh in range(2):
                            hd = 2 * fc + hh
                            for th in range(2):
                                tv = p.emit("vector", lambda E,
                                            o=VSL[th * 64:th * 64 + 64, hd * 128:(hd + 1) * 128],
                                            i=bank[hh * 64:hh * 64 + 64, th * 128:(th + 1) * 128]:
                                            E.tensor_copy(o, i), inc=(hh == 1 and th == 1))
                        banks.release(bi, p.tok("vector"))
                wb_pe_done[buf] = p.tok("tensor")
                gi += 1

        bank4 = banks.acquire(ksum_bank)
        qkv_path(d_wk, 'k')
        tks = p.tok("tensor")
        p.wait("vector", tks)
        tkev = p.emit("vector", lambda E, o=KSUM[:], i=pb[ksum_bank][:, 0:TT0]:
                      E.tensor_copy(o, i), inc=True)
        banks.release(ksum_bank, tkev)
        p.vdrain()
        qkv_path(d_wq, 'q')
        qkv_path(d_wv, 'v')
        slab_dve = p.tok("vector")
        p.vdrain()

        p.wait("tensor", slab_dve)
        sq_v = [SQK[h * 64:h * 64 + 64, :].rearrange("p (i t) -> p t i", t=128) for h in (0, 1)]
        vs_v = [VSL[h * 64:h * 64 + 64, :].rearrange("p (j t) -> p t j", t=128) for h in (0, 1)]
        npair = 128
        nbk = (npair + 6) // 7
        for bk in range(nbk):
            bi = bk % 4
            bank = banks.acquire(bi)
            pairs = range(bk * 7, min((bk + 1) * 7, npair))
            tmm = None
            for s, g in enumerate(pairs):
                for h in (0, 1):
                    tmm = p.emit("tensor", lambda E, o=bank[h * 64:h * 64 + 64, s * 65:s * 65 + 65],
                                 l=sq_v[h][:, g, :], r=vs_v[h][:, g, :],
                                 tp=(h * 64, h * 64):
                                 E.matmul(o, l, r, start=True, stop=True, tile_position=tp),
                                 inc=(h == 1 and s == len(list(pairs)) - 1))
            ns = len(list(pairs))
            p.wait("vector", tmm)
            qn_view = bank[:, 0:ns * 65].rearrange("p (s c) -> p s c", c=65)[:, :, 64]
            p.emit("vector", lambda E, o=rq[:, 0:ns], i=qn_view:
                   E.tensor_scalar(o, i, 1e-6, None, ALU.add))
            p.vdrain()
            p.emit("vector", lambda E, o=rq[:, 8:8 + ns], i=rq[:, 0:ns]: E.reciprocal(o, i))
            p.vdrain()
            tev = None
            for s, g in enumerate(pairs):
                dst = ATE[:, (tile % 2) * 8192 + g * 64: (tile % 2) * 8192 + g * 64 + 64]
                tev = p.emit("vector", lambda E, o=dst, i=bank[:, s * 65:s * 65 + 64],
                             r=rq[:, 8 + s:9 + s]:
                             E.tensor_scalar(o, i, r, None, ALU.mult),
                             inc=(s == len(list(pairs)) - 1))
            banks.release(bi, tev)
        slab_pe_done = p.tok("tensor")
        ate_dve = p.tok("vector")

        if tile % 2 == 1:
            p.wait("tensor", ate_dve)
            rr0 = (tile - 1) * TT0
            wi = 0
            for h in (0, 1):
                for cc in range(DC):
                    bi = 4 + h
                    bank = banks.acquire(bi)
                    tmm = None
                    for jh in (0, 1):
                        wbuf = wi % 2
                        p.wait("sync", wob_pe_done[wbuf])
                        wob_toks[wbuf] = load_wo(cc, jh, wbuf)
                        p.wait("tensor", wob_toks[wbuf])
                        for jj in range(32):
                            j = jh * 32 + jj
                            lhsT = WOB[wbuf][h * 64:h * 64 + 64, jj * 128:(jj + 1) * 128]
                            rhs = ATE[h * 64:h * 64 + 64, :].rearrange(
                                "p (g j) -> p j g", j=64)[:, j, :]
                            tmm = p.emit("tensor", lambda E, o=bank[:, 0:256], l=lhsT, r=rhs,
                                         st=(j == 0), sp=(j == 63):
                                         E.matmul(o, l, r, start=st, stop=sp), inc=(jj == 31))
                        wob_pe_done[wbuf] = p.tok("tensor")
                        wi += 1
                    p.wait("scalar", tmm)
                    tab = p.emit("scalar", lambda E, o=ATB[h][:], i=bank[:, 0:256],
                                 b=BOC[:, cc:cc + 1]:
                                 E.activation(o, i, AF.Identity, bias=b), inc=True)
                    banks.release(bi, tab)
                    p.wait("tensor", tab)
                    for gb in range(2):
                        gchunk = (rr0 + h * 128 + gb * 256) // 128
                        slot = gb
                        p.wait("tensor", bufs["_ln_evac"][slot])
                        tpe = p.emit("tensor", lambda E, o=pbt[slot][:, 0:128],
                                     i=ATB[h][:, gb * 128:(gb + 1) * 128], idn=IDB[:]:
                                     E.transpose(o, i, idn), inc=True)
                        p.wait("vector", tpe)
                        bufs["_ln_evac"][slot] = p.emit(
                            "vector", lambda E, o=X[:, gchunk * D + cc * 128: gchunk * D + cc * 128 + 128],
                            i=pbt[slot][:, 0:128]:
                            E.tensor_tensor(o, o, i, ALU.add), inc=True)



def prep_l0attn(inputs):
    wq = np.asarray(inputs["L0_attn_Wq"])
    wk = np.asarray(inputs["L0_attn_Wk"])
    wv = np.asarray(inputs["L0_attn_Wv"])
    wo = np.asarray(inputs["L0_attn_Wo"])
    bo = np.asarray(inputs["L0_attn_bo"])

    def chunk_qkv(w):
        return np.ascontiguousarray(
            w.reshape(8, 128, 32, 128).transpose(2, 0, 1, 3)).astype(BT)

    wop = wo.reshape(64, 64, 1024)
    wop = np.concatenate([wop, wop], axis=0)
    wopc = np.ascontiguousarray(
        wop.reshape(128, 64, 8, 128).transpose(2, 0, 1, 3)).astype(BT)
    em2 = np.fromfunction(lambda p_, m: (p_ % 64 == m % 64), (128, 128)).astype(BT)
    return {
        "l0a_wq": chunk_qkv(wq),
        "l0a_wk": chunk_qkv(wk),
        "l0a_wv": chunk_qkv(wv),
        "l0a_wo": wopc,
        "l0a_bo": np.ascontiguousarray(bo.reshape(8, 128).T, dtype=np.float32),
        "l0a_emat2": em2,
        "identb": np.eye(128, dtype=BT),
    }




NHP_L1A = 8
NHH_L1A = 2
STG_L1A = 10


def emit_l1attn(nc, p, C):
    d_wqkv = nc.dram_tensor("l1a_wqkv", [8, 3, 8, 128, 128], BF16, kind="ExternalInput").ap()
    d_wpj = nc.dram_tensor("l1a_wpj", [8, 128, 1024], BF16, kind="ExternalInput").ap()
    d_i7 = nc.dram_tensor("l1a_i7", [64, 3 * 64], F32, kind="ExternalInput").ap()

    X, IDB, IDF, ybf, stat = C["X"], C["IDB"], C["IDF"], C["ybf"], C["stat"]
    epsb, pb, pbt, banks = C["epsb"], C["pb"], C["pbt"], C["banks"]
    abf, af32 = C["abf"], C["af32"]
    YT = abf.take(DC * T)
    SVT = abf.take(DC * T)
    QKV = abf.take(3 * T)
    WQB = [abf.take(24 * 128)]
    WPJ = abf.take(DC * 1024)
    S3s = af32.take(T)[0:64, :]
    E3 = abf.take(T)[0:64, :]
    E3T = abf.take(16 * 64)
    VTM = abf.take(16 * 64)
    KN1T = abf.take(T)[0:64, :]
    WTs = abf.take(T)[0:64, :]
    KN1C = abf.take(64)
    SM = af32.take(3 * 64)
    QLT2 = abf.take(64)
    KLT2 = abf.take(64)
    _sm = af32.take(1056)
    K2 = _sm[0:64, 0:64]
    K2T = _sm[0:64, 64:128]
    Vcur = _sm[0:64, 128:192]
    VcurT = _sm[0:64, 192:256]
    KVs = _sm[0:64, 256:320]
    KVT = _sm[0:64, 320:384]
    T1 = _sm[0:64, 384:448]
    T2 = _sm[0:64, 448:512]
    T3 = _sm[0:64, 512:576]
    Vn = _sm[0:64, 576:640]
    I7 = _sm[0:64, 640:832]
    ONE64 = _sm[0:64, 832:833]
    ONER = _sm[0:1, 833:897]
    sden = _sm[0:64, 897:901]
    srow = _sm[0:1, 901:981]
    sm2 = _sm[:, 981:989]
    VnT = _sm[0:64, 989:1053]
    _zb = abf.take(128)
    Zb = _zb[0:64, 0:64]
    Bn = _zb[0:64, 64:128]

    p.dma("sync", I7[:], d_i7)
    p.dma("sync", WPJ[:].rearrange("p (s f) -> p s f", s=8), d_wpj.transpose([1, 0, 2]))
    consts_tok = p.tok("sync")
    x_toks = C.get("x_toks")

    def load_wq(hp):
        src = d_wqkv[hp].transpose([2, 0, 1, 3])
        p.dma("gpsimd", WQB[0][:].rearrange("p (a s f) -> p a s f", a=3, s=8), src)
        return p.tok("gpsimd")

    wq_tok = load_wq(0)

    p.wait("vector", consts_tok)
    p.wait("tensor", consts_tok)
    p.wait("scalar", consts_tok)
    p.emit("vector", lambda E, o=ONE64[:]: E.memset(o, 1.0))
    p.emit("vector", lambda E, o=ONER[:]: E.memset(o, 1.0))
    p.vdrain()
    bufs = C["bufs"]

    ln_done = emit_ln(p, X, YT, bufs, x_toks, eps=1e-5)
    p.wait("tensor", ln_done[-1])

    def mm64(bi, lhsT, rhs, n, extra=None, fp=False):
        bank = banks.acquire(bi)
        tmm = p.emit("tensor", lambda E, o=bank[0:lhsT.shape[1], 0:n], l=lhsT, r=rhs:
                     E.matmul(o, l, r, start=True, stop=True), inc=True)
        return bank, tmm

    qkv_dve = None
    for hp in range(NHP_L1A):
        buf = 0
        p.wait("tensor", wq_tok)
        for slot in range(3):
            for ts in range(4):
                bi = (slot * 4 + ts) % 4
                bank = banks.acquire(bi)
                tmm = None
                for dc in range(DC):
                    lhsT = WQB[buf][:, (slot * 8 + dc) * 128:(slot * 8 + dc) * 128 + 128]
                    rhs = YT[:, dc * T + ts * 512: dc * T + (ts + 1) * 512]
                    tmm = p.emit("tensor", lambda E, o=bank[:], l=lhsT, r=rhs,
                                 st=(dc == 0), sp=(dc == DC - 1):
                                 E.matmul(o, l, r, start=st, stop=sp), inc=(dc == DC - 1))
                dst = QKV[:, slot * T + ts * 512: slot * T + (ts + 1) * 512]
                if slot == 0:
                    p.wait("scalar", tmm)
                    p.wait("scalar", qkv_dve)
                    tev = p.emit("scalar", lambda E, o=dst, i=bank[:]:
                                 E.activation(o, i, AF.Identity, scale=0.125), inc=True)
                    eng = "scalar"
                else:
                    p.wait("vector", tmm)
                    p.wait("vector", qkv_dve)
                    tev = p.emit("vector", lambda E, o=dst, i=bank[:]:
                                 E.tensor_copy(o, i), inc=True)
                    eng = "vector"
                banks.release(bi, tev)
        qkv_done_v = p.tok("vector")
        qkv_done_s = p.tok("scalar")
        if hp < NHP_L1A - 1:
            p.wait("gpsimd", p.tok("tensor"))
            wq_tok = load_wq(hp + 1)

        for hh in range(NHH_L1A):
            h = 2 * hp + hh
            Qc = QKV[hh * 64:hh * 64 + 64, 0:T]
            Kc = QKV[hh * 64:hh * 64 + 64, T:2 * T]
            Vc = QKV[hh * 64:hh * 64 + 64, 2 * T:3 * T]
            QLT = QLT2[hh * 64:hh * 64 + 64, :]
            KLT = KLT2[hh * 64:hh * 64 + 64, :]
            p.wait("vector", qkv_done_v)
            p.wait("vector", qkv_done_s)
            p.wait("vector", p.tok("scalar"))
            p.emit("vector", lambda E, o=SM[0:64, 0:64], i=Qc.rearrange("p (m s) -> p m s", s=32):
                   E.tensor_reduce(o, i, AX.X, ALU.add))
            p.emit("vector", lambda E, o=SM[0:64, 64:128], i=Kc.rearrange("p (m s) -> p m s", s=32):
                   E.tensor_reduce(o, i, AX.X, ALU.add))
            p.vdrain()
            p.emit("vector", lambda E, o=QLT, i=SM[0:64, 0:64]:
                   E.tensor_scalar(o, i, 1.0 / 32, None, ALU.mult))
            tql = p.emit("vector", lambda E, o=KLT, i=SM[0:64, 64:128]:
                         E.tensor_scalar(o, i, 1.0 / 32, None, ALU.mult), inc=True)
            p.vdrain()
            if STG_L1A < 2:
                continue
            p.wait("tensor", tql)
            bank, tmm = mm64(4, QLT, KLT, 64)
            p.wait("vector", tmm)
            tng = p.emit("vector", lambda E, o=sden[:, 0:1], i=bank[0:64, 0:64]:
                         E.reduce_max(o, i, AX.X, negate=True), inc=True)
            p.vdrain()
            p.wait("scalar", tng)
            tex = p.emit("scalar", lambda E, o=K2[:], i=bank[0:64, 0:64], b=sden[:, 0:1],
                         a=sden[:, 1:2]:
                         E.activation(o, i, AF.Exp, bias=b, accum_out=a), inc=True)
            p.wait("vector", tex)
            banks.release(4, tex)
            p.emit("vector", lambda E, o=sden[:, 2:3], i=sden[:, 1:2]: E.reciprocal(o, i))
            p.vdrain()
            tk2 = p.emit("vector", lambda E, o=K2[:], i=K2[:], r=sden[:, 2:3]:
                         E.tensor_scalar(o, i, r, None, ALU.mult), inc=True)
            p.vdrain()
            p.wait("tensor", tk2)
            bank5 = banks.acquire(5)
            ttr = p.emit("tensor", lambda E, o=bank5[0:64, 0:64], i=K2[:], idn=IDF[0:64, 0:64]:
                         E.transpose(o, i, idn), inc=True)
            p.wait("vector", ttr)
            tk2t = p.emit("vector", lambda E, o=K2T[:], i=bank5[0:64, 0:64]:
                          E.tensor_copy(o, i), inc=True)
            banks.release(5, tk2t)
            if STG_L1A < 3:
                continue
            p.wait("tensor", tk2t)
            bank, tmm = mm64(4, K2[:], ONE64[:], 1)
            p.wait("vector", tmm)
            tcs = p.emit("vector", lambda E, o=sden[:, 3:4], i=bank[0:64, 0:1]:
                         E.tensor_copy(o, i), inc=True)
            banks.release(4, tcs)
            p.wait("tensor", tcs)
            bank5 = banks.acquire(5)
            ttr = p.emit("tensor", lambda E, o=bank5[0:1, 0:64], i=sden[:, 3:4],
                         idn=IDF[0:64, 0:64]: E.transpose(o, i, idn), inc=True)
            p.wait("vector", ttr)
            trow = p.emit("vector", lambda E, o=srow[:, 0:64], i=bank5[0:1, 0:64]:
                          E.tensor_copy(o, i), inc=True)
            banks.release(5, trow)
            p.vdrain()
            p.emit("vector", lambda E, o=srow[:, 64:65], i=srow[:, 0:64]:
                   E.reduce_max(o, i, AX.X))
            p.vdrain()
            trcp = p.emit("vector", lambda E, o=srow[:, 65:66], i=srow[:, 64:65]:
                          E.reciprocal(o, i), inc=True)
            p.wait("tensor", trcp)
            bank, tmm = mm64(4, ONER[:], srow[:, 65:66], 1)
            p.wait("vector", tmm)
            tden = p.emit("vector", lambda E, o=sden[:, 2:3], i=bank[0:64, 0:1]:
                          E.tensor_copy(o, i), inc=True)
            banks.release(4, tden)
            p.vdrain()
            p.emit("vector", lambda E, o=Vcur[:], i=K2T[:], r=sden[:, 2:3]:
                   E.tensor_scalar(o, i, r, None, ALU.mult))
            tv0 = p.emit("vector", lambda E, o=VcurT[:], i=K2[:], r=sden[:, 2:3]:
                         E.tensor_scalar(o, i, r, None, ALU.mult), inc=True)
            p.vdrain()
            if STG_L1A < 4:
                continue
            tprev = tv0
            for it in range(6):
                p.wait("tensor", tprev)
                bank, tmm = mm64(4, K2T[:], Vcur[:], 64)
                p.wait("vector", tmm)
                tkv = p.emit("vector", lambda E, o=KVs[:], i=bank[0:64, 0:64]:
                             E.tensor_copy(o, i), inc=True)
                banks.release(4, tkv)
                p.vdrain()
                t1t = p.emit("vector", lambda E, o=T1[:], i=KVs[:], s7=I7[:, 0:64]:
                             E.scalar_tensor_tensor(o, i, -1.0, s7, ALU.mult, ALU.add),
                             inc=True)
                p.wait("tensor", t1t)
                bank5 = banks.acquire(5)
                ttr = p.emit("tensor", lambda E, o=bank5[0:64, 0:64], i=KVs[:],
                             idn=IDF[0:64, 0:64]: E.transpose(o, i, idn), inc=True)
                p.wait("vector", ttr)
                tkvt = p.emit("vector", lambda E, o=KVT[:], i=bank5[0:64, 0:64]:
                              E.tensor_copy(o, i), inc=True)
                banks.release(5, tkvt)
                p.wait("tensor", tkvt)
                bank, tmm = mm64(4, KVT[:], T1[:], 64)
                p.wait("vector", tmm)
                tm2 = p.emit("vector", lambda E, o=T2[:], i=bank[0:64, 0:64],
                             s15=I7[:, 64:128]:
                             E.scalar_tensor_tensor(o, i, -1.0, s15, ALU.mult, ALU.add),
                             inc=True)
                banks.release(4, tm2)
                p.vdrain()
                p.wait("tensor", tm2)
                bank, tmm = mm64(4, KVT[:], T2[:], 64)
                p.wait("vector", tmm)
                tm3 = p.emit("vector", lambda E, o=T3[:], i=bank[0:64, 0:64],
                             s13=I7[:, 128:192]:
                             E.scalar_tensor_tensor(o, i, -1.0, s13, ALU.mult, ALU.add),
                             inc=True)
                banks.release(4, tm3)
                p.vdrain()
                p.wait("tensor", tm3)
                bank, tmm = mm64(4, VcurT[:], T3[:], 64)
                bank5, tmm2 = mm64(5, T3[:], VcurT[:], 64)
                p.wait("vector", tmm)
                p.wait("vector", tmm2)
                p.emit("vector", lambda E, o=Vn[:], i=bank[0:64, 0:64]:
                       E.tensor_scalar(o, i, 0.25, None, ALU.mult))
                tvn = p.emit("vector", lambda E, o=VnT[:], i=bank5[0:64, 0:64]:
                             E.tensor_scalar(o, i, 0.25, None, ALU.mult), inc=True)
                banks.release(4, tvn)
                banks.release(5, tvn)
                p.vdrain()
                p.emit("vector", lambda E, o=Vcur[:], i=Vn[:]: E.tensor_copy(o, i))
                tprev = p.emit("vector", lambda E, o=VcurT[:], i=VnT[:]:
                               E.tensor_copy(o, i), inc=True)
                p.vdrain()
            tzb = p.emit("vector", lambda E, o=Zb[:], i=Vcur[:]: E.tensor_copy(o, i),
                         inc=True)
            p.vdrain()

            if STG_L1A < 5:
                continue
            tex = None
            for tc in range(NCH):
                bi = tc % 2
                bank = banks.acquire(bi)
                lhsT = Qc[:, tc * 128:(tc + 1) * 128]
                tmm = p.emit("tensor", lambda E, o=bank[:, 0:64], l=lhsT, r=KLT:
                             E.matmul(o, l, r, start=True, stop=True), inc=True)
                p.wait("vector", tmm)
                p.wait("vector", tex)
                tng = p.emit("vector", lambda E, o=sm2[:, 0:1], i=bank[:, 0:64]:
                             E.reduce_max(o, i, AX.X, negate=True), inc=True)
                p.vdrain()
                p.wait("scalar", tng)
                tex = p.emit("scalar", lambda E, o=KN1C[:], i=bank[:, 0:64],
                             b=sm2[:, 0:1], a=sm2[:, 1:2]:
                             E.activation(o, i, AF.Exp, bias=b, accum_out=a), inc=True)
                banks.release(bi, tex)
                p.wait("vector", tex)
                p.emit("vector", lambda E, o=sm2[:, 2:3], i=sm2[:, 1:2]: E.reciprocal(o, i))
                p.vdrain()
                tkn = p.emit("vector", lambda E, o=KN1C[:], i=KN1C[:], r=sm2[:, 2:3]:
                             E.tensor_scalar(o, i, r, None, ALU.mult), inc=True)
                p.wait("tensor", tkn)
                slot = tc % 2
                p.wait("tensor", bufs["_ln_evac"][slot])
                tpe = p.emit("tensor", lambda E, o=pbt[slot][0:64, 0:128], i=KN1C[:],
                             idn=IDB[:]: E.transpose(o, i, idn), inc=True)
                p.wait("vector", tpe)
                bufs["_ln_evac"][slot] = p.emit(
                    "vector", lambda E, o=KN1T[:, tc * 128:(tc + 1) * 128],
                    i=pbt[slot][0:64, 0:128]: E.tensor_copy(o, i), inc=True)
            kn1_done = p.tok("vector")

            if STG_L1A < 6:
                continue
            for ts in range(4):
                bi = 2 + ts % 2
                bank = banks.acquire(bi)
                tmm = p.emit("tensor", lambda E, o=bank[0:64, 0:512], l=QLT,
                             r=Kc[:, ts * 512:(ts + 1) * 512]:
                             E.matmul(o, l, r, start=True, stop=True), inc=True)
                p.wait("vector", tmm)
                tcp = p.emit("vector", lambda E, o=S3s[:, ts * 512:(ts + 1) * 512],
                             i=bank[0:64, 0:512]: E.tensor_copy(o, i), inc=True)
                banks.release(bi, tcp)
            p.vdrain()
            p.wait("vector", p.tok("scalar"))
            tng = p.emit("vector", lambda E, o=sden[:, 0:1], i=S3s[:]:
                         E.reduce_max(o, i, AX.X, negate=True), inc=True)
            p.vdrain()
            p.wait("scalar", tng)
            tex = p.emit("scalar", lambda E, o=E3[:], i=S3s[:], b=sden[:, 0:1],
                         a=sden[:, 1:2]: E.activation(o, i, AF.Exp, bias=b,
                                                      accum_out=a), inc=True)
            p.wait("vector", tex)
            p.emit("vector", lambda E, o=sden[:, 2:3], i=sden[:, 1:2]: E.reciprocal(o, i))
            p.vdrain()
            if STG_L1A < 7:
                continue
            p.wait("tensor", tex)
            idv = IDB[0:64, 0:64] if hh == 0 else IDB[64:128, 64:128]
            for tc in range(NCH):
                slot = tc % 2
                p.wait("tensor", bufs["_ln_evac"][slot])
                tpe = p.emit("tensor", lambda E, o=pbt[slot][:, 0:64],
                             i=E3[:, tc * 128:(tc + 1) * 128], idn=IDB[0:64, 0:64]:
                             E.transpose(o, i, idn), inc=True)
                p.wait("vector", tpe)
                bufs["_ln_evac"][slot] = p.emit(
                    "vector", lambda E, o=E3T[:, tc * 64:(tc + 1) * 64],
                    i=pbt[slot][:, 0:64]: E.tensor_copy(o, i), inc=True)
                p.wait("tensor", bufs["_ln_evac"][slot])
                tpe = p.emit("tensor", lambda E, o=pbt[slot][:, 0:64],
                             i=Vc[:, tc * 128:(tc + 1) * 128], idn=idv:
                             E.transpose(o, i, idn), inc=True)
                p.wait("vector", tpe)
                bufs["_ln_evac"][slot] = p.emit(
                    "vector", lambda E, o=VTM[:, tc * 64:(tc + 1) * 64],
                    i=pbt[slot][:, 0:64]: E.tensor_copy(o, i), inc=True)
            tev3 = p.tok("vector")

            if STG_L1A < 8:
                continue
            p.wait("tensor", tev3)
            bank = banks.acquire(4)
            tmm = None
            for tc in range(NCH):
                tmm = p.emit("tensor", lambda E, o=bank[0:64, 0:64],
                             l=E3T[:, tc * 64:(tc + 1) * 64],
                             r=VTM[:, tc * 64:(tc + 1) * 64],
                             st=(tc == 0), sp=(tc == NCH - 1):
                             E.matmul(o, l, r, start=st, stop=sp), inc=(tc == NCH - 1))
            p.wait("vector", tmm)
            tb = p.emit("vector", lambda E, o=Bn[:], i=bank[0:64, 0:64], r=sden[:, 2:3]:
                        E.tensor_scalar(o, i, r, None, ALU.mult), inc=True)
            banks.release(4, tb)
            p.vdrain()

            if STG_L1A < 9:
                continue
            p.wait("tensor", tzb)
            p.wait("tensor", kn1_done)
            p.wait("tensor", tb)
            for ts in range(4):
                bi = 2 + ts % 2
                bank = banks.acquire(bi)
                tmm = p.emit("tensor", lambda E, o=bank[0:64, 0:512], l=Zb[:],
                             r=KN1T[:, ts * 512:(ts + 1) * 512]:
                             E.matmul(o, l, r, start=True, stop=True), inc=True)
                p.wait("vector", tmm)
                tcp = p.emit("vector", lambda E, o=WTs[:, ts * 512:(ts + 1) * 512],
                             i=bank[0:64, 0:512]: E.tensor_copy(o, i), inc=True)
                banks.release(bi, tcp)
            wt_done = p.tok("vector")
            if STG_L1A < 10:
                continue
            p.wait("tensor", wt_done)
            for ts in range(4):
                bi = 2 + ts % 2
                bank = banks.acquire(bi)
                tmm = p.emit("tensor", lambda E, o=bank[0:64, 0:512], l=Bn[:],
                             r=WTs[:, ts * 512:(ts + 1) * 512]:
                             E.matmul(o, l, r, start=True, stop=True), inc=True)
                dst = SVT[hh * 64:hh * 64 + 64, hp * T + ts * 512: hp * T + (ts + 1) * 512]
                p.wait("vector", tmm)
                tcp = p.emit("vector", lambda E, o=dst, i=bank[0:64, 0:512]:
                             E.tensor_copy(o, i), inc=True)
                banks.release(bi, tcp)
        qkv_dve = p.tok("vector")

    svt_done = p.tok("vector")
    p.wait("tensor", svt_done)
    for tc in range(NCH):
        for oh in range(2):
            bi = (tc * 2 + oh) % 4
            bank = banks.acquire(bi)
            tmm = None
            for dc in range(DC):
                lhsT = SVT[:, dc * T + tc * 128: dc * T + tc * 128 + 128]
                rhs = WPJ[:, dc * 1024 + oh * 512: dc * 1024 + (oh + 1) * 512]
                tmm = p.emit("tensor", lambda E, o=bank[:], l=lhsT, r=rhs,
                             st=(dc == 0), sp=(dc == DC - 1):
                             E.matmul(o, l, r, start=st, stop=sp), inc=(dc == DC - 1))
            xs = X[:, tc * D + oh * 512: tc * D + (oh + 1) * 512]
            p.wait("vector", tmm)
            tev = p.emit("vector", lambda E, o=xs, i=bank[:]:
                         E.tensor_add(o, o, i), inc=True)
            banks.release(bi, tev)



def prep_l1attn(inputs):
    wqkv = np.asarray(inputs["L1_attn_w_qkv"])
    wpj = np.asarray(inputs["L1_attn_w_proj"])
    wq = wqkv.reshape(8, 128, 3, 8, 128)
    wq = np.ascontiguousarray(wq.transpose(3, 2, 0, 1, 4)).astype(BT)
    i7 = np.concatenate([7 * np.eye(64), 15 * np.eye(64), 13 * np.eye(64)],
                        axis=1).astype(np.float32)
    return {
        "l1a_wqkv": wq,
        "l1a_wpj": np.ascontiguousarray(wpj.reshape(8, 128, 1024)).astype(BT),
        "l1a_i7": i7,
        "identb": np.eye(128, dtype=BT),
        "identf": np.eye(128, dtype=np.float32),
    }



NEXP_L1F = 16
STG_L1F = 3


def emit_l1ffn(nc, p, C):
    d_gate = nc.dram_tensor("l1f_gate", [D, 16], BF16, kind="ExternalInput").ap()
    d_w1 = nc.dram_tensor("l1f_w1", [16, 2, 4, 128, 2048], BF16, kind="ExternalInput").ap()
    d_w2 = nc.dram_tensor("l1f_w2", [16, 2, 8, 128, 1024], BF16, kind="ExternalInput").ap()
    d_b1 = nc.dram_tensor("l1f_b1c", [128, 256], F32, kind="ExternalInput").ap()
    d_b2 = nc.dram_tensor("l1f_b2", [16, 1024], F32, kind="ExternalInput").ap()

    X, IDB, IDF, ybf, stat = C["X"], C["IDB"], C["IDF"], C["ybf"], C["stat"]
    epsb, pb, pbt, banks = C["epsb"], C["pb"], C["pbt"], C["banks"]
    abf, af32 = C["abf"], C["af32"]
    YT = abf.take(DC * T)
    HID = abf.take(16 * 512)
    W1buf = [abf.take(4 * 2048), abf.take(4 * 2048)]
    W2buf = [abf.take(8 * 1024), abf.take(8 * 1024)]
    GATE = abf.take(DC * 16)
    B1C = af32.take(256)
    B2 = af32.take(1024)[0:16, :]
    GT = af32.take(NCH * 16)
    GTT = af32.take(2 * 128)[0:16, :]
    gsc = af32.take(NCH * 16)
    gm = af32.take(16)

    for ap, dr in ((B1C[:], d_b1), (B2[:], d_b2)):
        p.dma("sync", ap, dr)
    for dc in range(DC):
        p.dma("sync", GATE[:, dc * 16:(dc + 1) * 16], d_gate[dc * 128:(dc + 1) * 128, :])
    consts_tok = p.tok("sync")
    x_toks = C.get("x_toks")

    def load_w1(ei):
        for h in range(2):
            src = d_w1[ei, h].transpose([1, 0, 2])
            p.dma("gpsimd", W1buf[h][:].rearrange("p (s f) -> p s f", s=4), src)
        return p.tok("gpsimd")

    def load_w2(ei):
        for h in range(2):
            src = d_w2[ei, h].transpose([1, 0, 2])
            p.dma("gpsimd", W2buf[h][:].rearrange("p (s f) -> p s f", s=8), src)
        return p.tok("gpsimd")

    w1_tok = load_w1(0)
    w2_tok = load_w2(0)

    p.wait("vector", consts_tok)
    p.wait("tensor", consts_tok)
    p.wait("scalar", consts_tok)
    lnbufs = C["bufs"]
    ln_done = emit_ln(p, X, YT, lnbufs, x_toks, eps=1e-5)

    gtt_war = [None, None]
    for c in (range(NCH) if STG_L1F >= 2 else []):
        bi = c % 2
        bank = banks.acquire(bi)
        p.wait("tensor", ln_done[c])
        tpe = None
        for dc in range(DC):
            lhsT = YT[:, dc * T + c * 128: dc * T + c * 128 + 128]
            rhs = GATE[:, dc * 16:(dc + 1) * 16]
            tpe = p.emit("tensor", lambda E, o=bank[:, 0:16], l=lhsT, r=rhs,
                         st=(dc == 0), sp=(dc == DC - 1):
                         E.matmul(o, l, r, start=st, stop=sp),
                         inc=(dc == DC - 1))
        gs = gsc[:, c * 16:(c + 1) * 16]
        g8 = gs[:, 0:8]
        gtc = GT[:, c * 16:(c + 1) * 16]
        p.wait("vector", tpe)
        p.emit("vector", lambda E, o=g8, i=bank[:, 0:16]: E.max(o, i))
        p.vdrain()
        p.emit("vector", lambda E, o=gs[:, 8:9], i=gs[:, 0:1]:
               E.tensor_scalar(o, i, -1.0, None, ALU.mult))
        tmask = p.emit("vector", lambda E, o=gm[:, 0:16], i=bank[:, 0:16], m3=gs[:, 2:3]:
                       E.tensor_scalar(o, i, m3, None, ALU.is_ge), inc=True)
        p.vdrain()
        p.wait("scalar", tmask)
        texp = p.emit("scalar", lambda E, o=gtc, i=bank[:, 0:16], b=gs[:, 8:9]:
                      E.activation(o, i, AF.Exp, bias=b), inc=True)
        banks.release(bi, texp)
        p.wait("vector", texp)
        p.emit("vector", lambda E, o=gtc, i=gtc, m=gm[:, 0:16], s=gs[:, 9:10]:
               E.scalar_tensor_tensor(o, i, 1.0, m, ALU.mult, ALU.mult, accum_out=s))
        p.vdrain()
        p.emit("vector", lambda E, o=gs[:, 10:11], i=gs[:, 9:10]: E.reciprocal(o, i))
        p.vdrain()
        tgt = p.emit("vector", lambda E, o=gtc, i=gtc, r=gs[:, 10:11]:
                     E.tensor_scalar(o, i, r, None, ALU.mult), inc=True)
        if STG_L1F < 3:
            continue
        b2i = 2 + (c % 2)
        bank2 = banks.acquire(b2i)
        gslot = GTT[:, (c % 2) * 128:(c % 2) * 128 + 128]
        p.wait("tensor", tgt)
        ttr = p.emit("tensor", lambda E, o=bank2[0:16, 0:128], i=gtc, idn=IDF[:]:
                     E.transpose(o, i, idn), inc=True)
        p.wait("vector", ttr)
        p.wait("vector", gtt_war[c % 2])
        tcp = p.emit("vector", lambda E, o=gslot,
                     i=bank2[0:16, 0:128]: E.tensor_copy(o, i), inc=True)
        p.wait("tensor", tcp)
        tfree = None
        for oh in range(2):
            p.wait("tensor", tfree)
            tmm = p.emit("tensor", lambda E, o=bank2[:, 0:512], l=gslot,
                         r=B2[:, oh * 512:(oh + 1) * 512]:
                         E.matmul(o, l, r, start=True, stop=True), inc=True)
            p.wait("vector", tmm)
            tfree = p.emit("vector", lambda E, o=X[:, c * D + oh * 512: c * D + (oh + 1) * 512],
                           a=bank2[:, 0:512]: E.tensor_add(o, o, a), inc=True)
        banks.release(b2i, tfree)
        gtt_war[c % 2] = p.tok("tensor")
    p.vdrain()
    init_done_dve = p.tok("vector")

    hid_free_pe = None
    evac_done = {}
    for ei in range(NEXP_L1F):
        p.wait("tensor", w1_tok)
        mm2_last_pe = None
        for th in range(4):
            g = 0
            hid_evac_act = None
            hid_evac_dve = None
            for hc in range(16):
                bi = g % 3
                bank = banks.acquire(bi)
                tmm = None
                for dc in range(DC):
                    lhsT = W1buf[dc // 4][:, (dc % 4) * 2048 + hc * 128: (dc % 4) * 2048 + hc * 128 + 128]
                    rhs = YT[:, dc * T + th * 512: dc * T + (th + 1) * 512]
                    tmm = p.emit("tensor", lambda E, o=bank[:], l=lhsT, r=rhs,
                                 st=(dc == 0), sp=(dc == DC - 1):
                                 E.matmul(o, l, r, start=st, stop=sp),
                                 inc=(dc == DC - 1))
                dst = HID[:, hc * 512:(hc + 1) * 512]
                bcol = B1C[:, ei * 16 + hc: ei * 16 + hc + 1]
                eng = "scalar" if bi < 2 else "vector"
                p.wait(eng, tmm)
                p.wait(eng, hid_free_pe)
                if eng == "scalar":
                    tev = p.emit("scalar", lambda E, o=dst, i=bank[:], b=bcol:
                                 E.activation(o, i, AF.Relu, bias=b), inc=True)
                    hid_evac_act = tev
                else:
                    tev = p.emit("vector", lambda E, o=dst, i=bank[:], b=bcol:
                                 E.tensor_scalar(o, i, b, 0.0, ALU.add, ALU.max), inc=True)
                    hid_evac_dve = tev
                banks.release(bi, tev)
                g += 1
            if th == 3 and ei < NEXP_L1F - 1:
                mm1_pe_done = p.tok("tensor")
                p.wait("gpsimd", mm1_pe_done)
                w1_next = load_w1(ei + 1)
            if th == 0:
                p.wait("tensor", w2_tok)
            p.wait("tensor", hid_evac_act)
            p.wait("tensor", hid_evac_dve)
            g = 0
            for tc in range(4):
                gc = th * 4 + tc
                for oh in range(2):
                    bi = 3 + (g % 3)
                    bank = banks.acquire(bi)
                    tmm = None
                    for hc in range(16):
                        lhsT = HID[:, hc * 512 + tc * 128: hc * 512 + tc * 128 + 128]
                        rhs = W2buf[hc // 8][:, (hc % 8) * 1024 + oh * 512: (hc % 8) * 1024 + (oh + 1) * 512]
                        tmm = p.emit("tensor", lambda E, o=bank[:], l=lhsT, r=rhs,
                                     st=(hc == 0), sp=(hc == 15):
                                     E.matmul(o, l, r, start=st, stop=sp),
                                     inc=(hc == 15))
                    Xs = X[:, gc * D + oh * 512: gc * D + (oh + 1) * 512]
                    gcol = GT[:, gc * 16 + ei: gc * 16 + ei + 1]
                    p.wait("vector", tmm)
                    tev = p.emit("vector", lambda E, o=Xs, i=bank[:], s=gcol:
                                 E.scalar_tensor_tensor(o, i, s, o, ALU.mult, ALU.add),
                                 inc=True)
                    banks.release(bi, tev)
                    g += 1
            mm2_last_pe = p.tok("tensor")
            hid_free_pe = mm2_last_pe
        if ei < NEXP_L1F - 1:
            p.wait("gpsimd", mm2_last_pe)
            w2_tok = load_w2(ei + 1)
            w1_tok = w1_next




def prep_l1ffn(inputs):
    w1 = np.asarray(inputs["L1_ffn_W1"])
    w2 = np.asarray(inputs["L1_ffn_W2"])
    b1 = np.asarray(inputs["L1_ffn_b1"])
    b2 = np.asarray(inputs["L1_ffn_b2"])
    gate = np.asarray(inputs["L1_ffn_gate"])
    return {
        "l1f_gate": gate.astype(BT),
        "l1f_w1": np.ascontiguousarray(w1.reshape(16, 2, 4, 128, 2048)).astype(BT),
        "l1f_w2": np.ascontiguousarray(w2.reshape(16, 2, 8, 128, 1024)).astype(BT),
        "l1f_b1c": np.ascontiguousarray(
            b1.reshape(16, 16, 128).transpose(2, 0, 1).reshape(128, 256)).astype(np.float32),
        "l1f_b2": b2.astype(np.float32),
        "identb": np.eye(128, dtype=BT),
        "identf": np.eye(128, dtype=np.float32),
    }


def build_nc(build_fn):
    nc = bass.Bass("TRN2", target_bir_lowering=False)
    with ExitStack() as ctx:
        build_fn(nc, ctx)
    return nc


def run_launch(nc, prep, x_percore, n_cores=8):
    in_maps = []
    for c in range(n_cores):
        m = dict(prep)
        m["x_in"] = np.ascontiguousarray(x_percore[c], dtype=np.float32)
        in_maps.append(m)
    res = run_bass_kernel_spmd(nc, in_maps, core_ids=list(range(n_cores)))
    return np.stack([r["x_out"] for r in res.results])


def _standalone(emit_fn):
    def build(nc, ctx):
        p = Prog(nc, ctx)
        C = make_common(nc, ctx, p)
        d_x = nc.dram_tensor("x_in", [T, D], F32, kind="ExternalInput").ap()
        d_out = nc.dram_tensor("x_out", [T, D], F32, kind="ExternalOutput").ap()
        C["x_toks"] = emit_load_x(p, C["X"], d_x)
        emit_fn(nc, p, C)
        emit_store_x(p, d_out, C["X"], [p.tok("vector")])
        p.finalize()
    return build


build_l0attn = _standalone(emit_l0attn)
build_l0ffn = _standalone(emit_l0ffn)
build_l1attn = _standalone(emit_l1attn)
build_l1ffn = _standalone(emit_l1ffn)


FUSE_N = 4
FUSE_LIST = ["a0", "f0", "a1", "f1"]
PAD_NOPS = 0
FORCE_BARRIER = False


def build_fused(nc, ctx):
    p = Prog(nc, ctx)
    C = make_common(nc, ctx, p)
    d_x = nc.dram_tensor("x_in", [T, D], F32, kind="ExternalInput").ap()
    d_out = nc.dram_tensor("x_out", [T, D], F32, kind="ExternalOutput").ap()
    C["x_toks"] = emit_load_x(p, C["X"], d_x)
    _all = {"a0": emit_l0attn, "f0": emit_l0ffn, "a1": emit_l1attn, "f1": emit_l1ffn}
    fns = [_all[k] for k in FUSE_LIST[:FUSE_N]]
    for i, fn in enumerate(fns):
        C["abf"].reset()
        C["af32"].reset()
        fn(nc, p, C)
        if i < len(fns) - 1 or FORCE_BARRIER:
            emit_barrier(p, C)
            C["x_toks"] = None
    for _ in range(PAD_NOPS):
        p.emit("vector", lambda E: E.nop())
    emit_store_x(p, d_out, C["X"], [p.tok("vector")])
    p.finalize()


def prep_all(inputs):
    prep = prep_common()
    for pf in (prep_l0attn, prep_l0ffn, prep_l1attn, prep_l1ffn):
        prep.update(pf(inputs))
    return prep


_NC_CACHE = {}


def _get_nc(name, build_fn):
    if name not in _NC_CACHE:
        _NC_CACHE[name] = build_nc(build_fn)
    return _NC_CACHE[name]


def kernel(**inputs):
    x = np.ascontiguousarray(np.asarray(inputs["L0_attn_x"], dtype=np.float32))
    nc = _get_nc("fused", build_fused)
    prep = prep_all(inputs)
    return run_launch(nc, prep, x)
